# version 22
# baseline (speedup 1.0000x reference)
import math
import numpy as np
import ml_dtypes
import concourse.bass as bass
import concourse.mybir as mybir
from concourse.bass_utils import run_bass_kernel_spmd

F32 = mybir.dt.float32
BF16 = mybir.dt.bfloat16
I32 = mybir.dt.int32
ACT = mybir.ActivationFunctionType
ALU = mybir.AluOpType
AX = mybir.AxisListType

S = 2048
D = 2048
H = 16
DH = 128
T = 1024
NQT = 8
HID = 8192
O2 = 16384
EPS = 1e-5
ISQ = 1.0 / math.sqrt(128.0)
NEGBIG = -1.0e38
STAGE_ALL = 99

OF_G, OF_B, OF_BV, OF_BO, OF_NW = 38912, 40960, 43008, 45056, 47104


class Prog:
    def __init__(self, stage=STAGE_ALL, dbg=False):
        self.stage = stage
        self.dbg = dbg
        nc = bass.Bass("TRN2", target_bir_lowering=False)
        self.nc = nc
        dram = lambda n, sh, dt=F32, kind="ExternalInput": nc.dram_tensor(n, sh, dt, kind=kind)
        skind = "ExternalOutput" if dbg else "Internal"

        self.x = dram("x", [S, D])
        self.xq = dram("xq", [T, D])
        self.wq = dram("wq", [D, D])
        self.wk = dram("wk", [D, D])
        self.wv = dram("wv", [D, D])
        self.wo = dram("wo", [D, D])
        self.bq_p = dram("bq_p", [128, H])
        self.bk_p = dram("bk_p", [128, H])
        self.bv_rep = dram("bv_rep", [128, D])
        self.bo_rep = dram("bo_rep", [128, D])
        self.g_rep = dram("g_rep", [128, D])
        self.b_rep = dram("b_rep", [128, D])
        self.nw_rep = dram("nw_rep", [128, D])
        self.wgv = dram("wgv", [O2, D])
        self.wout = dram("wout", [D, HID])
        self.maskq = dram("maskq", [NQT, 128, 256])
        self.ident = dram("ident", [128, 128])
        self.identb = dram("identb", [128, 128], BF16)
        self.y = dram("y", [T, D], kind="ExternalOutput")

        self.Kscr = dram("Kscr", [H, DH, S], kind=skind)
        self.Vscr = dram("Vscr", [H, S, DH], kind=skind)
        self.Qscr = dram("Qscr", [H, DH, T], kind=skind)
        self.X1scr = dram("X1scr", [T, D], kind=skind)
        self.HMID = dram("HMID", [T, HID], kind=skind)
        self.TOUT = dram("TOUT", [16, 64, 128, 128], BF16, kind="Internal")
        if dbg:
            self.dbg_s = dram("dbg_s", [128, 16], kind="ExternalOutput")

        self.AF = nc.alloc_sbuf_tensor("AF", [128, 49152], F32)
        fx = lambda n, cols, dt=F32: nc.alloc_sbuf_tensor(n, [128, cols], dt)
        self.C_bq = fx("C_bq", H)
        self.C_bqs = fx("C_bqs", H)
        self.C_bk = fx("C_bk", H)
        self.C_id = fx("C_id", 128)
        self.C_idb = fx("C_idb", 128, BF16)
        self.C_ones = fx("C_ones", 2)
        self.C_127 = fx("C_127", 1)
        self.V_sgv = fx("V_sgv", 4)
        self.V_sout = fx("V_sout", 4)
        self.ST_mu = fx("ST_mu", 16)
        self.ST_R = fx("ST_R", 16)
        self.STq_mu = fx("STq_mu", NQT)
        self.STq_R = fx("STq_R", NQT)
        self.ST1_mu = fx("ST1_mu", NQT)
        self.ST1_r1 = fx("ST1_r1", NQT)
        self.ST1_al = fx("ST1_al", NQT)
        self.ST1_a = fx("ST1_a", NQT)
        self.ST1_b = fx("ST1_b", NQT)
        self.C1c = fx("C1c", NQT)
        self.RMX = fx("RMX", NQT)
        self.SC2 = fx("SC2", NQT)
        self.SO2 = fx("SO2", NQT)
        self.SE = fx("SE", 4)
        self.TMP = fx("TMP", 16)
        self.BN6 = fx("BN6", 96)
        self.BN2 = fx("BN2", 4)
        self.PARTS = fx("PARTS", 392)

        self.PS_SC = nc.alloc_psum_tensor("PS_SC", [128, 2048], F32)
        self.PS_A = nc.alloc_psum_tensor("PS_A", [128, 512], F32)
        self.PS_B = nc.alloc_psum_tensor("PS_B", [128, 512], F32)
        self.PS_C = nc.alloc_psum_tensor("PS_C", [128, 512], F32)
        self.PS_D = nc.alloc_psum_tensor("PS_D", [128, 512], F32)

        self.sems = {}
        self.cnt = {}
        self._sem_cms = []
        for name in ["LD", "CR", "DV", "AC", "AE", "PE", "GP", "ST", "Y", "VH",
                     "L0", "L1", "L2", "L3", "X0", "X1", "G0", "G1",
                     "S0", "S1", "Y0", "Y1", "H0", "H1"]:
            cm = nc.semaphore(name)
            self.sems[name] = cm.__enter__()
            self._sem_cms.append(cm)
            self.cnt[name] = 0

    def af(self, off, cols):
        return self.AF.ap()[:, off:off + cols]

    def afb(self, off_f32, cols_bf16):
        return self.AF.ap().bitcast(BF16)[:, 2 * off_f32: 2 * off_f32 + cols_bf16]

    def af3(self, off, stride, n, inner):
        return bass.AP(self.AF, off, [[49152, 128], [stride, n], [1, inner]])

    def bap(self, off_b, dims_b):
        f32dims = [[s // 2, n] for (s, n) in dims_b[:-1]] + \
                  [[1, dims_b[-1][1] // 2]]
        return bass.AP(self.AF, off_b // 2,
                       [[49152, 128]] + f32dims).bitcast(BF16)

    DMA_SEMS = ["LD", "GP", "ST", "Y", "L0", "L1", "L2", "L3", "X0", "X1",
                "G0", "G1", "S0", "S1", "Y0", "Y1", "H0", "H1"]

    def fence_block(self, name="fence"):
        with self.nc.Block(name) as blk:
            @blk.sync
            def _(sy):
                for s in self.DMA_SEMS + ["DV", "AC", "AE", "PE", "CR", "VH"]:
                    if self.cnt[s]:
                        sy.wait_ge(self.sems[s], self.cnt[s])

    def inc(self, name, n=16):
        self.cnt[name] += n
        return self.sems[name]

    def dma(self, eng, dst, src, sem="LD"):
        eng.dma_start(dst, src).then_inc(self.inc(sem, 16), 16)

    def vmark(self, inst):
        inst.then_inc(self.inc("VH", 1), 1)
        return self.cnt["VH"]

    def vbar(self, v, inst):
        v.wait_ge(self.sems["VH"], self.vmark(inst))

    P1B = 31744
    P1QS = ["G0", "G1", "GP"]

    def p1_buf(self, k):
        return self.af(self.P1B + (k % 3) * 2048, 2048)

    def p1_chunk_src(self, k):
        if k < 128:
            return self.wgv.ap()[k * 128:(k + 1) * 128, :]
        r, cc = divmod(k - 128, 4)
        return self.wout.ap()[r * 128:(r + 1) * 128, cc * 2048:(cc + 1) * 2048]

    def p1_load_sw(self, gp, k):
        if k >= 3:
            gp.wait_ge(self.sems["CR"], self._p1cr0 + k - 2)
        self.dma(gp, self.p1_buf(k), self.p1_chunk_src(k), self.P1QS[k % 3])

    def p1_reduce_sw(self, sc):
        k = self._p1k_sw
        self._p1k_sw += 1
        q = self.P1QS[k % 3]
        sc.wait_ge(self.sems[q], self._p1q0[q] + 16 * (k // 3 + 1))
        buf = self.p1_buf(k)
        sc.activation(buf, buf, ACT.Abs,
                      accum_out=self.PARTS.ap()[:, k:k + 1]
                      ).then_inc(self.inc("CR", 1), 1)

    def stats(self, v, off, cols=2048):
        nchunk = cols // 512
        last = None
        for cc in range(nchunk):
            last = v.bn_stats(self.BN6.ap()[:, cc * 6:(cc + 1) * 6],
                              self.af(off + cc * 512, 512))
        self.vbar(v, last)
        self.vbar(v, v.bn_aggr(self.BN2.ap()[:, 0:2], self.BN6.ap()[:, 0:6 * nchunk]))

    def rp_pipe(self, v, zb, cols):
        P = self
        st = {"pend": None, "p": 0}

        def zoff(p, j):
            return zb + p * 3 * cols + j * cols

        def s3(item):
            p, dst_b, m2 = item
            v.wait_ge(P.sems["VH"], m2)
            v.scalar_tensor_tensor(
                P.bap(dst_b, [[1, cols]]),
                P.af(zoff(p, 1), cols).bitcast(I32), -128.0,
                P.af(zoff(p, 2), cols), ALU.add,
                ALU.add).then_inc(P.inc("DV", 1), 1)

        def feed(pre, src, scale_col, dst_b):
            prev = st["pend"]
            if prev:
                s3(prev)
            p = st["p"]
            st["p"] ^= 1
            pre()
            m0 = P.vmark(v.tensor_scalar(P.af(zoff(p, 0), cols), src, scale_col,
                                         128.0, ALU.mult, ALU.add))
            v.wait_ge(P.sems["VH"], m0)
            m1 = P.vmark(v.tensor_copy(P.af(zoff(p, 1), cols).bitcast(I32),
                                       P.af(zoff(p, 0), cols)))
            v.wait_ge(P.sems["VH"], m1)
            m2 = P.vmark(v.scalar_tensor_tensor(
                P.af(zoff(p, 2), cols), P.af(zoff(p, 0), cols), -0.5,
                P.af(zoff(p, 1), cols).bitcast(I32), ALU.add, ALU.is_gt))
            st["pend"] = (p, dst_b, m2)

        def flush():
            prev = st["pend"]
            if prev is not None:
                s3(prev)
            st["pend"] = None
        return feed, flush

    def round_pos(self, v, src, scale_col, z0, z1, z2, dst_b, cols):
        self.vbar(v, v.tensor_scalar(self.af(z0, cols), src, scale_col, 128.0,
                                     ALU.mult, ALU.add))
        self.vbar(v, v.tensor_copy(self.af(z1, cols).bitcast(I32),
                                   self.af(z0, cols)))
        self.vbar(v, v.scalar_tensor_tensor(self.af(z2, cols),
                                            self.af(z0, cols), -0.5,
                                            self.af(z1, cols).bitcast(I32),
                                            ALU.add, ALU.is_gt))
        return v.scalar_tensor_tensor(self.bap(dst_b, [[1, cols]]),
                                      self.af(z1, cols).bitcast(I32), -128.0,
                                      self.af(z2, cols), ALU.add, ALU.add)

    def build(self):
        self.p0_consts()
        if self.stage == 1:
            self.p1_absum()
        if self.stage >= 2:
            self.p2_proj(True)
        if self.stage >= 3:
            self.p2_proj(False)
        if self.stage >= 4:
            self.p3_attn()
        if self.stage >= 5:
            self.p4_ffn_a()
        if self.stage >= 6:
            self.p5_gv()
        if self.stage >= 7:
            self.p6_tout()
            self.p7_out2()
        self.p_final()
        for cm in reversed(self._sem_cms):
            cm.__exit__(None, None, None)
        return self.nc

    def p0_consts(self):
        nc = self.nc
        with nc.Block("p0") as blk:
            @blk.sync
            def _(sy):
                self.dma(sy, self.C_bq.ap(), self.bq_p.ap())
                self.dma(sy, self.C_bk.ap(), self.bk_p.ap())
                self.dma(sy, self.C_id.ap(), self.ident.ap())
                self.dma(sy, self.C_idb.ap(), self.identb.ap())
                self.dma(sy, self.af(OF_G, D), self.g_rep.ap())
                self.dma(sy, self.af(OF_B, D), self.b_rep.ap())
                self.dma(sy, self.af(OF_BV, D), self.bv_rep.ap())
                self.dma(sy, self.af(OF_BO, D), self.bo_rep.ap())
                self.dma(sy, self.af(OF_NW, D), self.nw_rep.ap())

            @blk.vector
            def _(v):
                v.memset(self.C_ones.ap()[:, 0:1], 1.0)
                v.memset(self.C_127.ap(), 127.0)
                v.memset(self.RMX.ap(), 0.0)
                v.memset(self.TMP.ap(), 0.0)
                v.memset(self.SE.ap(), 0.0)

            @blk.gpsimd
            def _(gp):
                gp.wait_ge(self.sems["LD"], self.cnt["LD"])

    def p1_absum(self):
        nc = self.nc
        tiles = []
        for r in range(O2 // 128):
            tiles.append((self.wgv.ap()[r * 128:(r + 1) * 128, :], 0))
        for r in range(D // 128):
            for cc in range(4):
                tiles.append((self.wout.ap()[r * 128:(r + 1) * 128,
                                             cc * 2048:(cc + 1) * 2048], 1))
        NB = 4
        self.fence_block("f_p1")
        cr0 = self.cnt["CR"]
        c0 = dict(self.cnt)
        with nc.Block("p1") as blk:
            @blk.sync
            def _(sy):
                for t, (src, _) in enumerate(tiles):
                    if t >= NB:
                        sy.wait_ge(self.sems["CR"], cr0 + (t - NB + 1))
                    self.dma(sy, self.af((t % NB) * 2048, 2048), src,
                             f"L{t % 4}")

            PARTS = 4 * 2048

            @blk.vector
            def _(v):
                for t, (_, j) in enumerate(tiles):
                    v.wait_ge(self.sems[f"L{t % 4}"],
                              c0[f"L{t % 4}"] + 16 * (t // 4 + 1))
                    v.tensor_reduce(self.af(PARTS + t, 1), self.af((t % NB) * 2048, 2048),
                                    AX.X, ALU.add,
                                    apply_absolute_value=True).then_inc(self.inc("CR", 1), 1)
                v.wait_ge(self.sems["CR"], cr0 + len(tiles))
                self.vbar(v, v.tensor_reduce(self.TMP.ap()[:, 0:1],
                                             self.af(PARTS, 128), AX.X, ALU.add))
                self.vbar(v, v.tensor_reduce(self.TMP.ap()[:, 1:2],
                                             self.af(PARTS + 128, 64), AX.X, ALU.add))
                v.nop().then_inc(self.inc("CR", 1), 1)

            @blk.tensor
            def _(te):
                te.wait_ge(self.sems["CR"], cr0 + len(tiles) + 1)
                te.matmul(self.PS_A.ap()[0:1, 0:1], self.TMP.ap()[:, 0:1],
                          self.C_ones.ap()[:, 0:1], start=True, stop=True)
                te.matmul(self.PS_B.ap()[0:1, 0:1], self.TMP.ap()[:, 1:2],
                          self.C_ones.ap()[:, 0:1], start=True, stop=True).then_inc(self.inc("PE", 1), 1)

            @blk.scalar
            def _(sc):
                sc.wait_ge(self.sems["PE"], self.cnt["PE"])
                sc.copy(self.TMP.ap()[0:1, 8:9], self.PS_A.ap()[0:1, 0:1])
                sc.copy(self.TMP.ap()[0:1, 9:10], self.PS_B.ap()[0:1, 0:1]).then_inc(self.inc("AC", 1), 1)

            @blk.tensor
            def _(te):
                te.wait_ge(self.sems["AC"], self.cnt["AC"])
                te.matmul(self.PS_C.ap()[:, 0:2],
                          self.C_ones.ap()[0:1, 0:1].to_broadcast((1, 128)),
                          self.TMP.ap()[0:1, 8:10], start=True, stop=True).then_inc(self.inc("PE", 1), 1)

            @blk.vector
            def _(v):
                v.wait_ge(self.sems["PE"], self.cnt["PE"])
                v.tensor_scalar(self.V_sgv.ap()[:, 0:1], self.PS_C.ap()[:, 0:1],
                                1.0 / (O2 * D), 1e-5, ALU.mult, ALU.max)
                self.vbar(v, v.tensor_scalar(self.V_sout.ap()[:, 0:1],
                                             self.PS_C.ap()[:, 1:2],
                                             1.0 / (D * HID), 1e-5, ALU.mult, ALU.max))
                insts = []
                for vv in (self.V_sgv, self.V_sout):
                    insts.append(v.tensor_scalar(vv.ap()[:, 1:2], vv.ap()[:, 0:1],
                                                 0.5, None, ALU.mult, ALU.bypass))
                    insts.append(v.tensor_scalar(vv.ap()[:, 2:3], vv.ap()[:, 0:1],
                                                 -0.5, None, ALU.mult, ALU.bypass))
                insts[-1].then_inc(self.inc("DV", 1), 1)

            if self.dbg:
                @blk.scalar
                def _(sc):
                    sc.wait_ge(self.sems["DV"], self.cnt["DV"])
                    sc.copy(self.TMP.ap()[:, 12:13], self.V_sgv.ap()[:, 0:1])
                    sc.copy(self.TMP.ap()[:, 13:14], self.V_sout.ap()[:, 0:1]).then_inc(self.inc("AC", 1), 1)
                    sc.wait_ge(self.sems["AC"], self.cnt["AC"])

                @blk.gpsimd
                def _(gp):
                    gp.wait_ge(self.sems["AC"], self.cnt["AC"])
                    self.dma(gp, self.dbg_s.ap(), self.TMP.ap()[:, 0:16], "GP")

    def p2_proj(self, is_q):
        nc = self.nc
        ntile = NQT if is_q else 16
        ncols = ntile * 128
        xsrc = self.xq if is_q else self.x
        mu_st, R_st = (self.STq_mu, self.STq_R) if is_q else (self.ST_mu, self.ST_R)
        XNTH = 0
        XNTM = 8 * ncols
        XB = 16 * ncols
        XBH = XB + 2048
        XBM = XBH + 1024
        tag = "q" if is_q else "kv"

        XB0 = (16384, 18432) if is_q else (32768, 34816)
        XBH = 20480 if is_q else 36864
        XBM = 21504 if is_q else 37888
        self.fence_block(f"f_p2{tag}x")
        if is_q:
            self._p1k_sw = 0
            self._p1cr0 = self.cnt["CR"]
            self._p1q0 = dict(self.cnt)
        ld0, dv0, ac0, ae0, pe0 = (self.cnt[k] for k in ["LD", "DV", "AC", "AE", "PE"])
        cx0 = dict(self.cnt)
        d_split = lambda t: dv0 + 2 * t + 2
        a_sqrt = lambda t: ac0 + (1 if t == 0 else 9 * t - 7)
        a_evac = lambda t, r: ac0 + 9 * t + 3 + r
        with nc.Block(f"p2{tag}_xn") as blk:
            @blk.sync
            def _(sy):
                for t in range(ntile):
                    if t >= 2:
                        sy.wait_ge(self.sems["DV"], d_split(t - 2))
                    self.dma(sy, self.af(XB0[t % 2], 2048),
                             xsrc.ap()[t * 128:(t + 1) * 128, :], f"L{t % 2}")

            if is_q:
                @blk.gpsimd
                def _(gp):
                    for k in range(27):
                        self.p1_load_sw(gp, k)

            @blk.vector
            def _(v):
                if is_q:
                    v.tensor_scalar(self.C_bqs.ap(), self.C_bq.ap(), ISQ, None,
                                    ALU.mult, ALU.bypass)
                m = {}
                for slot in range(ntile + 1):
                    tn, to = slot, slot - 1
                    pn, po = tn % 2, to % 2
                    if to >= 0:
                        v.wait_ge(self.sems["VH"], m["st", to])
                        self.vbar(v, v.bn_aggr(self.BN2.ap()[:, 2 * po:2 * po + 2],
                                               self.BN6.ap()[:, 24 * po:24 * po + 24]))
                        m["mu", to] = self.vmark(v.tensor_copy(
                            mu_st.ap()[:, to:to + 1], self.BN2.ap()[:, 2 * po:2 * po + 1]))
                        v.tensor_scalar(self.TMP.ap()[:, 4 + po:5 + po],
                                        self.BN2.ap()[:, 2 * po + 1:2 * po + 2],
                                        1.0 + EPS, EPS * EPS, ALU.mult,
                                        ALU.add).then_inc(self.inc("DV", 1), 1)
                    if tn < ntile:
                        v.wait_ge(self.sems[f"L{tn % 2}"],
                                  cx0[f"L{tn % 2}"] + 16 * (tn // 2 + 1))
                        v.bn_stats(self.BN6.ap()[:, 24 * pn:24 * pn + 6],
                                   self.af(XB0[pn], 512))
                    if to >= 0:
                        v.wait_ge(self.sems["AC"], a_sqrt(to))
                        m["rc", to] = self.vmark(v.reciprocal(
                            R_st.ap()[:, to:to + 1], self.TMP.ap()[:, 6 + po:7 + po]))
                    if tn < ntile:
                        v.bn_stats(self.BN6.ap()[:, 24 * pn + 6:24 * pn + 12],
                                   self.af(XB0[pn] + 512, 512))
                    if to >= 0:
                        v.wait_ge(self.sems["VH"], m["mu", to])
                        m["s1", to] = self.vmark(v.scalar_tensor_tensor(
                            self.af(XB0[po], 2048), self.af(XB0[po], 2048),
                            mu_st.ap()[:, to:to + 1], self.af(OF_G, D),
                            ALU.subtract, ALU.mult))
                    if tn < ntile:
                        v.bn_stats(self.BN6.ap()[:, 24 * pn + 12:24 * pn + 18],
                                   self.af(XB0[pn] + 1024, 512))
                    if to >= 0:
                        v.wait_ge(self.sems["VH"], m["s1", to])
                        m["s2", to] = self.vmark(v.scalar_tensor_tensor(
                            self.af(XB0[po], 2048), self.af(XB0[po], 2048),
                            R_st.ap()[:, to:to + 1], self.af(OF_B, D),
                            ALU.mult, ALU.add))
                    if tn < ntile:
                        m["st", tn] = self.vmark(v.bn_stats(
                            self.BN6.ap()[:, 24 * pn + 18:24 * pn + 24],
                            self.af(XB0[pn] + 1536, 512)))
                    if to >= 0:
                        if to >= 1:
                            v.wait_ge(self.sems["PE"], pe0 + 8 * to)
                        v.wait_ge(self.sems["VH"], m["s2", to])
                        m["ha", to] = self.vmark(v.tensor_copy(
                            self.bap(2 * XBH, [[1, 1024]]), self.af(XB0[po], 1024)))
                        m["hb", to] = self.vmark(v.tensor_copy(
                            self.bap(2 * XBH + 1024, [[1, 1024]]),
                            self.af(XB0[po] + 1024, 1024)))
                        v.wait_ge(self.sems["VH"], m["ha", to])
                        v.scalar_tensor_tensor(self.bap(2 * XBM, [[1, 1024]]),
                                               self.af(XB0[po], 1024), 0.0,
                                               self.bap(2 * XBH, [[1, 1024]]),
                                               ALU.add, ALU.subtract)
                        v.wait_ge(self.sems["VH"], m["hb", to])
                        v.scalar_tensor_tensor(self.bap(2 * XBM + 1024, [[1, 1024]]),
                                               self.af(XB0[po] + 1024, 1024), 0.0,
                                               self.bap(2 * XBH + 1024, [[1, 1024]]),
                                               ALU.add, ALU.subtract
                                               ).then_inc(self.inc("DV", 1), 1)
                if not is_q:
                    v.wait_ge(self.sems["CR"], self._p1cr0 + 192)
                    v.tensor_reduce(self.TMP.ap()[:, 0:1], self.PARTS.ap()[:, 0:128],
                                    AX.X, ALU.add)
                    v.tensor_reduce(self.TMP.ap()[:, 1:2], self.PARTS.ap()[:, 128:192],
                                    AX.X, ALU.add).then_inc(self.inc("DV", 1), 1)
                    v.wait_ge(self.sems["PE"], pe0 + 8 * ntile + 2)
                    v.tensor_scalar(self.V_sgv.ap()[:, 0:1], self.PS_A.ap()[:, 4:5],
                                    1.0 / (O2 * D), 1e-5, ALU.mult, ALU.max)
                    self.vbar(v, v.tensor_scalar(self.V_sout.ap()[:, 0:1],
                                                 self.PS_A.ap()[:, 5:6],
                                                 1.0 / (D * HID), 1e-5, ALU.mult,
                                                 ALU.max))
                    insts = []
                    for vv in (self.V_sgv, self.V_sout):
                        insts.append(v.tensor_scalar(vv.ap()[:, 1:2], vv.ap()[:, 0:1],
                                                     0.5, None, ALU.mult, ALU.bypass))
                        insts.append(v.tensor_scalar(vv.ap()[:, 2:3], vv.ap()[:, 0:1],
                                                     -0.5, None, ALU.mult, ALU.bypass))
                    insts[-1].then_inc(self.inc("DV", 1), 1)

            @blk.tensor
            def _(te):
                for t in range(ntile):
                    te.wait_ge(self.sems["DV"], d_split(t))
                    for hm in range(2):
                        srcb = XBH if hm == 0 else XBM
                        for cg in range(4):
                            idx = 8 * t + 4 * hm + cg
                            if idx >= 2:
                                tp_, r = divmod(idx - 2, 8)
                                te.wait_ge(self.sems["AC"], a_evac(tp_, r))
                            psb = (self.PS_C if idx % 2 == 0
                                   else self.PS_D).ap().bitcast(BF16)
                            for c4 in range(4):
                                c = 4 * cg + c4
                                _t = te.transpose(psb[:, c4 * 128:(c4 + 1) * 128],
                                                  self.bap(2 * srcb + c * 128,
                                                           [[1, 128]]),
                                                  self.C_idb.ap())
                            _t.then_inc(self.inc("PE", 1), 1)
                if not is_q:
                    te.wait_ge(self.sems["DV"], dv0 + 2 * ntile + 1)
                    te.matmul(self.PS_A.ap()[0:1, 0:1], self.TMP.ap()[:, 0:1],
                              self.C_ones.ap()[:, 0:1], start=True, stop=True)
                    te.matmul(self.PS_B.ap()[0:1, 0:1], self.TMP.ap()[:, 1:2],
                              self.C_ones.ap()[:, 0:1], start=True,
                              stop=True).then_inc(self.inc("PE", 1), 1)
                    te.wait_ge(self.sems["AC"], ac0 + 9 * ntile + 1)
                    te.matmul(self.PS_A.ap()[:, 4:6],
                              self.C_ones.ap()[0:1, 0:1].to_broadcast((1, 128)),
                              self.TMP.ap()[0:1, 8:10], start=True,
                              stop=True).then_inc(self.inc("PE", 1), 1)

            @blk.scalar
            def _(sc):
                for t in range(ntile + 1):
                    if t < ntile:
                        sc.wait_ge(self.sems["DV"], dv0 + 2 * t + 1)
                        sc.activation(self.TMP.ap()[:, 6 + t % 2:7 + t % 2],
                                      self.TMP.ap()[:, 4 + t % 2:5 + t % 2],
                                      ACT.Sqrt).then_inc(self.inc("AC", 1), 1)
                    if t >= 1:
                        to = t - 1
                        for hm in range(2):
                            dstb = XNTH if hm == 0 else XNTM
                            for cg in range(4):
                                idx = 8 * to + 4 * hm + cg
                                sc.wait_ge(self.sems["PE"], pe0 + idx + 1)
                                psb = (self.PS_C if idx % 2 == 0
                                       else self.PS_D).ap().bitcast(BF16)
                                dst = self.bap(2 * dstb + (4 * cg) * ncols + to * 128,
                                               [[ncols, 4], [1, 128]])
                                sc.copy(dst, psb[:, 0:512]
                                        .rearrange("p (a c) -> p a c", a=4)
                                        ).then_inc(self.inc("AC", 1), 1)
                    if is_q and t < ntile:
                        for _ in range(3):
                            self.p1_reduce_sw(sc)
                if not is_q:
                    sc.wait_ge(self.sems["PE"], pe0 + 8 * ntile + 1)
                    sc.copy(self.TMP.ap()[0:1, 8:9], self.PS_A.ap()[0:1, 0:1])
                    sc.copy(self.TMP.ap()[0:1, 9:10],
                            self.PS_B.ap()[0:1, 0:1]).then_inc(self.inc("AC", 1), 1)

        ngrp = (T // 512) if is_q else (S // 512)
        wsrc = self.wq if is_q else self.wk
        bias = self.C_bqs if is_q else self.C_bk
        evsc = ISQ if is_q else 1.0
        dst_scr = self.Qscr if is_q else self.Kscr
        if is_q:
            WF = (22528, 24576)
            WH = (26624, 27648)
            WM = (28672, 29696)
            EVQ = 30720
        else:
            WF = (32768, 34816)
            WH = (36864, 37888)
            WM = (OF_G, OF_G + 1024)
            EVQ = OF_B
        self.fence_block(f"f_p2{tag}qk")
        ae0, pe0, dv0 = (self.cnt[k] for k in ["AE", "PE", "DV"])
        c0 = dict(self.cnt)
        with nc.Block(f"p2{tag}_qk") as blk:
            @blk.vector
            def _(v):
                for h in range(H):
                    lp = f"L{h % 2}"
                    v.wait_ge(self.sems[lp], c0[lp] + 16 * 16 * (h // 2 + 1))
                    if h >= 2:
                        v.wait_ge(self.sems["PE"], pe0 + (h - 1) * ngrp)
                    self.vbar(v, v.tensor_copy(self.bap(2 * WH[h % 2], [[1, 2048]]),
                                               self.af(WF[h % 2], 2048)))
                    v.scalar_tensor_tensor(self.bap(2 * WM[h % 2], [[1, 2048]]),
                                           self.af(WF[h % 2], 2048), 0.0,
                                           self.bap(2 * WH[h % 2], [[1, 2048]]),
                                           ALU.add, ALU.subtract
                                           ).then_inc(self.inc("DV", 1), 1)

            @blk.sync
            def _(sy):
                for h in range(H):
                    if h >= 2:
                        sy.wait_ge(self.sems["DV"], dv0 + h - 1)
                    for c in range(16):
                        self.dma(sy, self.af(WF[h % 2] + c * 128, 128),
                                 wsrc.ap()[c * 128:(c + 1) * 128, h * 128:(h + 1) * 128],
                                 f"L{h % 2}")

            if is_q:
                @blk.gpsimd
                def _(gp):
                    for k in range(27, 192):
                        self.p1_load_sw(gp, k)

            @blk.tensor
            def _(te):
                for h in range(H):
                    te.wait_ge(self.sems["DV"], dv0 + h + 1)
                    for grp in range(ngrp):
                        idx = h * ngrp + grp
                        if idx >= 2:
                            te.wait_ge(self.sems["AE"], ae0 + idx - 1)
                        ps = (self.PS_C if idx % 2 == 0 else self.PS_D).ap()
                        for c in range(16):
                            for (wb, xb) in ((WH, XNTH), (WH, XNTM), (WM, XNTH)):
                                _pe_last = te.matmul(
                                    ps,
                                    self.bap(2 * wb[h % 2] + c * 128, [[1, 128]]),
                                    self.bap(2 * xb + c * ncols + grp * 512,
                                             [[1, 512]]),
                                    start=(c == 0 and wb is WH and xb is XNTH),
                                    stop=(c == 15 and xb is XNTH and wb is WM))
                        _pe_last.then_inc(self.inc("PE", 1), 1)

            @blk.scalar
            def _(sc):
                for h in range(H):
                    for grp in range(ngrp):
                        idx = h * ngrp + grp
                        sc.wait_ge(self.sems["PE"], pe0 + idx + 1)
                        sp = f"S{idx % 2}"
                        if idx >= 2:
                            sc.wait_ge(self.sems[sp], c0[sp] + 16 * (idx // 2))
                        ps = (self.PS_C if idx % 2 == 0 else self.PS_D).ap()
                        ev = self.af(EVQ + (idx % 2) * 512, 512)
                        sc.activation(ev, ps, ACT.Identity, bias=bias.ap()[:, h:h + 1],
                                      scale=evsc).then_inc(self.inc("AE", 1), 1)
                        sc.wait_ge(self.sems["AE"], self.cnt["AE"])
                        self.dma(sc, dst_scr.ap()[h, :, grp * 512:(grp + 1) * 512],
                                 ev, sp)
                        if is_q:
                            quota = 24 + ((idx + 1) * 168 + 31) // 32
                            while self._p1k_sw < quota:
                                self.p1_reduce_sw(sc)

        if is_q:
            return

        WVF = 32768
        WVH = 33792
        WVM = 37888
        EV2 = 41984
        self.fence_block("f_p2v")
        dv0, pe0 = (self.cnt[k] for k in ["DV", "PE"])
        c0 = dict(self.cnt)

        def ev_dv(idx):
            return dv0 + 32 * (idx // 16) + 16 + (idx % 16) + 1

        with nc.Block("p2kv_v") as blk:
            @blk.sync
            def _(sy):
                for oc in range(4):
                    for c in range(16):
                        li = oc * 16 + c
                        if li >= 2:
                            sy.wait_ge(self.sems["DV"],
                                       dv0 + 32 * ((li - 2) // 16) + ((li - 2) % 16) + 1)
                        self.dma(sy, self.af(WVF + (li % 2) * 512, 512),
                                 self.wv.ap()[c * 128:(c + 1) * 128,
                                              oc * 512:(oc + 1) * 512],
                                 f"L{li % 2}")

            @blk.vector
            def _(v):
                for oc in range(4):
                    for c in range(16):
                        li = oc * 16 + c
                        lp = f"L{li % 2}"
                        v.wait_ge(self.sems[lp], c0[lp] + 16 * (li // 2 + 1))
                        if oc > 0 and c == 0:
                            v.wait_ge(self.sems["PE"], pe0 + oc * 16)
                        w = self.af(WVF + (li % 2) * 512, 512)
                        self.vbar(v, v.tensor_copy(
                            self.bap(2 * (WVH + c * 256), [[1, 512]]), w))
                        v.scalar_tensor_tensor(
                            self.bap(2 * (WVM + c * 256), [[1, 512]]), w, 0.0,
                            self.bap(2 * (WVH + c * 256), [[1, 512]]),
                            ALU.add, ALU.subtract).then_inc(self.inc("DV", 1), 1)
                    for tk in range(16):
                        idx = oc * 16 + tk
                        v.wait_ge(self.sems["PE"], pe0 + idx + 1)
                        sp = f"G{idx % 2}"
                        v.wait_ge(self.sems[sp], c0[sp] + 64 * (idx // 2))
                        ps = (self.PS_C if idx % 2 == 0 else self.PS_D).ap()
                        ev = self.af(EV2 + (idx % 2) * 512, 512)
                        v.tensor_tensor(ev, ps, self.af(OF_BV + oc * 512, 512),
                                        ALU.add).then_inc(self.inc("DV", 1), 1)

            @blk.tensor
            def _(te):
                for oc in range(4):
                    te.wait_ge(self.sems["DV"], dv0 + 32 * oc + 16)
                    for tk in range(16):
                        idx = oc * 16 + tk
                        if idx >= 2:
                            te.wait_ge(self.sems["DV"], ev_dv(idx - 2))
                        ps = (self.PS_C if idx % 2 == 0 else self.PS_D).ap()
                        for c in range(16):
                            for (xb, wv_) in ((XNTH, WVH), (XNTM, WVH), (XNTH, WVM)):
                                _pe_last = te.matmul(
                                    ps,
                                    self.bap(2 * xb + c * ncols + tk * 128, [[1, 128]]),
                                    self.bap(2 * (wv_ + c * 256), [[1, 512]]),
                                    start=(c == 0 and xb is XNTH and wv_ is WVH),
                                    stop=(c == 15 and xb is XNTH and wv_ is WVM))
                        _pe_last.then_inc(self.inc("PE", 1), 1)

            @blk.gpsimd
            def _(gp):
                for oc in range(4):
                    for tk in range(16):
                        idx = oc * 16 + tk
                        gp.wait_ge(self.sems["DV"], ev_dv(idx))
                        ev = self.af(EV2 + (idx % 2) * 512, 512)
                        sp = f"G{idx % 2}"
                        for j in range(4):
                            h = oc * 4 + j
                            self.dma(gp, self.Vscr.ap()[h, tk * 128:(tk + 1) * 128, :],
                                     ev[:, j * 128:(j + 1) * 128], sp)

    def p3_attn(self):
        nc = self.nc
        PB0, AVT = 0, 16384
        KB0, VB0, QB0, PTB0, MASKS = 24576, 28672, 34816, 35840, 36864
        X1B, WOB, X0B = 0, 8192, 24576
        st_done = self.cnt["ST"]
        gp_done = self.cnt["GP"]
        VBS = ["X1", "S0", "S1"]

        for s in range(2):
            Rs = 1024 * (s + 1)
            nkt = Rs // 128
            self.fence_block(f"f_p3s{s}")
            c0 = dict(self.cnt)

            def rngq(qt):
                return 256 * (4 * s + qt + 1)

            def qlo_of(kt):
                q = 0
                while 256 * (4 * s + q + 1) <= kt * 128:
                    q += 1
                return q

            pe_prog, dv_prog, ac_prog = [], [], []
            for i in range(17):
                do_sm, do_kt = i <= 15, i >= 1
                h, hp = i, i - 1
                scpos = {(qt * nkt) // 4: qt for qt in range(4)}
                if do_sm and not do_kt:
                    for qt in range(4):
                        pe_prog.append(("sc", h, qt))
                    for qt in range(4):
                        ac_prog.append(("exp", h, qt))
                if do_kt:
                    pe_prog.append(("T", hp, 0))
                    for kt in range(nkt):
                        if do_sm and kt in scpos:
                            pe_prog.append(("sc", h, scpos[kt]))
                        if kt + 1 < nkt:
                            pe_prog.append(("T", hp, kt + 1))
                        pe_prog.append(("PV", hp, kt))
                    for kt in range(nkt):
                        ac_prog.append(("evac", hp, kt))
                        if do_sm and kt in scpos:
                            ac_prog.append(("exp", h, scpos[kt]))
                if do_sm:
                    for kind, qt in [("bTT", 0), ("bRed", 0),
                                     ("bTT", 1), ("bRed", 1), ("rcp", 0),
                                     ("bTT", 2), ("bRed", 2), ("rcp", 1),
                                     ("nTS", 0),
                                     ("bTT", 3), ("bRed", 3), ("rcp", 2),
                                     ("nTS", 1), ("rcp", 3),
                                     ("nTS", 2), ("nTS", 3)]:
                        dv_prog.append((kind, h, qt))
                if do_kt:
                    dv_prog.append(("avt", hp))
            PEn = {op: c0["PE"] + k + 1 for k, op in enumerate(pe_prog)}
            DVn = {op: c0["DV"] + k + 1 for k, op in enumerate(dv_prog)}
            ACn = {op: c0["AC"] + k + 1 for k, op in enumerate(ac_prog)}

            with nc.Block(f"p3_s{s}") as blk:
                @blk.sync
                def _(sy):
                    if s == 0:
                        sy.wait_ge(self.sems["ST"], st_done)
                        sy.wait_ge(self.sems["GP"], gp_done)
                        for i in range(NQT):
                            self.dma(sy, self.af(MASKS + i * 256, 256),
                                     self.maskq.ap()[i, :, :], "X0")
                    for h in range(H):
                        if h >= 2:
                            sy.wait_ge(self.sems["PE"], PEn[("sc", h - 2, 3)])
                        self.dma(sy, self.af(KB0 + (h % 2) * 2048, Rs),
                                 self.Kscr.ap()[h, :, 0:Rs], f"L{h % 2}")
                        self.dma(sy, self.af(QB0 + (h % 2) * 512, 512),
                                 self.Qscr.ap()[h, :, s * 512:(s + 1) * 512],
                                 f"L{2 + h % 2}")
                        if h >= 3:
                            sy.wait_ge(self.sems["PE"], PEn[("PV", h - 3, nkt - 1)])
                        vsrc = bass.AP(self.Vscr, h * S * DH,
                                       [[DH, 128], [128 * DH, nkt], [1, DH]])
                        self.dma(sy, self.af3(VB0 + (h % 3) * 2048, 128, nkt, 128),
                                 vsrc, VBS[h % 3])

                @blk.tensor
                def _(te):
                    for op in pe_prog:
                        kind, h, x = op
                        if kind == "sc":
                            qt = x
                            rng = rngq(qt)
                            if qt == 0:
                                lp = f"L{h % 2}"
                                te.wait_ge(self.sems[lp], c0[lp] + 16 * (h // 2 + 1))
                                lp = f"L{2 + h % 2}"
                                te.wait_ge(self.sems[lp], c0[lp] + 16 * (h // 2 + 1))
                            prev = ("exp", h, qt - 1) if qt > 0 else ("exp", h - 1, 3)
                            if prev in ACn:
                                te.wait_ge(self.sems["AC"], ACn[prev])
                            for ck in range((rng + 511) // 512):
                                n = min(512, rng - ck * 512)
                                _l = te.matmul(
                                    self.PS_SC.ap()[:, ck * 512:ck * 512 + n],
                                    self.af(QB0 + (h % 2) * 512 + qt * 128, 128),
                                    self.af(KB0 + (h % 2) * 2048 + ck * 512, n),
                                    start=True, stop=True)
                            _l.then_inc(self.inc("PE", 1), 1)
                        elif kind == "T":
                            kt = x
                            if kt == 0:
                                te.wait_ge(self.sems["DV"], DVn[("nTS", h, 3)])
                            if kt >= 2:
                                te.wait_ge(self.sems["AC"], ACn[("evac", h, kt - 2)])
                            elif h >= 1:
                                te.wait_ge(self.sems["AC"],
                                           ACn[("evac", h - 1, nkt - 2 + kt)])
                            ps = (self.PS_A if kt % 2 == 0 else self.PS_B).ap()
                            qlo = qlo_of(kt)
                            for qt in range(qlo, 4):
                                _l = te.transpose(
                                    ps[:, qt * 128:(qt + 1) * 128],
                                    self.af(PB0 + (h % 2) * 8192 + qt * 2048
                                            + kt * 128, 128),
                                    self.C_id.ap())
                            _l.then_inc(self.inc("PE", 1), 1)
                        else:
                            kt = x
                            te.wait_ge(self.sems["AC"], ACn[("evac", h, kt)])
                            if kt == 0:
                                vp = VBS[h % 3]
                                te.wait_ge(self.sems[vp], c0[vp] + 16 * (h // 3 + 1))
                                if h >= 1:
                                    te.wait_ge(self.sems["DV"], DVn[("avt", h - 1)])
                            qlo = qlo_of(kt)
                            te.matmul(self.PS_C.ap()[:, qlo * 128:512],
                                      self.af(VB0 + (h % 3) * 2048 + kt * 128, 128),
                                      self.af(PTB0 + (kt % 2) * 512 + qlo * 128,
                                              512 - qlo * 128),
                                      start=(kt == 0), stop=(kt == nkt - 1)
                                      ).then_inc(self.inc("PE", 1), 1)

                @blk.vector
                def _(v):
                    first = True
                    for op in dv_prog:
                        kind, h = op[0], op[1]
                        if kind == "bTT":
                            qt = op[2]
                            rng = rngq(qt)
                            v.wait_ge(self.sems["PE"], PEn[("sc", h, qt)])
                            if first and s == 0:
                                v.wait_ge(self.sems["X0"], c0["X0"] + 16 * NQT)
                                first = False
                            i = 4 * s + qt
                            v.tensor_tensor(
                                self.PS_SC.ap()[:, rng - 256:rng],
                                self.PS_SC.ap()[:, rng - 256:rng],
                                self.af(MASKS + i * 256, 256),
                                ALU.add).then_inc(self.inc("DV", 1), 1)
                        elif kind == "bRed":
                            qt = op[2]
                            rng = rngq(qt)
                            v.wait_ge(self.sems["DV"], DVn[("bTT", h, qt)])
                            if ("exp", h - 1, qt) in ACn:
                                v.wait_ge(self.sems["AC"], ACn[("exp", h - 1, qt)])
                            v.tensor_reduce(
                                self.TMP.ap()[:, 4 + qt:5 + qt],
                                self.PS_SC.ap()[:, 0:rng],
                                AX.X, ALU.max,
                                negate=True).then_inc(self.inc("DV", 1), 1)
                        elif kind == "rcp":
                            qt = op[2]
                            v.wait_ge(self.sems["AC"], ACn[("exp", h, qt)])
                            v.reciprocal(self.TMP.ap()[:, 8 + qt:9 + qt],
                                         self.SE.ap()[:, qt:qt + 1]
                                         ).then_inc(self.inc("DV", 1), 1)
                        elif kind == "nTS":
                            qt = op[2]
                            rng = rngq(qt)
                            v.wait_ge(self.sems["DV"], DVn[("rcp", h, qt)])
                            v.tensor_scalar(
                                self.af(PB0 + (h % 2) * 8192 + qt * 2048, rng),
                                self.af(PB0 + (h % 2) * 8192 + qt * 2048, rng),
                                self.TMP.ap()[:, 8 + qt:9 + qt], None, ALU.mult,
                                ALU.bypass).then_inc(self.inc("DV", 1), 1)
                        else:
                            v.wait_ge(self.sems["PE"], PEn[("PV", h, nkt - 1)])
                            v.tensor_copy(self.af(AVT + h * 512, 512),
                                          self.PS_C.ap()).then_inc(self.inc("DV", 1), 1)

                @blk.scalar
                def _(sc):
                    for op in ac_prog:
                        kind, h, x = op
                        if kind == "exp":
                            qt = x
                            rng = rngq(qt)
                            sc.wait_ge(self.sems["DV"], DVn[("bRed", h, qt)])
                            sc.activation(self.af(PB0 + (h % 2) * 8192 + qt * 2048, rng),
                                          self.PS_SC.ap()[:, 0:rng], ACT.Exp,
                                          bias=self.TMP.ap()[:, 4 + qt:5 + qt],
                                          scale=1.0,
                                          accum_out=self.SE.ap()[:, qt:qt + 1]
                                          ).then_inc(self.inc("AC", 1), 1)
                        else:
                            kt = x
                            sc.wait_ge(self.sems["PE"], PEn[("T", h, kt)])
                            if kt >= 2:
                                sc.wait_ge(self.sems["PE"], PEn[("PV", h, kt - 2)])
                            elif h >= 1:
                                sc.wait_ge(self.sems["PE"],
                                           PEn[("PV", h - 1, nkt - 2 + kt)])
                            ps = (self.PS_A if kt % 2 == 0 else self.PS_B).ap()
                            qlo = qlo_of(kt)
                            sc.copy(self.af(PTB0 + (kt % 2) * 512 + qlo * 128,
                                            512 - qlo * 128),
                                    ps[:, qlo * 128:512]
                                    ).then_inc(self.inc("AC", 1), 1)

            self.fence_block(f"f_p3w{s}")
            ld0, dv0, ac0, pe0 = (self.cnt[k] for k in ["LD", "DV", "AC", "PE"])
            c0 = dict(self.cnt)
            with nc.Block(f"p3_wo{s}") as blk:
                @blk.sync
                def _(sy):
                    for oc in range(4):
                        if oc > 0:
                            sy.wait_ge(self.sems["PE"], pe0 + oc * 4)
                        for c in range(16):
                            self.dma(sy, self.af(WOB + c * 512, 512),
                                     self.wo.ap()[c * 128:(c + 1) * 128,
                                                  oc * 512:(oc + 1) * 512])
                        for qt in range(4):
                            idx = oc * 4 + qt
                            if idx >= 2:
                                sy.wait_ge(self.sems["DV"], dv0 + idx - 1)
                            self.dma(sy, self.af(X0B + (idx % 2) * 512, 512),
                                     self.xq.ap()[(4 * s + qt) * 128:(4 * s + qt + 1) * 128,
                                                  oc * 512:(oc + 1) * 512],
                                     f"X{idx % 2}")

                @blk.tensor
                def _(te):
                    for oc in range(4):
                        te.wait_ge(self.sems["LD"], ld0 + 16 * 16 * (oc + 1))
                        for qt in range(4):
                            idx = oc * 4 + qt
                            if idx >= 2:
                                te.wait_ge(self.sems["DV"], dv0 + idx - 1)
                            ps = (self.PS_C if idx % 2 == 0 else self.PS_D).ap()
                            for c in range(16):
                                _pe_last = te.matmul(
                                    ps, self.af(AVT + c * 512 + qt * 128, 128),
                                    self.af(WOB + c * 512, 512),
                                    start=(c == 0), stop=(c == 15))
                            _pe_last.then_inc(self.inc("PE", 1), 1)

                @blk.vector
                def _(v):
                    mkbn = {}

                    def bnchunk(idx):
                        oc, qt = idx // 4, idx % 4
                        v.wait_ge(self.sems["DV"], dv0 + idx + 1)
                        mkbn[qt] = self.vmark(v.bn_stats(
                            self.BN6.ap()[:, qt * 24 + oc * 6:qt * 24 + oc * 6 + 6],
                            self.af(X1B + qt * 2048 + oc * 512, 512)))

                    for oc in range(4):
                        for qt in range(4):
                            idx = oc * 4 + qt
                            v.wait_ge(self.sems["PE"], pe0 + idx + 1)
                            v.wait_ge(self.sems[f"X{idx % 2}"],
                                      c0[f"X{idx % 2}"] + 16 * (idx // 2 + 1))
                            ps = (self.PS_C if idx % 2 == 0 else self.PS_D).ap()
                            dst = self.af(X1B + qt * 2048 + oc * 512, 512)
                            self.vbar(v, v.scalar_tensor_tensor(
                                dst, ps, 0.0,
                                self.af(X0B + (idx % 2) * 512, 512),
                                ALU.add, ALU.add))
                            v.tensor_tensor(dst, dst, self.af(OF_BO + oc * 512, 512),
                                            ALU.add).then_inc(self.inc("DV", 1), 1)
                            if idx >= 1:
                                bnchunk(idx - 1)
                    bnchunk(15)
                    for qt in range(4):
                        i = 4 * s + qt
                        v.wait_ge(self.sems["VH"], mkbn[qt])
                        self.vbar(v, v.bn_aggr(self.BN2.ap()[:, 0:2],
                                               self.BN6.ap()[:, qt * 24:qt * 24 + 24]))
                        v.tensor_copy(self.ST1_mu.ap()[:, i:i + 1], self.BN2.ap()[:, 0:1])
                        v.tensor_scalar(self.TMP.ap()[:, 4:5], self.BN2.ap()[:, 1:2],
                                        EPS, None, ALU.add, ALU.bypass)
                        v.tensor_scalar(self.TMP.ap()[:, 5:6], self.BN2.ap()[:, 1:2],
                                        1.0 + EPS, EPS * EPS, ALU.mult,
                                        ALU.add).then_inc(self.inc("DV", 1), 1)
                        v.wait_ge(self.sems["AC"], ac0 + qt + 1)
                        v.reciprocal(self.ST1_r1.ap()[:, i:i + 1], self.TMP.ap()[:, 6:7])
                        self.vbar(v, v.reciprocal(self.ST1_al.ap()[:, i:i + 1],
                                                  self.TMP.ap()[:, 7:8]))
                        v.tensor_scalar(self.ST1_a.ap()[:, i:i + 1],
                                        self.ST1_r1.ap()[:, i:i + 1], 1.0, None,
                                        ALU.add, ALU.bypass)
                        v.scalar_tensor_tensor(self.ST1_b.ap()[:, i:i + 1],
                                               self.ST1_mu.ap()[:, i:i + 1], -1.0,
                                               self.ST1_r1.ap()[:, i:i + 1],
                                               ALU.mult, ALU.mult).then_inc(self.inc("DV", 1), 1)

                @blk.scalar
                def _(sc):
                    for qt in range(4):
                        sc.wait_ge(self.sems["DV"], dv0 + 16 + 2 * qt + 1)
                        sc.activation(self.TMP.ap()[:, 6:7], self.TMP.ap()[:, 4:5], ACT.Sqrt)
                        sc.activation(self.TMP.ap()[:, 7:8], self.TMP.ap()[:, 5:6],
                                      ACT.Sqrt).then_inc(self.inc("AC", 1), 1)

                @blk.gpsimd
                def _(gp):
                    for qt in range(4):
                        i = 4 * s + qt
                        gp.wait_ge(self.sems["DV"], dv0 + 16 + 2 * qt + 2)
                        self.dma(gp, self.X1scr.ap()[i * 128:(i + 1) * 128, :],
                                 self.af(X1B + qt * 2048, 2048), "GP")

    def p4_ffn_a(self):
        nc = self.nc
        AQ1T, XB4, GB, Z0, Z1, Z2, AQN = 0, 8192, 10240, 12288, 14336, 16384, 18432
        gp_done = self.cnt["GP"]
        self.fence_block("f_p4")
        ld0, dv0, ae0, pe0 = (self.cnt[k] for k in ["LD", "DV", "AE", "PE"])
        with nc.Block("p4") as blk:
            @blk.sync
            def _(sy):
                sy.wait_ge(self.sems["GP"], gp_done)
                for i in range(NQT):
                    if i > 0:
                        sy.wait_ge(self.sems["DV"], dv0 + i)
                    self.dma(sy, self.af(XB4, 2048),
                             self.X1scr.ap()[i * 128:(i + 1) * 128, :])

            @blk.vector
            def _(v):
                for i in range(NQT):
                    v.wait_ge(self.sems["LD"], ld0 + 16 * (i + 1))
                    if i > 0:
                        v.wait_ge(self.sems["PE"], pe0 + 4 * i)
                    self.vbar(v, v.scalar_tensor_tensor(
                        self.af(GB, 2048), self.af(XB4, 2048),
                        self.ST1_mu.ap()[:, i:i + 1],
                        self.af(OF_NW, D), ALU.subtract, ALU.mult))
                    self.vbar(v, v.tensor_reduce(self.TMP.ap()[:, 4:5],
                                                 self.af(GB, 2048), AX.X,
                                                 ALU.max, apply_absolute_value=True))
                    self.vbar(v, v.tensor_scalar(self.TMP.ap()[:, 4:5],
                                                 self.TMP.ap()[:, 4:5],
                                                 1e-5, None, ALU.max, ALU.bypass))
                    self.vbar(v, v.reciprocal(self.TMP.ap()[:, 5:6],
                                               self.TMP.ap()[:, 4:5]))
                    v.tensor_scalar(self.TMP.ap()[:, 5:6], self.TMP.ap()[:, 5:6],
                                    127.0, None, ALU.mult, ALU.bypass)
                    self.vbar(v, v.tensor_tensor(self.TMP.ap()[:, 6:7],
                                                 self.ST1_al.ap()[:, i:i + 1],
                                                 self.TMP.ap()[:, 4:5], ALU.mult))
                    v.tensor_tensor(self.TMP.ap()[:, 7:8], self.TMP.ap()[:, 6:7],
                                    self.V_sgv.ap()[:, 0:1], ALU.mult)
                    self.round_pos(v, self.af(GB, 2048), self.TMP.ap()[:, 5:6],
                                   Z0, Z1, Z2, 2 * AQN, 2048)
                    v.tensor_scalar(self.C1c.ap()[:, i:i + 1], self.TMP.ap()[:, 7:8],
                                    1.0 / 127.0, None, ALU.mult,
                                    ALU.bypass).then_inc(self.inc("DV", 1), 1)

            @blk.tensor
            def _(te):
                for i in range(NQT):
                    te.wait_ge(self.sems["DV"], dv0 + i + 1)
                    for cg in range(4):
                        idx = 4 * i + cg
                        if idx >= 2:
                            te.wait_ge(self.sems["AE"], ae0 + idx - 1)
                        psb = (self.PS_C if idx % 2 == 0 else self.PS_D).ap().bitcast(BF16)
                        for c4 in range(4):
                            c = 4 * cg + c4
                            _pe_last = te.transpose(psb[:, c4 * 128:(c4 + 1) * 128],
                                         self.bap(2 * AQN + c * 128, [[1, 128]]),
                                         self.C_idb.ap())
                        _pe_last.then_inc(self.inc("PE", 1), 1)

            @blk.scalar
            def _(sc):
                for i in range(NQT):
                    for cg in range(4):
                        idx = 4 * i + cg
                        sc.wait_ge(self.sems["PE"], pe0 + idx + 1)
                        psb = (self.PS_C if idx % 2 == 0 else self.PS_D).ap().bitcast(BF16)
                        dst = self.bap(2 * AQ1T + (4 * cg) * 1024 + i * 128,
                                       [[1024, 4], [1, 128]])
                        sc.copy(dst, psb[:, 0:512].rearrange("p (a c) -> p a c", a=4)
                                ).then_inc(self.inc("AE", 1), 1)

    def p5_gv(self):
        nc = self.nc
        AQ1T = 0
        TT0 = 8192
        WGB = 24576
        M1B = 40960
        NTB = 43008
        SGB = 45056
        HMB = 46080
        self.fence_block("f_p5")
        GSEM = ["L0", "L1", "L2", "L3", "X0", "X1", "S0", "S1"]
        c00 = dict(self.cnt)
        for b in range(17):
            tern = b <= 15
            mm = b >= 1
            tp = b % 2
            mp = (b - 1) % 2
            c0 = dict(self.cnt)
            pe_per = (1 if mm else 0) + (4 if tern else 0)
            ac_per = (1 if mm else 0) + (4 if tern else 0)
            dv_per = (1 if mm else 0) + (1 if tern else 0)

            def dv_after_tern(g):
                return c0["DV"] + dv_per * g + 1

            def dv_after_hm(k):
                return c0["DV"] + dv_per * k + (2 if tern else 1)

            def pe_after_mm(k):
                return c0["PE"] + pe_per * k + 1

            def pe_after_tg(k, cg):
                return c0["PE"] + pe_per * k + (1 if mm else 0) + cg + 1

            def ac_after_silu(k):
                return c0["AC"] + ac_per * k + 1

            def ac_after_evac(k, cg):
                return c0["AC"] + ac_per * k + (1 if mm else 0) + cg + 1

            with nc.Block(f"p5_{b}") as blk:
                @blk.sync
                def _(sy):
                    if b == 0:
                        for g in range(8):
                            part, ot = g // 4, g % 4
                            self.dma(sy, self.af(WGB + g * 2048, 2048),
                                     self.wgv.ap()[part * HID + ot * 128:
                                                   part * HID + (ot + 1) * 128, :],
                                     GSEM[g])
                    if b <= 14:
                        for g in range(8):
                            part, ot = g // 4, g % 4
                            rowbase = part * HID + (b + 1) * 512
                            sy.wait_ge(self.sems["DV"], dv_after_tern(g))
                            self.dma(sy, self.af(WGB + g * 2048, 2048),
                                     self.wgv.ap()[rowbase + ot * 128:
                                                   rowbase + (ot + 1) * 128, :],
                                     GSEM[g])

                @blk.vector
                def _(v):
                    for k in range(8):
                        if tern:
                            g = k
                            lp = GSEM[g]
                            v.wait_ge(self.sems[lp], c00[lp] + 16 * (b + 1))
                            if g >= 2:
                                v.wait_ge(self.sems["PE"], pe_after_tg(g - 2, 3))
                            w = self.af(WGB + g * 2048, 2048)
                            m1 = self.bap(2 * (M1B + (g % 2) * 1024), [[1, 2048]])
                            self.vbar(v, v.tensor_scalar(
                                m1, w,
                                self.V_sgv.ap()[:, 2:3], None,
                                ALU.is_lt, ALU.bypass))
                            v.scalar_tensor_tensor(
                                self.bap(2 * (NTB + (g % 2) * 1024), [[1, 2048]]),
                                w, self.V_sgv.ap()[:, 1:2], m1,
                                ALU.is_gt, ALU.subtract
                                ).then_inc(self.inc("DV", 1), 1)
                        if mm:
                            qt = k
                            v.wait_ge(self.sems["AC"], ac_after_silu(qt))
                            gpar = f"G{qt % 2}"
                            v.wait_ge(self.sems[gpar], c0[gpar] + 16 * (qt // 2))
                            hm = self.af(HMB + (qt % 2) * 512, 512)
                            psv = self.PS_SC.ap()[:, (qt % 2) * 1024:(qt % 2) * 1024 + 512]
                            self.vbar(v, v.scalar_tensor_tensor(
                                hm, psv,
                                self.C1c.ap()[:, qt:qt + 1],
                                self.af(SGB + (qt % 2) * 512, 512), ALU.mult, ALU.mult))
                            self.vbar(v, v.tensor_reduce(
                                self.TMP.ap()[:, 4:5], hm, AX.X, ALU.max,
                                apply_absolute_value=True))
                            v.tensor_tensor(self.RMX.ap()[:, qt:qt + 1],
                                            self.RMX.ap()[:, qt:qt + 1],
                                            self.TMP.ap()[:, 4:5],
                                            ALU.max).then_inc(self.inc("DV", 1), 1)

                @blk.tensor
                def _(te):
                    for k in range(8):
                        if mm:
                            qt = k
                            if qt >= 2:
                                te.wait_ge(self.sems["DV"], dv_after_hm(qt - 2))
                            psv = self.PS_SC.ap()[:, (qt % 2) * 1024:(qt % 2) * 1024 + 512]
                            psg = self.PS_SC.ap()[:, (qt % 2) * 1024 + 512:
                                                  (qt % 2) * 1024 + 1024]
                            for c in range(16):
                                te.matmul(psv,
                                          self.bap(2 * AQ1T + c * 1024 + qt * 128,
                                                   [[1, 128]]),
                                          self.bap(2 * (TT0 + mp * 8192) + c * 512,
                                                   [[1, 512]]),
                                          start=(c == 0), stop=(c == 15))
                                _l = te.matmul(psg,
                                               self.bap(2 * AQ1T + c * 1024 + qt * 128,
                                                        [[1, 128]]),
                                               self.bap(2 * (TT0 + mp * 8192 + 4096)
                                                        + c * 512, [[1, 512]]),
                                               start=(c == 0), stop=(c == 15))
                            _l.then_inc(self.inc("PE", 1), 1)
                        if tern:
                            g = k
                            te.wait_ge(self.sems["DV"], dv_after_tern(g))
                            part, ot = g // 4, g % 4
                            base = TT0 + tp * 8192 + (0 if part == 0 else 4096)
                            for cg in range(4):
                                tid = 8 * b + g
                                if b * 32 + g * 4 + cg >= 2:
                                    kp, cgp = divmod(g * 4 + cg - 2, 4)
                                    if g * 4 + cg >= 2:
                                        te.wait_ge(self.sems["AC"],
                                                   ac_after_evac(kp, cgp))
                                psb = (self.PS_C if (g * 4 + cg) % 2 == 0
                                       else self.PS_D).ap().bitcast(BF16)
                                for c4 in range(4):
                                    c = 4 * cg + c4
                                    _t = te.transpose(
                                        psb[:, c4 * 128:(c4 + 1) * 128],
                                        self.bap(2 * (NTB + (g % 2) * 1024) + c * 128,
                                                 [[1, 128]]),
                                        self.C_idb.ap())
                                _t.then_inc(self.inc("PE", 1), 1)

                @blk.scalar
                def _(sc):
                    for k in range(8):
                        if mm:
                            qt = k
                            sc.wait_ge(self.sems["PE"], pe_after_mm(qt))
                            psg = self.PS_SC.ap()[:, (qt % 2) * 1024 + 512:
                                                  (qt % 2) * 1024 + 1024]
                            sc.activation(self.af(SGB + (qt % 2) * 512, 512), psg,
                                          ACT.Silu, bias=0.0,
                                          scale=self.C1c.ap()[:, qt:qt + 1]
                                          ).then_inc(self.inc("AC", 1), 1)
                        if tern:
                            g = k
                            part, ot = g // 4, g % 4
                            base = TT0 + tp * 8192 + (0 if part == 0 else 4096)
                            for cg in range(4):
                                sc.wait_ge(self.sems["PE"], pe_after_tg(g, cg))
                                psb = (self.PS_C if (g * 4 + cg) % 2 == 0
                                       else self.PS_D).ap().bitcast(BF16)
                                dst = self.bap(2 * base + (4 * cg) * 512 + ot * 128,
                                               [[512, 4], [1, 128]])
                                sc.copy(dst, psb[:, 0:512]
                                        .rearrange("p (a c) -> p a c", a=4)
                                        ).then_inc(self.inc("AC", 1), 1)

                if mm:
                    @blk.gpsimd
                    def _(gp):
                        for qt in range(8):
                            gp.wait_ge(self.sems["DV"], dv_after_hm(qt))
                            self.dma(gp, self.HMID.ap()[qt * 128:(qt + 1) * 128,
                                                        (b - 1) * 512:b * 512],
                                     self.af(HMB + (qt % 2) * 512, 512),
                                     f"G{qt % 2}")

    def p6_tout(self):
        nc = self.nc
        WOB, M1C, NT5, TSB = 0, 12288, 14336, 16384
        LQ = ["L0", "L1", "L2", "L3", "X0", "X1"]
        SQ = ["S0", "S1", "LD"]
        self.fence_block("f_p6")
        c0 = dict(self.cnt)
        DVn = lambda ci: c0["DV"] + ci + 1
        PEn = lambda g: c0["PE"] + g + 1
        ACn = lambda g: c0["AC"] + g + 1
        store_cnt = {}
        qn = {q: 0 for q in SQ}
        for bt in range(128):
            q = SQ[bt % 3]
            qn[q] += 1
            store_cnt[bt] = (q, qn[q])
        with nc.Block("p6") as blk:
            @blk.sync
            def _(sy):
                for ci in range(64):
                    if ci >= 6:
                        sy.wait_ge(self.sems["DV"], DVn(ci - 6))
                    rt, ch = divmod(ci, 4)
                    self.dma(sy, self.af(WOB + (ci % 6) * 2048, 2048),
                             self.wout.ap()[rt * 128:(rt + 1) * 128,
                                            ch * 2048:(ch + 1) * 2048],
                             LQ[ci % 6])

            @blk.vector
            def _(v):
                mk = {}
                for s in range(65):
                    if s < 64:
                        q = LQ[s % 6]
                        v.wait_ge(self.sems[q], c0[q] + 16 * (s // 6 + 1))
                        if s >= 2:
                            v.wait_ge(self.sems["DV"], DVn(s - 2))
                        mk[s] = self.vmark(v.tensor_scalar(
                            self.bap(2 * (M1C + (s % 2) * 1024), [[1, 2048]]),
                            self.af(WOB + (s % 6) * 2048, 2048),
                            self.V_sout.ap()[:, 2:3], None,
                            ALU.is_lt, ALU.bypass))
                    if s >= 1:
                        ci = s - 1
                        if ci >= 2:
                            v.wait_ge(self.sems["PE"], PEn(4 * (ci - 2) + 3))
                        v.wait_ge(self.sems["VH"], mk[ci])
                        v.scalar_tensor_tensor(
                            self.bap(2 * (NT5 + (ci % 2) * 1024), [[1, 2048]]),
                            self.af(WOB + (ci % 6) * 2048, 2048),
                            self.V_sout.ap()[:, 1:2],
                            self.bap(2 * (M1C + (ci % 2) * 1024), [[1, 2048]]),
                            ALU.is_gt, ALU.subtract).then_inc(self.inc("DV", 1), 1)

            @blk.tensor
            def _(te):
                for ci in range(64):
                    te.wait_ge(self.sems["DV"], DVn(ci))
                    for u in range(4):
                        g = 4 * ci + u
                        if g >= 2:
                            te.wait_ge(self.sems["AC"], ACn(g - 2))
                        psb = (self.PS_C if g % 2 == 0
                               else self.PS_D).ap().bitcast(BF16)
                        for s in range(4):
                            tl = 4 * u + s
                            _l = te.transpose(
                                psb[:, s * 128:(s + 1) * 128],
                                self.bap(2 * (NT5 + (ci % 2) * 1024) + tl * 128,
                                         [[1, 128]]),
                                self.C_idb.ap())
                        _l.then_inc(self.inc("PE", 1), 1)

            @blk.scalar
            def _(sc):
                for ci in range(64):
                    rt, ch = divmod(ci, 4)
                    for u in range(4):
                        g = 4 * ci + u
                        bt = g // 2
                        sc.wait_ge(self.sems["PE"], PEn(g))
                        if u % 2 == 0 and bt >= 3:
                            q, n = store_cnt[bt - 3]
                            sc.wait_ge(self.sems[q], c0[q] + 16 * n)
                        psb = (self.PS_C if g % 2 == 0
                               else self.PS_D).ap().bitcast(BF16)
                        tsb = self.bap(2 * (TSB + (bt % 3) * 512 + (u % 2) * 256),
                                       [[1, 512]])
                        sc.copy(tsb, psb[:, 0:512]).then_inc(self.inc("AC", 1), 1)
                        if u % 2 == 1:
                            sc.wait_ge(self.sems["AC"], self.cnt["AC"])
                            oc0 = ch * 16 + 4 * (u - 1)
                            dst = bass.AP(self.TOUT, rt * 1048576 + oc0 * 16384,
                                          [[128, 128], [16384, 8], [1, 128]])
                            self.dma(sc, dst,
                                     self.bap(2 * (TSB + (bt % 3) * 512),
                                              [[128, 8], [1, 128]]),
                                     SQ[bt % 3])

    def p7_out2(self):
        nc = self.nc
        AQ2T = 0
        HM2 = 32768
        ZA, ZBC, ZCC = 34816, 35328, 35840
        AQ2N = 36352
        AQN = (36352, 43776)
        ZP7 = 40704
        TOB = 36608
        X1B7 = 37632
        XH = 38656
        YB = 39680
        gp_done = self.cnt["GP"]
        st_done = self.cnt["ST"]
        TSEM = ["H0", "H1", "S0", "S1"]

        for b in range(3):
            bld = b <= 1
            mm = b >= 1
            self.fence_block(f"f_p7_{b}")
            c0 = dict(self.cnt)
            pe_prog, dv_prog, ac_prog = [], [], []
            if mm:
                k_iter = iter(range(64))
                for j in range(4):
                    for oc in range(64):
                        if bld and oc % 4 == 0:
                            try:
                                k = next(k_iter)
                                pe_prog.append(("T", k))
                            except StopIteration:
                                pass
                        pe_prog.append(("MM", j, oc))
            elif bld:
                for k in range(64):
                    pe_prog.append(("T", k))
            if bld and mm:
                for j in range(4):
                    for k in range(j * 16, (j + 1) * 16):
                        dv_prog.append(("rnd", k))
                    for qt in range(4):
                        dv_prog.append(("yb", j, qt))
            elif bld:
                for k in range(64):
                    dv_prog.append(("rnd", k))
            else:
                for j in range(4):
                    for qt in range(4):
                        dv_prog.append(("yb", j, qt))
            if bld:
                for k in range(64):
                    ac_prog.append(("evac", k))
            PEn = {op: c0["PE"] + i + 1 for i, op in enumerate(pe_prog)}
            DVn = {op: c0["DV"] + i + 1 for i, op in enumerate(dv_prog)}
            ACn = {op: c0["AC"] + i + 1 for i, op in enumerate(ac_prog)}

            with nc.Block(f"p7_{b}") as blk:
                @blk.sync
                def _(sy):
                    if b == 0:
                        sy.wait_ge(self.sems["GP"], gp_done)
                        sy.wait_ge(self.sems["ST"], st_done)

                    def hm2_load(k):
                        qt, oc16 = k // 16, k % 16
                        i = b * 4 + qt
                        if k >= 4:
                            sy.wait_ge(self.sems["DV"], DVn[("rnd", k - 4)])
                        self.dma(sy, self.af(HM2 + (k % 4) * 512, 512),
                                 self.HMID.ap()[i * 128:(i + 1) * 128,
                                                oc16 * 512:(oc16 + 1) * 512],
                                 f"L{k % 4}")

                    def tob_load(j, oc):
                        idx = j * 64 + oc
                        if idx >= 4:
                            sy.wait_ge(self.sems["PE"], PEn[("MM", (idx - 4) // 64,
                                                            (idx - 4) % 64)])
                        vsrc = bass.AP(self.TOUT, (j * 4) * 1048576 + oc * 16384,
                                       [[128, 128], [1048576, 4], [1, 128]])
                        self.dma(sy, self.bap(2 * (TOB + (idx % 4) * 256),
                                              [[128, 4], [1, 128]]),
                                 vsrc, TSEM[idx % 4])

                    def x1_load(j, qt):
                        i = (b - 1) * 4 + qt
                        idx2 = j * 4 + qt
                        if idx2 >= 2:
                            sy.wait_ge(self.sems["DV"], DVn[("yb", (idx2 - 2) // 4,
                                                            (idx2 - 2) % 4)])
                        self.dma(sy, self.af(X1B7 + (idx2 % 2) * 512, 512),
                                 self.X1scr.ap()[i * 128:(i + 1) * 128,
                                                 j * 512:(j + 1) * 512],
                                 f"X{idx2 % 2}")

                    if bld and mm:
                        for j in range(4):
                            for kk in range(16):
                                hm2_load(j * 16 + kk)
                                for o4 in range(4):
                                    tob_load(j, 4 * kk + o4)
                            for qt in range(4):
                                x1_load(j, qt)
                    elif bld:
                        for k in range(64):
                            hm2_load(k)
                    else:
                        for j in range(4):
                            for oc in range(64):
                                tob_load(j, oc)
                            for qt in range(4):
                                x1_load(j, qt)

                @blk.vector
                def _(v):
                    rp_feed, rp_flush = self.rp_pipe(v, ZP7, 512)
                    for op in dv_prog:
                        if op[0] == "rnd":
                            k = op[1]
                            qt, oc16 = k // 16, k % 16
                            i = b * 4 + qt
                            if oc16 == 0:
                                self.vbar(v, v.tensor_scalar(
                                    self.TMP.ap()[:, 4:5], self.RMX.ap()[:, i:i + 1],
                                    1e-5, None, ALU.max, ALU.bypass))
                                v.reciprocal(self.TMP.ap()[:, 6:7], self.TMP.ap()[:, 4:5])
                                self.vbar(v, v.tensor_tensor(
                                    self.TMP.ap()[:, 5:6], self.V_sout.ap()[:, 0:1],
                                    self.TMP.ap()[:, 4:5], ALU.mult))
                                v.tensor_scalar(self.SC2.ap()[:, i:i + 1],
                                                self.TMP.ap()[:, 6:7],
                                                127.0, None, ALU.mult, ALU.bypass)
                                self.vbar(v, v.tensor_scalar(
                                    self.SO2.ap()[:, i:i + 1], self.TMP.ap()[:, 5:6],
                                    1.0 / 127.0, None, ALU.mult, ALU.bypass))
                            def pre(k=k):
                                lp = f"L{k % 4}"
                                v.wait_ge(self.sems[lp], c0[lp] + 16 * (k // 4 + 1))
                                if k >= 2:
                                    v.wait_ge(self.sems["PE"], PEn[("T", k - 2)])
                            hm = self.af(HM2 + (k % 4) * 512, 512)
                            rp_feed(pre, hm, self.SC2.ap()[:, i:i + 1],
                                    2 * AQN[k % 2])
                        else:
                            rp_flush()
                            _, j, qt = op
                            i = (b - 1) * 4 + qt
                            idx2 = j * 4 + qt
                            v.wait_ge(self.sems["PE"], PEn[("MM", j, 63)])
                            v.wait_ge(self.sems[f"X{idx2 % 2}"],
                                      c0[f"X{idx2 % 2}"] + 16 * (idx2 // 2 + 1))
                            ypar = f"Y{idx2 % 2}"
                            v.wait_ge(self.sems[ypar], c0[ypar] + 16 * (idx2 // 2))
                            xh = self.af(XH + (idx2 % 2) * 512, 512)
                            self.vbar(v, v.tensor_scalar(
                                xh, self.af(X1B7 + (idx2 % 2) * 512, 512),
                                self.ST1_a.ap()[:, i:i + 1],
                                self.ST1_b.ap()[:, i:i + 1], ALU.mult, ALU.add))
                            v.scalar_tensor_tensor(
                                self.af(YB + (idx2 % 2) * 512, 512),
                                self.PS_SC.ap()[:, qt * 512:(qt + 1) * 512],
                                self.SO2.ap()[:, i:i + 1], xh, ALU.mult,
                                ALU.add).then_inc(self.inc("DV", 1), 1)
                    rp_flush()

                @blk.tensor
                def _(te):
                    for op in pe_prog:
                        if op[0] == "T":
                            k = op[1]
                            qt = k // 16
                            qtg = b * 4 + qt
                            te.wait_ge(self.sems["DV"], DVn[("rnd", k)])
                            if k >= 2:
                                te.wait_ge(self.sems["AC"], ACn[("evac", k - 2)])
                            psb = (self.PS_C if k % 2 == 0
                                   else self.PS_D).ap().bitcast(BF16)
                            for sub in range(4):
                                _l = te.transpose(psb[:, sub * 128:(sub + 1) * 128],
                                                  self.bap(2 * AQN[k % 2] + sub * 128,
                                                           [[1, 128]]),
                                                  self.C_idb.ap())
                            _l.then_inc(self.inc("PE", 1), 1)
                        else:
                            _, j, oc = op
                            idx = j * 64 + oc
                            qtg0 = (b - 1) * 4
                            te.wait_ge(self.sems[TSEM[idx % 4]],
                                       c0[TSEM[idx % 4]] + 16 * (idx // 4 + 1))
                            if oc == 0 and j > 0:
                                te.wait_ge(self.sems["DV"], DVn[("yb", j - 1, 3)])
                            for qt in range(4):
                                _l = te.matmul(
                                    self.PS_SC.ap()[:, qt * 512:(qt + 1) * 512],
                                    self.bap(2 * AQ2T + oc * 1024 + (qtg0 + qt) * 128,
                                             [[1, 128]]),
                                    self.bap(2 * (TOB + (idx % 4) * 256), [[1, 512]]),
                                    start=(oc == 0), stop=(oc == 63))
                            _l.then_inc(self.inc("PE", 1), 1)

                @blk.scalar
                def _(sc):
                    for op in ac_prog:
                        k = op[1]
                        qt, oc16 = k // 16, k % 16
                        qtg = b * 4 + qt
                        sc.wait_ge(self.sems["PE"], PEn[("T", k)])
                        psb = (self.PS_C if k % 2 == 0
                               else self.PS_D).ap().bitcast(BF16)
                        dst = self.bap(2 * AQ2T + (oc16 * 4) * 1024 + qtg * 128,
                                       [[1024, 4], [1, 128]])
                        sc.copy(dst, psb[:, 0:512]
                                .rearrange("p (a c) -> p a c", a=4)
                                ).then_inc(self.inc("AC", 1), 1)

                if mm:
                    @blk.gpsimd
                    def _(gp):
                        for j in range(4):
                            for qt in range(4):
                                i = (b - 1) * 4 + qt
                                idx2 = j * 4 + qt
                                gp.wait_ge(self.sems["DV"], DVn[("yb", j, qt)])
                                self.dma(gp, self.y.ap()[i * 128:(i + 1) * 128,
                                                         j * 512:(j + 1) * 512],
                                         self.af(YB + (idx2 % 2) * 512, 512),
                                         f"Y{idx2 % 2}")

    def p_final(self):
        nc = self.nc
        with nc.Block("fin") as blk:
            @blk.gpsimd
            def _(gp):
                for s in self.DMA_SEMS:
                    if self.cnt[s]:
                        gp.wait_ge(self.sems[s], self.cnt[s])


def _host_inputs(core, X, ln_g, ln_b, Wq, bq, Wk, bk, Wv, bv, Wo, bo,
                 norm_w, w_gv, w_out):
    b, g = core // 2, core % 2
    f32 = np.float32
    rep = lambda v: np.ascontiguousarray(np.broadcast_to(np.asarray(v, f32), (128, D)))
    xb = np.ascontiguousarray(np.asarray(X, f32)[b])
    xqv = np.ascontiguousarray(xb.reshape(16, 128, D)[g::2].reshape(T, D))
    mask = np.zeros((NQT, 128, 256), f32)
    for i in range(NQT):
        qg = 256 * i + 128 * g + np.arange(128)[:, None]
        kg = 256 * i + np.arange(256)[None, :]
        mask[i] = np.where(kg <= qg, 0.0, NEGBIG)
    return {
        "x": xb, "xq": xqv,
        "wq": np.ascontiguousarray(np.asarray(Wq, f32)),
        "wk": np.ascontiguousarray(np.asarray(Wk, f32)),
        "wv": np.ascontiguousarray(np.asarray(Wv, f32)),
        "wo": np.ascontiguousarray(np.asarray(Wo, f32)),
        "bq_p": np.ascontiguousarray(np.asarray(bq, f32).reshape(H, 128).T),
        "bk_p": np.ascontiguousarray(np.asarray(bk, f32).reshape(H, 128).T),
        "bv_rep": rep(bv), "bo_rep": rep(bo), "g_rep": rep(ln_g),
        "b_rep": rep(ln_b), "nw_rep": rep(norm_w),
        "wgv": np.ascontiguousarray(np.asarray(w_gv, f32)),
        "wout": np.ascontiguousarray(np.asarray(w_out, f32)),
        "maskq": mask,
        "ident": np.eye(128, dtype=f32),
        "identb": np.eye(128, dtype=ml_dtypes.bfloat16),
    }


def kernel(L0_attn_X, L0_attn_ln_g, L0_attn_ln_b, L0_attn_Wq, L0_attn_bq,
           L0_attn_Wk, L0_attn_bk, L0_attn_Wv, L0_attn_bv, L0_attn_Wo,
           L0_attn_bo, L0_ffn_norm_w, L0_ffn_w_gv, L0_ffn_w_out):
    prog = Prog()
    nc = prog.build()
    in_maps = [
        _host_inputs(c, L0_attn_X, L0_attn_ln_g, L0_attn_ln_b, L0_attn_Wq,
                     L0_attn_bq, L0_attn_Wk, L0_attn_bk, L0_attn_Wv, L0_attn_bv,
                     L0_attn_Wo, L0_attn_bo, L0_ffn_norm_w, L0_ffn_w_gv,
                     L0_ffn_w_out)
        for c in range(8)
    ]
    res = run_bass_kernel_spmd(nc, in_maps, core_ids=list(range(8)))
    out = np.empty((4, S, D), np.float32)
    for c in range(8):
        b, g = c // 2, c % 2
        out[b].reshape(16, 128, D)[g::2] = res.results[c]["y"].reshape(NQT, 128, D)
    return out


# version 23
# speedup vs baseline: 1.0023x; 1.0023x over previous
import math
import numpy as np
import ml_dtypes
import concourse.bass as bass
import concourse.mybir as mybir
from concourse.bass_utils import run_bass_kernel_spmd

F32 = mybir.dt.float32
BF16 = mybir.dt.bfloat16
I32 = mybir.dt.int32
ACT = mybir.ActivationFunctionType
ALU = mybir.AluOpType
AX = mybir.AxisListType

S = 2048
D = 2048
H = 16
DH = 128
T = 1024
NQT = 8
HID = 8192
O2 = 16384
EPS = 1e-5
ISQ = 1.0 / math.sqrt(128.0)
NEGBIG = -1.0e38
STAGE_ALL = 99

OF_G, OF_B, OF_BV, OF_BO, OF_NW = 38912, 40960, 43008, 45056, 47104


class Prog:
    def __init__(self, stage=STAGE_ALL, dbg=False):
        self.stage = stage
        self.dbg = dbg
        nc = bass.Bass("TRN2", target_bir_lowering=False)
        self.nc = nc
        dram = lambda n, sh, dt=F32, kind="ExternalInput": nc.dram_tensor(n, sh, dt, kind=kind)
        skind = "ExternalOutput" if dbg else "Internal"

        self.x = dram("x", [S, D])
        self.xq = dram("xq", [T, D])
        self.wq = dram("wq", [D, D])
        self.wk = dram("wk", [D, D])
        self.wv = dram("wv", [D, D])
        self.wo = dram("wo", [D, D])
        self.bq_p = dram("bq_p", [128, H])
        self.bk_p = dram("bk_p", [128, H])
        self.bv_rep = dram("bv_rep", [128, D])
        self.bo_rep = dram("bo_rep", [128, D])
        self.g_rep = dram("g_rep", [128, D])
        self.b_rep = dram("b_rep", [128, D])
        self.nw_rep = dram("nw_rep", [128, D])
        self.wgv = dram("wgv", [O2, D])
        self.wout = dram("wout", [D, HID])
        self.maskq = dram("maskq", [NQT, 128, 256])
        self.ident = dram("ident", [128, 128])
        self.identb = dram("identb", [128, 128], BF16)
        self.y = dram("y", [T, D], kind="ExternalOutput")

        self.Kscr = dram("Kscr", [H, DH, S], kind=skind)
        self.Vscr = dram("Vscr", [H, S, DH], kind=skind)
        self.Qscr = dram("Qscr", [H, DH, T], kind=skind)
        self.X1scr = dram("X1scr", [T, D], kind=skind)
        self.HMID = dram("HMID", [T, HID], kind=skind)
        self.TOUT = dram("TOUT", [16, 64, 128, 128], BF16, kind="Internal")
        if dbg:
            self.dbg_s = dram("dbg_s", [128, 16], kind="ExternalOutput")

        self.AF = nc.alloc_sbuf_tensor("AF", [128, 49152], F32)
        fx = lambda n, cols, dt=F32: nc.alloc_sbuf_tensor(n, [128, cols], dt)
        self.C_bq = fx("C_bq", H)
        self.C_bqs = fx("C_bqs", H)
        self.C_bk = fx("C_bk", H)
        self.C_id = fx("C_id", 128)
        self.C_idb = fx("C_idb", 128, BF16)
        self.C_ones = fx("C_ones", 2)
        self.C_127 = fx("C_127", 1)
        self.V_sgv = fx("V_sgv", 4)
        self.V_sout = fx("V_sout", 4)
        self.ST_mu = fx("ST_mu", 16)
        self.ST_R = fx("ST_R", 16)
        self.STq_mu = fx("STq_mu", NQT)
        self.STq_R = fx("STq_R", NQT)
        self.ST1_mu = fx("ST1_mu", NQT)
        self.ST1_r1 = fx("ST1_r1", NQT)
        self.ST1_al = fx("ST1_al", NQT)
        self.ST1_a = fx("ST1_a", NQT)
        self.ST1_b = fx("ST1_b", NQT)
        self.C1c = fx("C1c", NQT)
        self.RMX = fx("RMX", NQT)
        self.SC2 = fx("SC2", NQT)
        self.SO2 = fx("SO2", NQT)
        self.SE = fx("SE", 4)
        self.TMP = fx("TMP", 16)
        self.BN6 = fx("BN6", 96)
        self.BN2 = fx("BN2", 4)
        self.PARTS = fx("PARTS", 392)

        self.PS_SC = nc.alloc_psum_tensor("PS_SC", [128, 2048], F32)
        self.PS_A = nc.alloc_psum_tensor("PS_A", [128, 512], F32)
        self.PS_B = nc.alloc_psum_tensor("PS_B", [128, 512], F32)
        self.PS_C = nc.alloc_psum_tensor("PS_C", [128, 512], F32)
        self.PS_D = nc.alloc_psum_tensor("PS_D", [128, 512], F32)

        self.sems = {}
        self.cnt = {}
        self._sem_cms = []
        for name in ["LD", "CR", "DV", "AC", "AE", "PE", "GP", "ST", "Y", "VH",
                     "L0", "L1", "L2", "L3", "X0", "X1", "G0", "G1",
                     "S0", "S1", "Y0", "Y1", "H0", "H1"]:
            cm = nc.semaphore(name)
            self.sems[name] = cm.__enter__()
            self._sem_cms.append(cm)
            self.cnt[name] = 0

    def af(self, off, cols):
        return self.AF.ap()[:, off:off + cols]

    def afb(self, off_f32, cols_bf16):
        return self.AF.ap().bitcast(BF16)[:, 2 * off_f32: 2 * off_f32 + cols_bf16]

    def af3(self, off, stride, n, inner):
        return bass.AP(self.AF, off, [[49152, 128], [stride, n], [1, inner]])

    def bap(self, off_b, dims_b):
        f32dims = [[s // 2, n] for (s, n) in dims_b[:-1]] + \
                  [[1, dims_b[-1][1] // 2]]
        return bass.AP(self.AF, off_b // 2,
                       [[49152, 128]] + f32dims).bitcast(BF16)

    DMA_SEMS = ["LD", "GP", "ST", "Y", "L0", "L1", "L2", "L3", "X0", "X1",
                "G0", "G1", "S0", "S1", "Y0", "Y1", "H0", "H1"]

    def fence_block(self, name="fence"):
        with self.nc.Block(name) as blk:
            @blk.sync
            def _(sy):
                for s in self.DMA_SEMS + ["DV", "AC", "AE", "PE", "CR", "VH"]:
                    if self.cnt[s]:
                        sy.wait_ge(self.sems[s], self.cnt[s])

    def inc(self, name, n=16):
        self.cnt[name] += n
        return self.sems[name]

    def dma(self, eng, dst, src, sem="LD"):
        eng.dma_start(dst, src).then_inc(self.inc(sem, 16), 16)

    def vmark(self, inst):
        inst.then_inc(self.inc("VH", 1), 1)
        return self.cnt["VH"]

    def vbar(self, v, inst):
        v.wait_ge(self.sems["VH"], self.vmark(inst))

    P1B = 31744
    P1QS = ["G0", "G1", "GP"]

    def p1_buf(self, k):
        return self.af(self.P1B + (k % 3) * 2048, 2048)

    def p1_chunk_src(self, k):
        if k < 128:
            return self.wgv.ap()[k * 128:(k + 1) * 128, :]
        r, cc = divmod(k - 128, 4)
        return self.wout.ap()[r * 128:(r + 1) * 128, cc * 2048:(cc + 1) * 2048]

    def p1_load_sw(self, gp, k):
        if k >= 3:
            gp.wait_ge(self.sems["CR"], self._p1cr0 + k - 2)
        self.dma(gp, self.p1_buf(k), self.p1_chunk_src(k), self.P1QS[k % 3])

    def p1_reduce_sw(self, sc):
        k = self._p1k_sw
        self._p1k_sw += 1
        q = self.P1QS[k % 3]
        sc.wait_ge(self.sems[q], self._p1q0[q] + 16 * (k // 3 + 1))
        buf = self.p1_buf(k)
        sc.activation(buf, buf, ACT.Abs,
                      accum_out=self.PARTS.ap()[:, k:k + 1]
                      ).then_inc(self.inc("CR", 1), 1)

    def stats(self, v, off, cols=2048):
        nchunk = cols // 512
        last = None
        for cc in range(nchunk):
            last = v.bn_stats(self.BN6.ap()[:, cc * 6:(cc + 1) * 6],
                              self.af(off + cc * 512, 512))
        self.vbar(v, last)
        self.vbar(v, v.bn_aggr(self.BN2.ap()[:, 0:2], self.BN6.ap()[:, 0:6 * nchunk]))

    def rp_pipe(self, v, zb, cols):
        P = self
        st = {"pend": None, "p": 0}

        def zoff(p, j):
            return zb + p * 3 * cols + j * cols

        def s3(item):
            p, dst_b, m2 = item
            v.wait_ge(P.sems["VH"], m2)
            v.scalar_tensor_tensor(
                P.bap(dst_b, [[1, cols]]),
                P.af(zoff(p, 1), cols).bitcast(I32), -128.0,
                P.af(zoff(p, 2), cols), ALU.add,
                ALU.add).then_inc(P.inc("DV", 1), 1)

        def feed(pre, src, scale_col, dst_b):
            prev = st["pend"]
            if prev:
                s3(prev)
            p = st["p"]
            st["p"] ^= 1
            pre()
            m0 = P.vmark(v.tensor_scalar(P.af(zoff(p, 0), cols), src, scale_col,
                                         128.0, ALU.mult, ALU.add))
            v.wait_ge(P.sems["VH"], m0)
            m1 = P.vmark(v.tensor_copy(P.af(zoff(p, 1), cols).bitcast(I32),
                                       P.af(zoff(p, 0), cols)))
            v.wait_ge(P.sems["VH"], m1)
            m2 = P.vmark(v.scalar_tensor_tensor(
                P.af(zoff(p, 2), cols), P.af(zoff(p, 0), cols), -0.5,
                P.af(zoff(p, 1), cols).bitcast(I32), ALU.add, ALU.is_gt))
            st["pend"] = (p, dst_b, m2)

        def flush():
            prev = st["pend"]
            if prev is not None:
                s3(prev)
            st["pend"] = None
        return feed, flush

    def round_pos(self, v, src, scale_col, z0, z1, z2, dst_b, cols):
        self.vbar(v, v.tensor_scalar(self.af(z0, cols), src, scale_col, 128.0,
                                     ALU.mult, ALU.add))
        self.vbar(v, v.tensor_copy(self.af(z1, cols).bitcast(I32),
                                   self.af(z0, cols)))
        self.vbar(v, v.scalar_tensor_tensor(self.af(z2, cols),
                                            self.af(z0, cols), -0.5,
                                            self.af(z1, cols).bitcast(I32),
                                            ALU.add, ALU.is_gt))
        return v.scalar_tensor_tensor(self.bap(dst_b, [[1, cols]]),
                                      self.af(z1, cols).bitcast(I32), -128.0,
                                      self.af(z2, cols), ALU.add, ALU.add)

    def build(self):
        self.p0_consts()
        if self.stage == 1:
            self.p1_absum()
        if self.stage >= 2:
            self.p2_proj(True)
        if self.stage >= 3:
            self.p2_proj(False)
        if self.stage >= 4:
            self.p3_attn()
        if self.stage >= 5:
            self.p4_ffn_a()
        if self.stage >= 6:
            self.p5_gv()
        if self.stage >= 7:
            self.p6_tout()
            self.p7_out2()
        self.p_final()
        for cm in reversed(self._sem_cms):
            cm.__exit__(None, None, None)
        return self.nc

    def p0_consts(self):
        nc = self.nc
        with nc.Block("p0") as blk:
            @blk.sync
            def _(sy):
                self.dma(sy, self.C_bq.ap(), self.bq_p.ap())
                self.dma(sy, self.C_bk.ap(), self.bk_p.ap())
                self.dma(sy, self.C_id.ap(), self.ident.ap())
                self.dma(sy, self.C_idb.ap(), self.identb.ap())
                self.dma(sy, self.af(OF_G, D), self.g_rep.ap())
                self.dma(sy, self.af(OF_B, D), self.b_rep.ap())
                self.dma(sy, self.af(OF_BV, D), self.bv_rep.ap())
                self.dma(sy, self.af(OF_BO, D), self.bo_rep.ap())
                self.dma(sy, self.af(OF_NW, D), self.nw_rep.ap())

            @blk.vector
            def _(v):
                v.memset(self.C_ones.ap()[:, 0:1], 1.0)
                v.memset(self.C_127.ap(), 127.0)
                v.memset(self.RMX.ap(), 0.0)
                v.memset(self.TMP.ap(), 0.0)
                v.memset(self.SE.ap(), 0.0)

            @blk.gpsimd
            def _(gp):
                gp.wait_ge(self.sems["LD"], self.cnt["LD"])

    def p1_absum(self):
        nc = self.nc
        tiles = []
        for r in range(O2 // 128):
            tiles.append((self.wgv.ap()[r * 128:(r + 1) * 128, :], 0))
        for r in range(D // 128):
            for cc in range(4):
                tiles.append((self.wout.ap()[r * 128:(r + 1) * 128,
                                             cc * 2048:(cc + 1) * 2048], 1))
        NB = 4
        self.fence_block("f_p1")
        cr0 = self.cnt["CR"]
        c0 = dict(self.cnt)
        with nc.Block("p1") as blk:
            @blk.sync
            def _(sy):
                for t, (src, _) in enumerate(tiles):
                    if t >= NB:
                        sy.wait_ge(self.sems["CR"], cr0 + (t - NB + 1))
                    self.dma(sy, self.af((t % NB) * 2048, 2048), src,
                             f"L{t % 4}")

            PARTS = 4 * 2048

            @blk.vector
            def _(v):
                for t, (_, j) in enumerate(tiles):
                    v.wait_ge(self.sems[f"L{t % 4}"],
                              c0[f"L{t % 4}"] + 16 * (t // 4 + 1))
                    v.tensor_reduce(self.af(PARTS + t, 1), self.af((t % NB) * 2048, 2048),
                                    AX.X, ALU.add,
                                    apply_absolute_value=True).then_inc(self.inc("CR", 1), 1)
                v.wait_ge(self.sems["CR"], cr0 + len(tiles))
                self.vbar(v, v.tensor_reduce(self.TMP.ap()[:, 0:1],
                                             self.af(PARTS, 128), AX.X, ALU.add))
                self.vbar(v, v.tensor_reduce(self.TMP.ap()[:, 1:2],
                                             self.af(PARTS + 128, 64), AX.X, ALU.add))
                v.nop().then_inc(self.inc("CR", 1), 1)

            @blk.tensor
            def _(te):
                te.wait_ge(self.sems["CR"], cr0 + len(tiles) + 1)
                te.matmul(self.PS_A.ap()[0:1, 0:1], self.TMP.ap()[:, 0:1],
                          self.C_ones.ap()[:, 0:1], start=True, stop=True)
                te.matmul(self.PS_B.ap()[0:1, 0:1], self.TMP.ap()[:, 1:2],
                          self.C_ones.ap()[:, 0:1], start=True, stop=True).then_inc(self.inc("PE", 1), 1)

            @blk.scalar
            def _(sc):
                sc.wait_ge(self.sems["PE"], self.cnt["PE"])
                sc.copy(self.TMP.ap()[0:1, 8:9], self.PS_A.ap()[0:1, 0:1])
                sc.copy(self.TMP.ap()[0:1, 9:10], self.PS_B.ap()[0:1, 0:1]).then_inc(self.inc("AC", 1), 1)

            @blk.tensor
            def _(te):
                te.wait_ge(self.sems["AC"], self.cnt["AC"])
                te.matmul(self.PS_C.ap()[:, 0:2],
                          self.C_ones.ap()[0:1, 0:1].to_broadcast((1, 128)),
                          self.TMP.ap()[0:1, 8:10], start=True, stop=True).then_inc(self.inc("PE", 1), 1)

            @blk.vector
            def _(v):
                v.wait_ge(self.sems["PE"], self.cnt["PE"])
                v.tensor_scalar(self.V_sgv.ap()[:, 0:1], self.PS_C.ap()[:, 0:1],
                                1.0 / (O2 * D), 1e-5, ALU.mult, ALU.max)
                self.vbar(v, v.tensor_scalar(self.V_sout.ap()[:, 0:1],
                                             self.PS_C.ap()[:, 1:2],
                                             1.0 / (D * HID), 1e-5, ALU.mult, ALU.max))
                insts = []
                for vv in (self.V_sgv, self.V_sout):
                    insts.append(v.tensor_scalar(vv.ap()[:, 1:2], vv.ap()[:, 0:1],
                                                 0.5, None, ALU.mult, ALU.bypass))
                    insts.append(v.tensor_scalar(vv.ap()[:, 2:3], vv.ap()[:, 0:1],
                                                 -0.5, None, ALU.mult, ALU.bypass))
                insts[-1].then_inc(self.inc("DV", 1), 1)

            if self.dbg:
                @blk.scalar
                def _(sc):
                    sc.wait_ge(self.sems["DV"], self.cnt["DV"])
                    sc.copy(self.TMP.ap()[:, 12:13], self.V_sgv.ap()[:, 0:1])
                    sc.copy(self.TMP.ap()[:, 13:14], self.V_sout.ap()[:, 0:1]).then_inc(self.inc("AC", 1), 1)
                    sc.wait_ge(self.sems["AC"], self.cnt["AC"])

                @blk.gpsimd
                def _(gp):
                    gp.wait_ge(self.sems["AC"], self.cnt["AC"])
                    self.dma(gp, self.dbg_s.ap(), self.TMP.ap()[:, 0:16], "GP")

    def p2_proj(self, is_q):
        nc = self.nc
        ntile = NQT if is_q else 16
        ncols = ntile * 128
        xsrc = self.xq if is_q else self.x
        mu_st, R_st = (self.STq_mu, self.STq_R) if is_q else (self.ST_mu, self.ST_R)
        XNTH = 0
        XNTM = 8 * ncols
        XB = 16 * ncols
        XBH = XB + 2048
        XBM = XBH + 1024
        tag = "q" if is_q else "kv"

        XB0 = (16384, 18432) if is_q else (32768, 34816)
        XBH = 20480 if is_q else 36864
        XBM = 21504 if is_q else 37888
        self.fence_block(f"f_p2{tag}x")
        if is_q:
            self._p1k_sw = 0
            self._p1cr0 = self.cnt["CR"]
            self._p1q0 = dict(self.cnt)
        ld0, dv0, ac0, ae0, pe0 = (self.cnt[k] for k in ["LD", "DV", "AC", "AE", "PE"])
        cx0 = dict(self.cnt)
        d_split = lambda t: dv0 + 2 * t + 2
        a_sqrt = lambda t: ac0 + (1 if t == 0 else 9 * t - 7)
        a_evac = lambda t, r: ac0 + 9 * t + 3 + r
        with nc.Block(f"p2{tag}_xn") as blk:
            @blk.sync
            def _(sy):
                for t in range(ntile):
                    if t >= 2:
                        sy.wait_ge(self.sems["DV"], d_split(t - 2))
                    self.dma(sy, self.af(XB0[t % 2], 2048),
                             xsrc.ap()[t * 128:(t + 1) * 128, :], f"L{t % 2}")
                    if is_q and t < 2:
                        for c in range(16):
                            self.dma(sy, self.af((22528, 24576)[t] + c * 128, 128),
                                     self.wq.ap()[c * 128:(c + 1) * 128,
                                                  t * 128:(t + 1) * 128],
                                     f"L{2 + t}")

            if is_q:
                @blk.gpsimd
                def _(gp):
                    for k in range(27):
                        self.p1_load_sw(gp, k)

            @blk.vector
            def _(v):
                if is_q:
                    v.tensor_scalar(self.C_bqs.ap(), self.C_bq.ap(), ISQ, None,
                                    ALU.mult, ALU.bypass)
                m = {}
                for slot in range(ntile + 1):
                    tn, to = slot, slot - 1
                    pn, po = tn % 2, to % 2
                    if to >= 0:
                        v.wait_ge(self.sems["VH"], m["st", to])
                        self.vbar(v, v.bn_aggr(self.BN2.ap()[:, 2 * po:2 * po + 2],
                                               self.BN6.ap()[:, 24 * po:24 * po + 24]))
                        m["mu", to] = self.vmark(v.tensor_copy(
                            mu_st.ap()[:, to:to + 1], self.BN2.ap()[:, 2 * po:2 * po + 1]))
                        v.tensor_scalar(self.TMP.ap()[:, 4 + po:5 + po],
                                        self.BN2.ap()[:, 2 * po + 1:2 * po + 2],
                                        1.0 + EPS, EPS * EPS, ALU.mult,
                                        ALU.add).then_inc(self.inc("DV", 1), 1)
                    if tn < ntile:
                        v.wait_ge(self.sems[f"L{tn % 2}"],
                                  cx0[f"L{tn % 2}"] + 16 * (tn // 2 + 1))
                        v.bn_stats(self.BN6.ap()[:, 24 * pn:24 * pn + 6],
                                   self.af(XB0[pn], 512))
                    if to >= 0:
                        v.wait_ge(self.sems["AC"], a_sqrt(to))
                        m["rc", to] = self.vmark(v.reciprocal(
                            R_st.ap()[:, to:to + 1], self.TMP.ap()[:, 6 + po:7 + po]))
                    if tn < ntile:
                        v.bn_stats(self.BN6.ap()[:, 24 * pn + 6:24 * pn + 12],
                                   self.af(XB0[pn] + 512, 512))
                    if to >= 0:
                        v.wait_ge(self.sems["VH"], m["mu", to])
                        m["s1", to] = self.vmark(v.scalar_tensor_tensor(
                            self.af(XB0[po], 2048), self.af(XB0[po], 2048),
                            mu_st.ap()[:, to:to + 1], self.af(OF_G, D),
                            ALU.subtract, ALU.mult))
                    if tn < ntile:
                        v.bn_stats(self.BN6.ap()[:, 24 * pn + 12:24 * pn + 18],
                                   self.af(XB0[pn] + 1024, 512))
                    if to >= 0:
                        v.wait_ge(self.sems["VH"], m["s1", to])
                        m["s2", to] = self.vmark(v.scalar_tensor_tensor(
                            self.af(XB0[po], 2048), self.af(XB0[po], 2048),
                            R_st.ap()[:, to:to + 1], self.af(OF_B, D),
                            ALU.mult, ALU.add))
                    if tn < ntile:
                        m["st", tn] = self.vmark(v.bn_stats(
                            self.BN6.ap()[:, 24 * pn + 18:24 * pn + 24],
                            self.af(XB0[pn] + 1536, 512)))
                    if to >= 0:
                        if to >= 1:
                            v.wait_ge(self.sems["PE"], pe0 + 8 * to)
                        v.wait_ge(self.sems["VH"], m["s2", to])
                        m["ha", to] = self.vmark(v.tensor_copy(
                            self.bap(2 * XBH, [[1, 1024]]), self.af(XB0[po], 1024)))
                        m["hb", to] = self.vmark(v.tensor_copy(
                            self.bap(2 * XBH + 1024, [[1, 1024]]),
                            self.af(XB0[po] + 1024, 1024)))
                        v.wait_ge(self.sems["VH"], m["ha", to])
                        v.scalar_tensor_tensor(self.bap(2 * XBM, [[1, 1024]]),
                                               self.af(XB0[po], 1024), 0.0,
                                               self.bap(2 * XBH, [[1, 1024]]),
                                               ALU.add, ALU.subtract)
                        v.wait_ge(self.sems["VH"], m["hb", to])
                        v.scalar_tensor_tensor(self.bap(2 * XBM + 1024, [[1, 1024]]),
                                               self.af(XB0[po] + 1024, 1024), 0.0,
                                               self.bap(2 * XBH + 1024, [[1, 1024]]),
                                               ALU.add, ALU.subtract
                                               ).then_inc(self.inc("DV", 1), 1)
                if not is_q:
                    v.wait_ge(self.sems["CR"], self._p1cr0 + 192)
                    v.tensor_reduce(self.TMP.ap()[:, 0:1], self.PARTS.ap()[:, 0:128],
                                    AX.X, ALU.add)
                    v.tensor_reduce(self.TMP.ap()[:, 1:2], self.PARTS.ap()[:, 128:192],
                                    AX.X, ALU.add).then_inc(self.inc("DV", 1), 1)
                    v.wait_ge(self.sems["PE"], pe0 + 8 * ntile + 2)
                    v.tensor_scalar(self.V_sgv.ap()[:, 0:1], self.PS_A.ap()[:, 4:5],
                                    1.0 / (O2 * D), 1e-5, ALU.mult, ALU.max)
                    self.vbar(v, v.tensor_scalar(self.V_sout.ap()[:, 0:1],
                                                 self.PS_A.ap()[:, 5:6],
                                                 1.0 / (D * HID), 1e-5, ALU.mult,
                                                 ALU.max))
                    insts = []
                    for vv in (self.V_sgv, self.V_sout):
                        insts.append(v.tensor_scalar(vv.ap()[:, 1:2], vv.ap()[:, 0:1],
                                                     0.5, None, ALU.mult, ALU.bypass))
                        insts.append(v.tensor_scalar(vv.ap()[:, 2:3], vv.ap()[:, 0:1],
                                                     -0.5, None, ALU.mult, ALU.bypass))
                    insts[-1].then_inc(self.inc("DV", 1), 1)

            @blk.tensor
            def _(te):
                for t in range(ntile):
                    te.wait_ge(self.sems["DV"], d_split(t))
                    for hm in range(2):
                        srcb = XBH if hm == 0 else XBM
                        for cg in range(4):
                            idx = 8 * t + 4 * hm + cg
                            if idx >= 2:
                                tp_, r = divmod(idx - 2, 8)
                                te.wait_ge(self.sems["AC"], a_evac(tp_, r))
                            psb = (self.PS_C if idx % 2 == 0
                                   else self.PS_D).ap().bitcast(BF16)
                            for c4 in range(4):
                                c = 4 * cg + c4
                                _t = te.transpose(psb[:, c4 * 128:(c4 + 1) * 128],
                                                  self.bap(2 * srcb + c * 128,
                                                           [[1, 128]]),
                                                  self.C_idb.ap())
                            _t.then_inc(self.inc("PE", 1), 1)
                if not is_q:
                    te.wait_ge(self.sems["DV"], dv0 + 2 * ntile + 1)
                    te.matmul(self.PS_A.ap()[0:1, 0:1], self.TMP.ap()[:, 0:1],
                              self.C_ones.ap()[:, 0:1], start=True, stop=True)
                    te.matmul(self.PS_B.ap()[0:1, 0:1], self.TMP.ap()[:, 1:2],
                              self.C_ones.ap()[:, 0:1], start=True,
                              stop=True).then_inc(self.inc("PE", 1), 1)
                    te.wait_ge(self.sems["AC"], ac0 + 9 * ntile + 1)
                    te.matmul(self.PS_A.ap()[:, 4:6],
                              self.C_ones.ap()[0:1, 0:1].to_broadcast((1, 128)),
                              self.TMP.ap()[0:1, 8:10], start=True,
                              stop=True).then_inc(self.inc("PE", 1), 1)

            @blk.scalar
            def _(sc):
                for t in range(ntile + 1):
                    if t < ntile:
                        sc.wait_ge(self.sems["DV"], dv0 + 2 * t + 1)
                        sc.activation(self.TMP.ap()[:, 6 + t % 2:7 + t % 2],
                                      self.TMP.ap()[:, 4 + t % 2:5 + t % 2],
                                      ACT.Sqrt).then_inc(self.inc("AC", 1), 1)
                    if t >= 1:
                        to = t - 1
                        for hm in range(2):
                            dstb = XNTH if hm == 0 else XNTM
                            for cg in range(4):
                                idx = 8 * to + 4 * hm + cg
                                sc.wait_ge(self.sems["PE"], pe0 + idx + 1)
                                psb = (self.PS_C if idx % 2 == 0
                                       else self.PS_D).ap().bitcast(BF16)
                                dst = self.bap(2 * dstb + (4 * cg) * ncols + to * 128,
                                               [[ncols, 4], [1, 128]])
                                sc.copy(dst, psb[:, 0:512]
                                        .rearrange("p (a c) -> p a c", a=4)
                                        ).then_inc(self.inc("AC", 1), 1)
                    if is_q and t < ntile:
                        for _ in range(3):
                            self.p1_reduce_sw(sc)
                if not is_q:
                    sc.wait_ge(self.sems["PE"], pe0 + 8 * ntile + 1)
                    sc.copy(self.TMP.ap()[0:1, 8:9], self.PS_A.ap()[0:1, 0:1])
                    sc.copy(self.TMP.ap()[0:1, 9:10],
                            self.PS_B.ap()[0:1, 0:1]).then_inc(self.inc("AC", 1), 1)

        ngrp = (T // 512) if is_q else (S // 512)
        wsrc = self.wq if is_q else self.wk
        bias = self.C_bqs if is_q else self.C_bk
        evsc = ISQ if is_q else 1.0
        dst_scr = self.Qscr if is_q else self.Kscr
        if is_q:
            WF = (22528, 24576)
            WH = (26624, 27648)
            WM = (28672, 29696)
            EVQ = 30720
        else:
            WF = (32768, 34816)
            WH = (36864, 37888)
            WM = (OF_G, OF_G + 1024)
            EVQ = OF_B
        self.fence_block(f"f_p2{tag}qk")
        ae0, pe0, dv0 = (self.cnt[k] for k in ["AE", "PE", "DV"])
        c0 = dict(self.cnt)
        with nc.Block(f"p2{tag}_qk") as blk:
            @blk.vector
            def _(v):
                for h in range(H):
                    if is_q and h < 2:
                        pass
                    else:
                        lp = f"L{h % 2}"
                        off = -256 if is_q else 0
                        v.wait_ge(self.sems[lp], c0[lp] + off + 16 * 16 * (h // 2 + 1))
                    if h >= 2:
                        v.wait_ge(self.sems["PE"], pe0 + (h - 1) * ngrp)
                    self.vbar(v, v.tensor_copy(self.bap(2 * WH[h % 2], [[1, 2048]]),
                                               self.af(WF[h % 2], 2048)))
                    v.scalar_tensor_tensor(self.bap(2 * WM[h % 2], [[1, 2048]]),
                                           self.af(WF[h % 2], 2048), 0.0,
                                           self.bap(2 * WH[h % 2], [[1, 2048]]),
                                           ALU.add, ALU.subtract
                                           ).then_inc(self.inc("DV", 1), 1)

            @blk.sync
            def _(sy):
                for h in range(H):
                    if is_q and h < 2:
                        continue
                    if h >= 2:
                        sy.wait_ge(self.sems["DV"], dv0 + h - 1)
                    for c in range(16):
                        self.dma(sy, self.af(WF[h % 2] + c * 128, 128),
                                 wsrc.ap()[c * 128:(c + 1) * 128, h * 128:(h + 1) * 128],
                                 f"L{h % 2}")

            if is_q:
                @blk.gpsimd
                def _(gp):
                    for k in range(27, 192):
                        self.p1_load_sw(gp, k)

            @blk.tensor
            def _(te):
                for h in range(H):
                    te.wait_ge(self.sems["DV"], dv0 + h + 1)
                    for grp in range(ngrp):
                        idx = h * ngrp + grp
                        if idx >= 2:
                            te.wait_ge(self.sems["AE"], ae0 + idx - 1)
                        ps = (self.PS_C if idx % 2 == 0 else self.PS_D).ap()
                        for c in range(16):
                            for (wb, xb) in ((WH, XNTH), (WH, XNTM), (WM, XNTH)):
                                _pe_last = te.matmul(
                                    ps,
                                    self.bap(2 * wb[h % 2] + c * 128, [[1, 128]]),
                                    self.bap(2 * xb + c * ncols + grp * 512,
                                             [[1, 512]]),
                                    start=(c == 0 and wb is WH and xb is XNTH),
                                    stop=(c == 15 and xb is XNTH and wb is WM))
                        _pe_last.then_inc(self.inc("PE", 1), 1)

            @blk.scalar
            def _(sc):
                for h in range(H):
                    for grp in range(ngrp):
                        idx = h * ngrp + grp
                        sc.wait_ge(self.sems["PE"], pe0 + idx + 1)
                        sp = f"S{idx % 2}"
                        if idx >= 2:
                            sc.wait_ge(self.sems[sp], c0[sp] + 16 * (idx // 2))
                        ps = (self.PS_C if idx % 2 == 0 else self.PS_D).ap()
                        ev = self.af(EVQ + (idx % 2) * 512, 512)
                        sc.activation(ev, ps, ACT.Identity, bias=bias.ap()[:, h:h + 1],
                                      scale=evsc).then_inc(self.inc("AE", 1), 1)
                        sc.wait_ge(self.sems["AE"], self.cnt["AE"])
                        self.dma(sc, dst_scr.ap()[h, :, grp * 512:(grp + 1) * 512],
                                 ev, sp)
                        if is_q:
                            quota = 24 + ((idx + 1) * 168 + 31) // 32
                            while self._p1k_sw < quota:
                                self.p1_reduce_sw(sc)

        if is_q:
            return

        WVF = 32768
        WVH = 33792
        WVM = 37888
        EV2 = 41984
        self.fence_block("f_p2v")
        dv0, pe0 = (self.cnt[k] for k in ["DV", "PE"])
        c0 = dict(self.cnt)

        def ev_dv(idx):
            return dv0 + 32 * (idx // 16) + 16 + (idx % 16) + 1

        with nc.Block("p2kv_v") as blk:
            @blk.sync
            def _(sy):
                for oc in range(4):
                    for c in range(16):
                        li = oc * 16 + c
                        if li >= 2:
                            sy.wait_ge(self.sems["DV"],
                                       dv0 + 32 * ((li - 2) // 16) + ((li - 2) % 16) + 1)
                        self.dma(sy, self.af(WVF + (li % 2) * 512, 512),
                                 self.wv.ap()[c * 128:(c + 1) * 128,
                                              oc * 512:(oc + 1) * 512],
                                 f"L{li % 2}")

            @blk.vector
            def _(v):
                for oc in range(4):
                    for c in range(16):
                        li = oc * 16 + c
                        lp = f"L{li % 2}"
                        v.wait_ge(self.sems[lp], c0[lp] + 16 * (li // 2 + 1))
                        if oc > 0 and c == 0:
                            v.wait_ge(self.sems["PE"], pe0 + oc * 16)
                        w = self.af(WVF + (li % 2) * 512, 512)
                        self.vbar(v, v.tensor_copy(
                            self.bap(2 * (WVH + c * 256), [[1, 512]]), w))
                        v.scalar_tensor_tensor(
                            self.bap(2 * (WVM + c * 256), [[1, 512]]), w, 0.0,
                            self.bap(2 * (WVH + c * 256), [[1, 512]]),
                            ALU.add, ALU.subtract).then_inc(self.inc("DV", 1), 1)
                    for tk in range(16):
                        idx = oc * 16 + tk
                        v.wait_ge(self.sems["PE"], pe0 + idx + 1)
                        sp = f"G{idx % 2}"
                        v.wait_ge(self.sems[sp], c0[sp] + 64 * (idx // 2))
                        ps = (self.PS_C if idx % 2 == 0 else self.PS_D).ap()
                        ev = self.af(EV2 + (idx % 2) * 512, 512)
                        v.tensor_tensor(ev, ps, self.af(OF_BV + oc * 512, 512),
                                        ALU.add).then_inc(self.inc("DV", 1), 1)

            @blk.tensor
            def _(te):
                for oc in range(4):
                    te.wait_ge(self.sems["DV"], dv0 + 32 * oc + 16)
                    for tk in range(16):
                        idx = oc * 16 + tk
                        if idx >= 2:
                            te.wait_ge(self.sems["DV"], ev_dv(idx - 2))
                        ps = (self.PS_C if idx % 2 == 0 else self.PS_D).ap()
                        for c in range(16):
                            for (xb, wv_) in ((XNTH, WVH), (XNTM, WVH), (XNTH, WVM)):
                                _pe_last = te.matmul(
                                    ps,
                                    self.bap(2 * xb + c * ncols + tk * 128, [[1, 128]]),
                                    self.bap(2 * (wv_ + c * 256), [[1, 512]]),
                                    start=(c == 0 and xb is XNTH and wv_ is WVH),
                                    stop=(c == 15 and xb is XNTH and wv_ is WVM))
                        _pe_last.then_inc(self.inc("PE", 1), 1)

            @blk.gpsimd
            def _(gp):
                for oc in range(4):
                    for tk in range(16):
                        idx = oc * 16 + tk
                        gp.wait_ge(self.sems["DV"], ev_dv(idx))
                        ev = self.af(EV2 + (idx % 2) * 512, 512)
                        sp = f"G{idx % 2}"
                        for j in range(4):
                            h = oc * 4 + j
                            self.dma(gp, self.Vscr.ap()[h, tk * 128:(tk + 1) * 128, :],
                                     ev[:, j * 128:(j + 1) * 128], sp)

    def p3_attn(self):
        nc = self.nc
        PB0, AVT = 0, 16384
        KB0, VB0, QB0, PTB0, MASKS = 24576, 28672, 34816, 35840, 36864
        X1B, WOB, X0B = 0, 8192, 24576
        st_done = self.cnt["ST"]
        gp_done = self.cnt["GP"]
        VBS = ["X1", "S0", "S1"]

        for s in range(2):
            Rs = 1024 * (s + 1)
            nkt = Rs // 128
            self.fence_block(f"f_p3s{s}")
            c0 = dict(self.cnt)

            def rngq(qt):
                return 256 * (4 * s + qt + 1)

            def qlo_of(kt):
                q = 0
                while 256 * (4 * s + q + 1) <= kt * 128:
                    q += 1
                return q

            pe_prog, dv_prog, ac_prog = [], [], []
            for i in range(17):
                do_sm, do_kt = i <= 15, i >= 1
                h, hp = i, i - 1
                scpos = {(qt * nkt) // 4: qt for qt in range(4)}
                if do_sm and not do_kt:
                    for qt in range(4):
                        pe_prog.append(("sc", h, qt))
                    for qt in range(4):
                        ac_prog.append(("exp", h, qt))
                if do_kt:
                    pe_prog.append(("T", hp, 0))
                    for kt in range(nkt):
                        if do_sm and kt in scpos:
                            pe_prog.append(("sc", h, scpos[kt]))
                        if kt + 1 < nkt:
                            pe_prog.append(("T", hp, kt + 1))
                        pe_prog.append(("PV", hp, kt))
                    for kt in range(nkt):
                        ac_prog.append(("evac", hp, kt))
                        if do_sm and kt in scpos:
                            ac_prog.append(("exp", h, scpos[kt]))
                if do_sm:
                    for kind, qt in [("bTT", 0), ("bRed", 0),
                                     ("bTT", 1), ("bRed", 1), ("rcp", 0),
                                     ("bTT", 2), ("bRed", 2), ("rcp", 1),
                                     ("nTS", 0),
                                     ("bTT", 3), ("bRed", 3), ("rcp", 2),
                                     ("nTS", 1), ("rcp", 3),
                                     ("nTS", 2), ("nTS", 3)]:
                        dv_prog.append((kind, h, qt))
                if do_kt:
                    dv_prog.append(("avt", hp))
            PEn = {op: c0["PE"] + k + 1 for k, op in enumerate(pe_prog)}
            DVn = {op: c0["DV"] + k + 1 for k, op in enumerate(dv_prog)}
            ACn = {op: c0["AC"] + k + 1 for k, op in enumerate(ac_prog)}

            with nc.Block(f"p3_s{s}") as blk:
                @blk.sync
                def _(sy):
                    if s == 0:
                        sy.wait_ge(self.sems["ST"], st_done)
                        sy.wait_ge(self.sems["GP"], gp_done)
                        for i in range(NQT):
                            self.dma(sy, self.af(MASKS + i * 256, 256),
                                     self.maskq.ap()[i, :, :], "X0")
                    for h in range(H):
                        if h >= 2:
                            sy.wait_ge(self.sems["PE"], PEn[("sc", h - 2, 3)])
                        self.dma(sy, self.af(KB0 + (h % 2) * 2048, Rs),
                                 self.Kscr.ap()[h, :, 0:Rs], f"L{h % 2}")
                        self.dma(sy, self.af(QB0 + (h % 2) * 512, 512),
                                 self.Qscr.ap()[h, :, s * 512:(s + 1) * 512],
                                 f"L{2 + h % 2}")
                        if h >= 3:
                            sy.wait_ge(self.sems["PE"], PEn[("PV", h - 3, nkt - 1)])
                        vsrc = bass.AP(self.Vscr, h * S * DH,
                                       [[DH, 128], [128 * DH, nkt], [1, DH]])
                        self.dma(sy, self.af3(VB0 + (h % 3) * 2048, 128, nkt, 128),
                                 vsrc, VBS[h % 3])

                @blk.tensor
                def _(te):
                    for op in pe_prog:
                        kind, h, x = op
                        if kind == "sc":
                            qt = x
                            rng = rngq(qt)
                            if qt == 0:
                                lp = f"L{h % 2}"
                                te.wait_ge(self.sems[lp], c0[lp] + 16 * (h // 2 + 1))
                                lp = f"L{2 + h % 2}"
                                te.wait_ge(self.sems[lp], c0[lp] + 16 * (h // 2 + 1))
                            prev = ("exp", h, qt - 1) if qt > 0 else ("exp", h - 1, 3)
                            if prev in ACn:
                                te.wait_ge(self.sems["AC"], ACn[prev])
                            for ck in range((rng + 511) // 512):
                                n = min(512, rng - ck * 512)
                                _l = te.matmul(
                                    self.PS_SC.ap()[:, ck * 512:ck * 512 + n],
                                    self.af(QB0 + (h % 2) * 512 + qt * 128, 128),
                                    self.af(KB0 + (h % 2) * 2048 + ck * 512, n),
                                    start=True, stop=True)
                            _l.then_inc(self.inc("PE", 1), 1)
                        elif kind == "T":
                            kt = x
                            if kt == 0:
                                te.wait_ge(self.sems["DV"], DVn[("nTS", h, 3)])
                            if kt >= 2:
                                te.wait_ge(self.sems["AC"], ACn[("evac", h, kt - 2)])
                            elif h >= 1:
                                te.wait_ge(self.sems["AC"],
                                           ACn[("evac", h - 1, nkt - 2 + kt)])
                            ps = (self.PS_A if kt % 2 == 0 else self.PS_B).ap()
                            qlo = qlo_of(kt)
                            for qt in range(qlo, 4):
                                _l = te.transpose(
                                    ps[:, qt * 128:(qt + 1) * 128],
                                    self.af(PB0 + (h % 2) * 8192 + qt * 2048
                                            + kt * 128, 128),
                                    self.C_id.ap())
                            _l.then_inc(self.inc("PE", 1), 1)
                        else:
                            kt = x
                            te.wait_ge(self.sems["AC"], ACn[("evac", h, kt)])
                            if kt == 0:
                                vp = VBS[h % 3]
                                te.wait_ge(self.sems[vp], c0[vp] + 16 * (h // 3 + 1))
                                if h >= 1:
                                    te.wait_ge(self.sems["DV"], DVn[("avt", h - 1)])
                            qlo = qlo_of(kt)
                            te.matmul(self.PS_C.ap()[:, qlo * 128:512],
                                      self.af(VB0 + (h % 3) * 2048 + kt * 128, 128),
                                      self.af(PTB0 + (kt % 2) * 512 + qlo * 128,
                                              512 - qlo * 128),
                                      start=(kt == 0), stop=(kt == nkt - 1)
                                      ).then_inc(self.inc("PE", 1), 1)

                @blk.vector
                def _(v):
                    first = True
                    for op in dv_prog:
                        kind, h = op[0], op[1]
                        if kind == "bTT":
                            qt = op[2]
                            rng = rngq(qt)
                            v.wait_ge(self.sems["PE"], PEn[("sc", h, qt)])
                            if first and s == 0:
                                v.wait_ge(self.sems["X0"], c0["X0"] + 16 * NQT)
                                first = False
                            i = 4 * s + qt
                            v.tensor_tensor(
                                self.PS_SC.ap()[:, rng - 256:rng],
                                self.PS_SC.ap()[:, rng - 256:rng],
                                self.af(MASKS + i * 256, 256),
                                ALU.add).then_inc(self.inc("DV", 1), 1)
                        elif kind == "bRed":
                            qt = op[2]
                            rng = rngq(qt)
                            v.wait_ge(self.sems["DV"], DVn[("bTT", h, qt)])
                            if ("exp", h - 1, qt) in ACn:
                                v.wait_ge(self.sems["AC"], ACn[("exp", h - 1, qt)])
                            v.tensor_reduce(
                                self.TMP.ap()[:, 4 + qt:5 + qt],
                                self.PS_SC.ap()[:, 0:rng],
                                AX.X, ALU.max,
                                negate=True).then_inc(self.inc("DV", 1), 1)
                        elif kind == "rcp":
                            qt = op[2]
                            v.wait_ge(self.sems["AC"], ACn[("exp", h, qt)])
                            v.reciprocal(self.TMP.ap()[:, 8 + qt:9 + qt],
                                         self.SE.ap()[:, qt:qt + 1]
                                         ).then_inc(self.inc("DV", 1), 1)
                        elif kind == "nTS":
                            qt = op[2]
                            rng = rngq(qt)
                            v.wait_ge(self.sems["DV"], DVn[("rcp", h, qt)])
                            v.tensor_scalar(
                                self.af(PB0 + (h % 2) * 8192 + qt * 2048, rng),
                                self.af(PB0 + (h % 2) * 8192 + qt * 2048, rng),
                                self.TMP.ap()[:, 8 + qt:9 + qt], None, ALU.mult,
                                ALU.bypass).then_inc(self.inc("DV", 1), 1)
                        else:
                            v.wait_ge(self.sems["PE"], PEn[("PV", h, nkt - 1)])
                            v.tensor_copy(self.af(AVT + h * 512, 512),
                                          self.PS_C.ap()).then_inc(self.inc("DV", 1), 1)

                @blk.scalar
                def _(sc):
                    for op in ac_prog:
                        kind, h, x = op
                        if kind == "exp":
                            qt = x
                            rng = rngq(qt)
                            sc.wait_ge(self.sems["DV"], DVn[("bRed", h, qt)])
                            sc.activation(self.af(PB0 + (h % 2) * 8192 + qt * 2048, rng),
                                          self.PS_SC.ap()[:, 0:rng], ACT.Exp,
                                          bias=self.TMP.ap()[:, 4 + qt:5 + qt],
                                          scale=1.0,
                                          accum_out=self.SE.ap()[:, qt:qt + 1]
                                          ).then_inc(self.inc("AC", 1), 1)
                        else:
                            kt = x
                            sc.wait_ge(self.sems["PE"], PEn[("T", h, kt)])
                            if kt >= 2:
                                sc.wait_ge(self.sems["PE"], PEn[("PV", h, kt - 2)])
                            elif h >= 1:
                                sc.wait_ge(self.sems["PE"],
                                           PEn[("PV", h - 1, nkt - 2 + kt)])
                            ps = (self.PS_A if kt % 2 == 0 else self.PS_B).ap()
                            qlo = qlo_of(kt)
                            sc.copy(self.af(PTB0 + (kt % 2) * 512 + qlo * 128,
                                            512 - qlo * 128),
                                    ps[:, qlo * 128:512]
                                    ).then_inc(self.inc("AC", 1), 1)

            self.fence_block(f"f_p3w{s}")
            ld0, dv0, ac0, pe0 = (self.cnt[k] for k in ["LD", "DV", "AC", "PE"])
            c0 = dict(self.cnt)
            with nc.Block(f"p3_wo{s}") as blk:
                @blk.sync
                def _(sy):
                    for oc in range(4):
                        if oc > 0:
                            sy.wait_ge(self.sems["PE"], pe0 + oc * 4)
                        for c in range(16):
                            self.dma(sy, self.af(WOB + c * 512, 512),
                                     self.wo.ap()[c * 128:(c + 1) * 128,
                                                  oc * 512:(oc + 1) * 512])
                        for qt in range(4):
                            idx = oc * 4 + qt
                            if idx >= 2:
                                sy.wait_ge(self.sems["DV"], dv0 + idx - 1)
                            self.dma(sy, self.af(X0B + (idx % 2) * 512, 512),
                                     self.xq.ap()[(4 * s + qt) * 128:(4 * s + qt + 1) * 128,
                                                  oc * 512:(oc + 1) * 512],
                                     f"X{idx % 2}")

                @blk.tensor
                def _(te):
                    for oc in range(4):
                        te.wait_ge(self.sems["LD"], ld0 + 16 * 16 * (oc + 1))
                        for qt in range(4):
                            idx = oc * 4 + qt
                            if idx >= 2:
                                te.wait_ge(self.sems["DV"], dv0 + idx - 1)
                            ps = (self.PS_C if idx % 2 == 0 else self.PS_D).ap()
                            for c in range(16):
                                _pe_last = te.matmul(
                                    ps, self.af(AVT + c * 512 + qt * 128, 128),
                                    self.af(WOB + c * 512, 512),
                                    start=(c == 0), stop=(c == 15))
                            _pe_last.then_inc(self.inc("PE", 1), 1)

                @blk.vector
                def _(v):
                    mkbn = {}

                    def bnchunk(idx):
                        oc, qt = idx // 4, idx % 4
                        v.wait_ge(self.sems["DV"], dv0 + idx + 1)
                        mkbn[qt] = self.vmark(v.bn_stats(
                            self.BN6.ap()[:, qt * 24 + oc * 6:qt * 24 + oc * 6 + 6],
                            self.af(X1B + qt * 2048 + oc * 512, 512)))

                    for oc in range(4):
                        for qt in range(4):
                            idx = oc * 4 + qt
                            v.wait_ge(self.sems["PE"], pe0 + idx + 1)
                            v.wait_ge(self.sems[f"X{idx % 2}"],
                                      c0[f"X{idx % 2}"] + 16 * (idx // 2 + 1))
                            ps = (self.PS_C if idx % 2 == 0 else self.PS_D).ap()
                            dst = self.af(X1B + qt * 2048 + oc * 512, 512)
                            self.vbar(v, v.scalar_tensor_tensor(
                                dst, ps, 0.0,
                                self.af(X0B + (idx % 2) * 512, 512),
                                ALU.add, ALU.add))
                            v.tensor_tensor(dst, dst, self.af(OF_BO + oc * 512, 512),
                                            ALU.add).then_inc(self.inc("DV", 1), 1)
                            if idx >= 1:
                                bnchunk(idx - 1)
                    bnchunk(15)
                    for qt in range(4):
                        i = 4 * s + qt
                        v.wait_ge(self.sems["VH"], mkbn[qt])
                        self.vbar(v, v.bn_aggr(self.BN2.ap()[:, 0:2],
                                               self.BN6.ap()[:, qt * 24:qt * 24 + 24]))
                        v.tensor_copy(self.ST1_mu.ap()[:, i:i + 1], self.BN2.ap()[:, 0:1])
                        v.tensor_scalar(self.TMP.ap()[:, 4:5], self.BN2.ap()[:, 1:2],
                                        EPS, None, ALU.add, ALU.bypass)
                        v.tensor_scalar(self.TMP.ap()[:, 5:6], self.BN2.ap()[:, 1:2],
                                        1.0 + EPS, EPS * EPS, ALU.mult,
                                        ALU.add).then_inc(self.inc("DV", 1), 1)
                        v.wait_ge(self.sems["AC"], ac0 + qt + 1)
                        v.reciprocal(self.ST1_r1.ap()[:, i:i + 1], self.TMP.ap()[:, 6:7])
                        self.vbar(v, v.reciprocal(self.ST1_al.ap()[:, i:i + 1],
                                                  self.TMP.ap()[:, 7:8]))
                        v.tensor_scalar(self.ST1_a.ap()[:, i:i + 1],
                                        self.ST1_r1.ap()[:, i:i + 1], 1.0, None,
                                        ALU.add, ALU.bypass)
                        v.scalar_tensor_tensor(self.ST1_b.ap()[:, i:i + 1],
                                               self.ST1_mu.ap()[:, i:i + 1], -1.0,
                                               self.ST1_r1.ap()[:, i:i + 1],
                                               ALU.mult, ALU.mult).then_inc(self.inc("DV", 1), 1)

                @blk.scalar
                def _(sc):
                    for qt in range(4):
                        sc.wait_ge(self.sems["DV"], dv0 + 16 + 2 * qt + 1)
                        sc.activation(self.TMP.ap()[:, 6:7], self.TMP.ap()[:, 4:5], ACT.Sqrt)
                        sc.activation(self.TMP.ap()[:, 7:8], self.TMP.ap()[:, 5:6],
                                      ACT.Sqrt).then_inc(self.inc("AC", 1), 1)

                @blk.gpsimd
                def _(gp):
                    for qt in range(4):
                        i = 4 * s + qt
                        gp.wait_ge(self.sems["DV"], dv0 + 16 + 2 * qt + 2)
                        self.dma(gp, self.X1scr.ap()[i * 128:(i + 1) * 128, :],
                                 self.af(X1B + qt * 2048, 2048), "GP")

    def p4_ffn_a(self):
        nc = self.nc
        AQ1T, XB4, GB, Z0, Z1, Z2, AQN = 0, 8192, 10240, 12288, 14336, 16384, 18432
        gp_done = self.cnt["GP"]
        self.fence_block("f_p4")
        ld0, dv0, ae0, pe0 = (self.cnt[k] for k in ["LD", "DV", "AE", "PE"])
        with nc.Block("p4") as blk:
            @blk.sync
            def _(sy):
                sy.wait_ge(self.sems["GP"], gp_done)
                for i in range(NQT):
                    if i > 0:
                        sy.wait_ge(self.sems["DV"], dv0 + i)
                    self.dma(sy, self.af(XB4, 2048),
                             self.X1scr.ap()[i * 128:(i + 1) * 128, :])

            @blk.vector
            def _(v):
                for i in range(NQT):
                    v.wait_ge(self.sems["LD"], ld0 + 16 * (i + 1))
                    if i > 0:
                        v.wait_ge(self.sems["PE"], pe0 + 4 * i)
                    self.vbar(v, v.scalar_tensor_tensor(
                        self.af(GB, 2048), self.af(XB4, 2048),
                        self.ST1_mu.ap()[:, i:i + 1],
                        self.af(OF_NW, D), ALU.subtract, ALU.mult))
                    self.vbar(v, v.tensor_reduce(self.TMP.ap()[:, 4:5],
                                                 self.af(GB, 2048), AX.X,
                                                 ALU.max, apply_absolute_value=True))
                    self.vbar(v, v.tensor_scalar(self.TMP.ap()[:, 4:5],
                                                 self.TMP.ap()[:, 4:5],
                                                 1e-5, None, ALU.max, ALU.bypass))
                    self.vbar(v, v.reciprocal(self.TMP.ap()[:, 5:6],
                                               self.TMP.ap()[:, 4:5]))
                    v.tensor_scalar(self.TMP.ap()[:, 5:6], self.TMP.ap()[:, 5:6],
                                    127.0, None, ALU.mult, ALU.bypass)
                    self.vbar(v, v.tensor_tensor(self.TMP.ap()[:, 6:7],
                                                 self.ST1_al.ap()[:, i:i + 1],
                                                 self.TMP.ap()[:, 4:5], ALU.mult))
                    v.tensor_tensor(self.TMP.ap()[:, 7:8], self.TMP.ap()[:, 6:7],
                                    self.V_sgv.ap()[:, 0:1], ALU.mult)
                    self.round_pos(v, self.af(GB, 2048), self.TMP.ap()[:, 5:6],
                                   Z0, Z1, Z2, 2 * AQN, 2048)
                    v.tensor_scalar(self.C1c.ap()[:, i:i + 1], self.TMP.ap()[:, 7:8],
                                    1.0 / 127.0, None, ALU.mult,
                                    ALU.bypass).then_inc(self.inc("DV", 1), 1)

            @blk.tensor
            def _(te):
                for i in range(NQT):
                    te.wait_ge(self.sems["DV"], dv0 + i + 1)
                    for cg in range(4):
                        idx = 4 * i + cg
                        if idx >= 2:
                            te.wait_ge(self.sems["AE"], ae0 + idx - 1)
                        psb = (self.PS_C if idx % 2 == 0 else self.PS_D).ap().bitcast(BF16)
                        for c4 in range(4):
                            c = 4 * cg + c4
                            _pe_last = te.transpose(psb[:, c4 * 128:(c4 + 1) * 128],
                                         self.bap(2 * AQN + c * 128, [[1, 128]]),
                                         self.C_idb.ap())
                        _pe_last.then_inc(self.inc("PE", 1), 1)

            @blk.scalar
            def _(sc):
                for i in range(NQT):
                    for cg in range(4):
                        idx = 4 * i + cg
                        sc.wait_ge(self.sems["PE"], pe0 + idx + 1)
                        psb = (self.PS_C if idx % 2 == 0 else self.PS_D).ap().bitcast(BF16)
                        dst = self.bap(2 * AQ1T + (4 * cg) * 1024 + i * 128,
                                       [[1024, 4], [1, 128]])
                        sc.copy(dst, psb[:, 0:512].rearrange("p (a c) -> p a c", a=4)
                                ).then_inc(self.inc("AE", 1), 1)

    def p5_gv(self):
        nc = self.nc
        AQ1T = 0
        TT0 = 8192
        WGB = 24576
        M1B = 40960
        NTB = 43008
        SGB = 45056
        HMB = 46080
        self.fence_block("f_p5")
        GSEM = ["L0", "L1", "L2", "L3", "X0", "X1", "S0", "S1"]
        c00 = dict(self.cnt)
        for b in range(17):
            tern = b <= 15
            mm = b >= 1
            tp = b % 2
            mp = (b - 1) % 2
            c0 = dict(self.cnt)
            pe_per = (1 if mm else 0) + (4 if tern else 0)
            ac_per = (1 if mm else 0) + (4 if tern else 0)
            dv_per = (1 if mm else 0) + (1 if tern else 0)

            def dv_after_tern(g):
                return c0["DV"] + dv_per * g + 1

            def dv_after_hm(k):
                return c0["DV"] + dv_per * k + (2 if tern else 1)

            def pe_after_mm(k):
                return c0["PE"] + pe_per * k + 1

            def pe_after_tg(k, cg):
                return c0["PE"] + pe_per * k + (1 if mm else 0) + cg + 1

            def ac_after_silu(k):
                return c0["AC"] + ac_per * k + 1

            def ac_after_evac(k, cg):
                return c0["AC"] + ac_per * k + (1 if mm else 0) + cg + 1

            with nc.Block(f"p5_{b}") as blk:
                @blk.sync
                def _(sy):
                    if b == 0:
                        for g in range(8):
                            part, ot = g // 4, g % 4
                            self.dma(sy, self.af(WGB + g * 2048, 2048),
                                     self.wgv.ap()[part * HID + ot * 128:
                                                   part * HID + (ot + 1) * 128, :],
                                     GSEM[g])
                    if b <= 14:
                        for g in range(8):
                            part, ot = g // 4, g % 4
                            rowbase = part * HID + (b + 1) * 512
                            sy.wait_ge(self.sems["DV"], dv_after_tern(g))
                            self.dma(sy, self.af(WGB + g * 2048, 2048),
                                     self.wgv.ap()[rowbase + ot * 128:
                                                   rowbase + (ot + 1) * 128, :],
                                     GSEM[g])

                @blk.vector
                def _(v):
                    for k in range(8):
                        if tern:
                            g = k
                            lp = GSEM[g]
                            v.wait_ge(self.sems[lp], c00[lp] + 16 * (b + 1))
                            if g >= 2:
                                v.wait_ge(self.sems["PE"], pe_after_tg(g - 2, 3))
                            w = self.af(WGB + g * 2048, 2048)
                            m1 = self.bap(2 * (M1B + (g % 2) * 1024), [[1, 2048]])
                            self.vbar(v, v.tensor_scalar(
                                m1, w,
                                self.V_sgv.ap()[:, 2:3], None,
                                ALU.is_lt, ALU.bypass))
                            v.scalar_tensor_tensor(
                                self.bap(2 * (NTB + (g % 2) * 1024), [[1, 2048]]),
                                w, self.V_sgv.ap()[:, 1:2], m1,
                                ALU.is_gt, ALU.subtract
                                ).then_inc(self.inc("DV", 1), 1)
                        if mm:
                            qt = k
                            v.wait_ge(self.sems["AC"], ac_after_silu(qt))
                            gpar = f"G{qt % 2}"
                            v.wait_ge(self.sems[gpar], c0[gpar] + 16 * (qt // 2))
                            hm = self.af(HMB + (qt % 2) * 512, 512)
                            psv = self.PS_SC.ap()[:, (qt % 2) * 1024:(qt % 2) * 1024 + 512]
                            self.vbar(v, v.scalar_tensor_tensor(
                                hm, psv,
                                self.C1c.ap()[:, qt:qt + 1],
                                self.af(SGB + (qt % 2) * 512, 512), ALU.mult, ALU.mult))
                            self.vbar(v, v.tensor_reduce(
                                self.TMP.ap()[:, 4:5], hm, AX.X, ALU.max,
                                apply_absolute_value=True))
                            v.tensor_tensor(self.RMX.ap()[:, qt:qt + 1],
                                            self.RMX.ap()[:, qt:qt + 1],
                                            self.TMP.ap()[:, 4:5],
                                            ALU.max).then_inc(self.inc("DV", 1), 1)

                @blk.tensor
                def _(te):
                    for k in range(8):
                        if mm:
                            qt = k
                            if qt >= 2:
                                te.wait_ge(self.sems["DV"], dv_after_hm(qt - 2))
                            psv = self.PS_SC.ap()[:, (qt % 2) * 1024:(qt % 2) * 1024 + 512]
                            psg = self.PS_SC.ap()[:, (qt % 2) * 1024 + 512:
                                                  (qt % 2) * 1024 + 1024]
                            for c in range(16):
                                te.matmul(psv,
                                          self.bap(2 * AQ1T + c * 1024 + qt * 128,
                                                   [[1, 128]]),
                                          self.bap(2 * (TT0 + mp * 8192) + c * 512,
                                                   [[1, 512]]),
                                          start=(c == 0), stop=(c == 15))
                                _l = te.matmul(psg,
                                               self.bap(2 * AQ1T + c * 1024 + qt * 128,
                                                        [[1, 128]]),
                                               self.bap(2 * (TT0 + mp * 8192 + 4096)
                                                        + c * 512, [[1, 512]]),
                                               start=(c == 0), stop=(c == 15))
                            _l.then_inc(self.inc("PE", 1), 1)
                        if tern:
                            g = k
                            te.wait_ge(self.sems["DV"], dv_after_tern(g))
                            part, ot = g // 4, g % 4
                            base = TT0 + tp * 8192 + (0 if part == 0 else 4096)
                            for cg in range(4):
                                tid = 8 * b + g
                                if b * 32 + g * 4 + cg >= 2:
                                    kp, cgp = divmod(g * 4 + cg - 2, 4)
                                    if g * 4 + cg >= 2:
                                        te.wait_ge(self.sems["AC"],
                                                   ac_after_evac(kp, cgp))
                                psb = (self.PS_C if (g * 4 + cg) % 2 == 0
                                       else self.PS_D).ap().bitcast(BF16)
                                for c4 in range(4):
                                    c = 4 * cg + c4
                                    _t = te.transpose(
                                        psb[:, c4 * 128:(c4 + 1) * 128],
                                        self.bap(2 * (NTB + (g % 2) * 1024) + c * 128,
                                                 [[1, 128]]),
                                        self.C_idb.ap())
                                _t.then_inc(self.inc("PE", 1), 1)

                @blk.scalar
                def _(sc):
                    for k in range(8):
                        if mm:
                            qt = k
                            sc.wait_ge(self.sems["PE"], pe_after_mm(qt))
                            psg = self.PS_SC.ap()[:, (qt % 2) * 1024 + 512:
                                                  (qt % 2) * 1024 + 1024]
                            sc.activation(self.af(SGB + (qt % 2) * 512, 512), psg,
                                          ACT.Silu, bias=0.0,
                                          scale=self.C1c.ap()[:, qt:qt + 1]
                                          ).then_inc(self.inc("AC", 1), 1)
                        if tern:
                            g = k
                            part, ot = g // 4, g % 4
                            base = TT0 + tp * 8192 + (0 if part == 0 else 4096)
                            for cg in range(4):
                                sc.wait_ge(self.sems["PE"], pe_after_tg(g, cg))
                                psb = (self.PS_C if (g * 4 + cg) % 2 == 0
                                       else self.PS_D).ap().bitcast(BF16)
                                dst = self.bap(2 * base + (4 * cg) * 512 + ot * 128,
                                               [[512, 4], [1, 128]])
                                sc.copy(dst, psb[:, 0:512]
                                        .rearrange("p (a c) -> p a c", a=4)
                                        ).then_inc(self.inc("AC", 1), 1)

                if mm:
                    @blk.gpsimd
                    def _(gp):
                        for qt in range(8):
                            gp.wait_ge(self.sems["DV"], dv_after_hm(qt))
                            self.dma(gp, self.HMID.ap()[qt * 128:(qt + 1) * 128,
                                                        (b - 1) * 512:b * 512],
                                     self.af(HMB + (qt % 2) * 512, 512),
                                     f"G{qt % 2}")

    def p6_tout(self):
        nc = self.nc
        WOB, M1C, NT5, TSB = 0, 12288, 14336, 16384
        LQ = ["L0", "L1", "L2", "L3", "X0", "X1"]
        SQ = ["S0", "S1", "LD"]
        self.fence_block("f_p6")
        c0 = dict(self.cnt)
        DVn = lambda ci: c0["DV"] + ci + 1
        PEn = lambda g: c0["PE"] + g + 1
        ACn = lambda g: c0["AC"] + g + 1
        store_cnt = {}
        qn = {q: 0 for q in SQ}
        for bt in range(128):
            q = SQ[bt % 3]
            qn[q] += 1
            store_cnt[bt] = (q, qn[q])
        with nc.Block("p6") as blk:
            @blk.sync
            def _(sy):
                for ci in range(64):
                    if ci >= 6:
                        sy.wait_ge(self.sems["DV"], DVn(ci - 6))
                    rt, ch = divmod(ci, 4)
                    self.dma(sy, self.af(WOB + (ci % 6) * 2048, 2048),
                             self.wout.ap()[rt * 128:(rt + 1) * 128,
                                            ch * 2048:(ch + 1) * 2048],
                             LQ[ci % 6])

            @blk.vector
            def _(v):
                mk = {}
                for s in range(65):
                    if s < 64:
                        q = LQ[s % 6]
                        v.wait_ge(self.sems[q], c0[q] + 16 * (s // 6 + 1))
                        if s >= 2:
                            v.wait_ge(self.sems["DV"], DVn(s - 2))
                        mk[s] = self.vmark(v.tensor_scalar(
                            self.bap(2 * (M1C + (s % 2) * 1024), [[1, 2048]]),
                            self.af(WOB + (s % 6) * 2048, 2048),
                            self.V_sout.ap()[:, 2:3], None,
                            ALU.is_lt, ALU.bypass))
                    if s >= 1:
                        ci = s - 1
                        if ci >= 2:
                            v.wait_ge(self.sems["PE"], PEn(4 * (ci - 2) + 3))
                        v.wait_ge(self.sems["VH"], mk[ci])
                        v.scalar_tensor_tensor(
                            self.bap(2 * (NT5 + (ci % 2) * 1024), [[1, 2048]]),
                            self.af(WOB + (ci % 6) * 2048, 2048),
                            self.V_sout.ap()[:, 1:2],
                            self.bap(2 * (M1C + (ci % 2) * 1024), [[1, 2048]]),
                            ALU.is_gt, ALU.subtract).then_inc(self.inc("DV", 1), 1)

            @blk.tensor
            def _(te):
                for ci in range(64):
                    te.wait_ge(self.sems["DV"], DVn(ci))
                    for u in range(4):
                        g = 4 * ci + u
                        if g >= 2:
                            te.wait_ge(self.sems["AC"], ACn(g - 2))
                        psb = (self.PS_C if g % 2 == 0
                               else self.PS_D).ap().bitcast(BF16)
                        for s in range(4):
                            tl = 4 * u + s
                            _l = te.transpose(
                                psb[:, s * 128:(s + 1) * 128],
                                self.bap(2 * (NT5 + (ci % 2) * 1024) + tl * 128,
                                         [[1, 128]]),
                                self.C_idb.ap())
                        _l.then_inc(self.inc("PE", 1), 1)

            @blk.scalar
            def _(sc):
                for ci in range(64):
                    rt, ch = divmod(ci, 4)
                    for u in range(4):
                        g = 4 * ci + u
                        bt = g // 2
                        sc.wait_ge(self.sems["PE"], PEn(g))
                        if u % 2 == 0 and bt >= 3:
                            q, n = store_cnt[bt - 3]
                            sc.wait_ge(self.sems[q], c0[q] + 16 * n)
                        psb = (self.PS_C if g % 2 == 0
                               else self.PS_D).ap().bitcast(BF16)
                        tsb = self.bap(2 * (TSB + (bt % 3) * 512 + (u % 2) * 256),
                                       [[1, 512]])
                        sc.copy(tsb, psb[:, 0:512]).then_inc(self.inc("AC", 1), 1)
                        if u % 2 == 1:
                            sc.wait_ge(self.sems["AC"], self.cnt["AC"])
                            oc0 = ch * 16 + 4 * (u - 1)
                            dst = bass.AP(self.TOUT, rt * 1048576 + oc0 * 16384,
                                          [[128, 128], [16384, 8], [1, 128]])
                            self.dma(sc, dst,
                                     self.bap(2 * (TSB + (bt % 3) * 512),
                                              [[128, 8], [1, 128]]),
                                     SQ[bt % 3])

    def p7_out2(self):
        nc = self.nc
        AQ2T = 0
        HM2 = 32768
        ZA, ZBC, ZCC = 34816, 35328, 35840
        AQ2N = 36352
        AQN = (36352, 43776)
        ZP7 = 40704
        TOB = 36608
        X1B7 = 37632
        XH = 38656
        YB = 39680
        gp_done = self.cnt["GP"]
        st_done = self.cnt["ST"]
        TSEM = ["H0", "H1", "S0", "S1"]

        for b in range(3):
            bld = b <= 1
            mm = b >= 1
            self.fence_block(f"f_p7_{b}")
            c0 = dict(self.cnt)
            pe_prog, dv_prog, ac_prog = [], [], []
            if mm:
                k_iter = iter(range(64))
                for j in range(4):
                    for oc in range(64):
                        if bld and oc % 4 == 0:
                            try:
                                k = next(k_iter)
                                pe_prog.append(("T", k))
                            except StopIteration:
                                pass
                        pe_prog.append(("MM", j, oc))
            elif bld:
                for k in range(64):
                    pe_prog.append(("T", k))
            if bld and mm:
                for j in range(4):
                    for k in range(j * 16, (j + 1) * 16):
                        dv_prog.append(("rnd", k))
                    for qt in range(4):
                        dv_prog.append(("yb", j, qt))
            elif bld:
                for k in range(64):
                    dv_prog.append(("rnd", k))
            else:
                for j in range(4):
                    for qt in range(4):
                        dv_prog.append(("yb", j, qt))
            if bld:
                for k in range(64):
                    ac_prog.append(("evac", k))
            PEn = {op: c0["PE"] + i + 1 for i, op in enumerate(pe_prog)}
            DVn = {op: c0["DV"] + i + 1 for i, op in enumerate(dv_prog)}
            ACn = {op: c0["AC"] + i + 1 for i, op in enumerate(ac_prog)}

            with nc.Block(f"p7_{b}") as blk:
                @blk.sync
                def _(sy):
                    if b == 0:
                        sy.wait_ge(self.sems["GP"], gp_done)
                        sy.wait_ge(self.sems["ST"], st_done)

                    def hm2_load(k):
                        qt, oc16 = k // 16, k % 16
                        i = b * 4 + qt
                        if k >= 4:
                            sy.wait_ge(self.sems["DV"], DVn[("rnd", k - 4)])
                        self.dma(sy, self.af(HM2 + (k % 4) * 512, 512),
                                 self.HMID.ap()[i * 128:(i + 1) * 128,
                                                oc16 * 512:(oc16 + 1) * 512],
                                 f"L{k % 4}")

                    def tob_load(j, oc):
                        idx = j * 64 + oc
                        if idx >= 4:
                            sy.wait_ge(self.sems["PE"], PEn[("MM", (idx - 4) // 64,
                                                            (idx - 4) % 64)])
                        vsrc = bass.AP(self.TOUT, (j * 4) * 1048576 + oc * 16384,
                                       [[128, 128], [1048576, 4], [1, 128]])
                        self.dma(sy, self.bap(2 * (TOB + (idx % 4) * 256),
                                              [[128, 4], [1, 128]]),
                                 vsrc, TSEM[idx % 4])

                    def x1_load(j, qt):
                        i = (b - 1) * 4 + qt
                        idx2 = j * 4 + qt
                        if idx2 >= 2:
                            sy.wait_ge(self.sems["DV"], DVn[("yb", (idx2 - 2) // 4,
                                                            (idx2 - 2) % 4)])
                        self.dma(sy, self.af(X1B7 + (idx2 % 2) * 512, 512),
                                 self.X1scr.ap()[i * 128:(i + 1) * 128,
                                                 j * 512:(j + 1) * 512],
                                 f"X{idx2 % 2}")

                    if bld and mm:
                        for j in range(4):
                            for kk in range(16):
                                hm2_load(j * 16 + kk)
                                for o4 in range(4):
                                    tob_load(j, 4 * kk + o4)
                            for qt in range(4):
                                x1_load(j, qt)
                    elif bld:
                        for k in range(64):
                            hm2_load(k)
                    else:
                        for j in range(4):
                            for oc in range(64):
                                tob_load(j, oc)
                            for qt in range(4):
                                x1_load(j, qt)

                @blk.vector
                def _(v):
                    rp_feed, rp_flush = self.rp_pipe(v, ZP7, 512)
                    for op in dv_prog:
                        if op[0] == "rnd":
                            k = op[1]
                            qt, oc16 = k // 16, k % 16
                            i = b * 4 + qt
                            if oc16 == 0:
                                self.vbar(v, v.tensor_scalar(
                                    self.TMP.ap()[:, 4:5], self.RMX.ap()[:, i:i + 1],
                                    1e-5, None, ALU.max, ALU.bypass))
                                v.reciprocal(self.TMP.ap()[:, 6:7], self.TMP.ap()[:, 4:5])
                                self.vbar(v, v.tensor_tensor(
                                    self.TMP.ap()[:, 5:6], self.V_sout.ap()[:, 0:1],
                                    self.TMP.ap()[:, 4:5], ALU.mult))
                                v.tensor_scalar(self.SC2.ap()[:, i:i + 1],
                                                self.TMP.ap()[:, 6:7],
                                                127.0, None, ALU.mult, ALU.bypass)
                                self.vbar(v, v.tensor_scalar(
                                    self.SO2.ap()[:, i:i + 1], self.TMP.ap()[:, 5:6],
                                    1.0 / 127.0, None, ALU.mult, ALU.bypass))
                            def pre(k=k):
                                lp = f"L{k % 4}"
                                v.wait_ge(self.sems[lp], c0[lp] + 16 * (k // 4 + 1))
                                if k >= 2:
                                    v.wait_ge(self.sems["PE"], PEn[("T", k - 2)])
                            hm = self.af(HM2 + (k % 4) * 512, 512)
                            rp_feed(pre, hm, self.SC2.ap()[:, i:i + 1],
                                    2 * AQN[k % 2])
                        else:
                            rp_flush()
                            _, j, qt = op
                            i = (b - 1) * 4 + qt
                            idx2 = j * 4 + qt
                            v.wait_ge(self.sems["PE"], PEn[("MM", j, 63)])
                            v.wait_ge(self.sems[f"X{idx2 % 2}"],
                                      c0[f"X{idx2 % 2}"] + 16 * (idx2 // 2 + 1))
                            ypar = f"Y{idx2 % 2}"
                            v.wait_ge(self.sems[ypar], c0[ypar] + 16 * (idx2 // 2))
                            xh = self.af(XH + (idx2 % 2) * 512, 512)
                            self.vbar(v, v.tensor_scalar(
                                xh, self.af(X1B7 + (idx2 % 2) * 512, 512),
                                self.ST1_a.ap()[:, i:i + 1],
                                self.ST1_b.ap()[:, i:i + 1], ALU.mult, ALU.add))
                            v.scalar_tensor_tensor(
                                self.af(YB + (idx2 % 2) * 512, 512),
                                self.PS_SC.ap()[:, qt * 512:(qt + 1) * 512],
                                self.SO2.ap()[:, i:i + 1], xh, ALU.mult,
                                ALU.add).then_inc(self.inc("DV", 1), 1)
                    rp_flush()

                @blk.tensor
                def _(te):
                    for op in pe_prog:
                        if op[0] == "T":
                            k = op[1]
                            qt = k // 16
                            qtg = b * 4 + qt
                            te.wait_ge(self.sems["DV"], DVn[("rnd", k)])
                            if k >= 2:
                                te.wait_ge(self.sems["AC"], ACn[("evac", k - 2)])
                            psb = (self.PS_C if k % 2 == 0
                                   else self.PS_D).ap().bitcast(BF16)
                            for sub in range(4):
                                _l = te.transpose(psb[:, sub * 128:(sub + 1) * 128],
                                                  self.bap(2 * AQN[k % 2] + sub * 128,
                                                           [[1, 128]]),
                                                  self.C_idb.ap())
                            _l.then_inc(self.inc("PE", 1), 1)
                        else:
                            _, j, oc = op
                            idx = j * 64 + oc
                            qtg0 = (b - 1) * 4
                            te.wait_ge(self.sems[TSEM[idx % 4]],
                                       c0[TSEM[idx % 4]] + 16 * (idx // 4 + 1))
                            if oc == 0 and j > 0:
                                te.wait_ge(self.sems["DV"], DVn[("yb", j - 1, 3)])
                            for qt in range(4):
                                _l = te.matmul(
                                    self.PS_SC.ap()[:, qt * 512:(qt + 1) * 512],
                                    self.bap(2 * AQ2T + oc * 1024 + (qtg0 + qt) * 128,
                                             [[1, 128]]),
                                    self.bap(2 * (TOB + (idx % 4) * 256), [[1, 512]]),
                                    start=(oc == 0), stop=(oc == 63))
                            _l.then_inc(self.inc("PE", 1), 1)

                @blk.scalar
                def _(sc):
                    for op in ac_prog:
                        k = op[1]
                        qt, oc16 = k // 16, k % 16
                        qtg = b * 4 + qt
                        sc.wait_ge(self.sems["PE"], PEn[("T", k)])
                        psb = (self.PS_C if k % 2 == 0
                               else self.PS_D).ap().bitcast(BF16)
                        dst = self.bap(2 * AQ2T + (oc16 * 4) * 1024 + qtg * 128,
                                       [[1024, 4], [1, 128]])
                        sc.copy(dst, psb[:, 0:512]
                                .rearrange("p (a c) -> p a c", a=4)
                                ).then_inc(self.inc("AC", 1), 1)

                if mm:
                    @blk.gpsimd
                    def _(gp):
                        for j in range(4):
                            for qt in range(4):
                                i = (b - 1) * 4 + qt
                                idx2 = j * 4 + qt
                                gp.wait_ge(self.sems["DV"], DVn[("yb", j, qt)])
                                self.dma(gp, self.y.ap()[i * 128:(i + 1) * 128,
                                                         j * 512:(j + 1) * 512],
                                         self.af(YB + (idx2 % 2) * 512, 512),
                                         f"Y{idx2 % 2}")

    def p_final(self):
        nc = self.nc
        with nc.Block("fin") as blk:
            @blk.gpsimd
            def _(gp):
                for s in self.DMA_SEMS:
                    if self.cnt[s]:
                        gp.wait_ge(self.sems[s], self.cnt[s])


def _host_inputs(core, X, ln_g, ln_b, Wq, bq, Wk, bk, Wv, bv, Wo, bo,
                 norm_w, w_gv, w_out):
    b, g = core // 2, core % 2
    f32 = np.float32
    rep = lambda v: np.ascontiguousarray(np.broadcast_to(np.asarray(v, f32), (128, D)))
    xb = np.ascontiguousarray(np.asarray(X, f32)[b])
    xqv = np.ascontiguousarray(xb.reshape(16, 128, D)[g::2].reshape(T, D))
    mask = np.zeros((NQT, 128, 256), f32)
    for i in range(NQT):
        qg = 256 * i + 128 * g + np.arange(128)[:, None]
        kg = 256 * i + np.arange(256)[None, :]
        mask[i] = np.where(kg <= qg, 0.0, NEGBIG)
    return {
        "x": xb, "xq": xqv,
        "wq": np.ascontiguousarray(np.asarray(Wq, f32)),
        "wk": np.ascontiguousarray(np.asarray(Wk, f32)),
        "wv": np.ascontiguousarray(np.asarray(Wv, f32)),
        "wo": np.ascontiguousarray(np.asarray(Wo, f32)),
        "bq_p": np.ascontiguousarray(np.asarray(bq, f32).reshape(H, 128).T),
        "bk_p": np.ascontiguousarray(np.asarray(bk, f32).reshape(H, 128).T),
        "bv_rep": rep(bv), "bo_rep": rep(bo), "g_rep": rep(ln_g),
        "b_rep": rep(ln_b), "nw_rep": rep(norm_w),
        "wgv": np.ascontiguousarray(np.asarray(w_gv, f32)),
        "wout": np.ascontiguousarray(np.asarray(w_out, f32)),
        "maskq": mask,
        "ident": np.eye(128, dtype=f32),
        "identb": np.eye(128, dtype=ml_dtypes.bfloat16),
    }


def kernel(L0_attn_X, L0_attn_ln_g, L0_attn_ln_b, L0_attn_Wq, L0_attn_bq,
           L0_attn_Wk, L0_attn_bk, L0_attn_Wv, L0_attn_bv, L0_attn_Wo,
           L0_attn_bo, L0_ffn_norm_w, L0_ffn_w_gv, L0_ffn_w_out):
    prog = Prog()
    nc = prog.build()
    in_maps = [
        _host_inputs(c, L0_attn_X, L0_attn_ln_g, L0_attn_ln_b, L0_attn_Wq,
                     L0_attn_bq, L0_attn_Wk, L0_attn_bk, L0_attn_Wv, L0_attn_bv,
                     L0_attn_Wo, L0_attn_bo, L0_ffn_norm_w, L0_ffn_w_gv,
                     L0_ffn_w_out)
        for c in range(8)
    ]
    res = run_bass_kernel_spmd(nc, in_maps, core_ids=list(range(8)))
    out = np.empty((4, S, D), np.float32)
    for c in range(8):
        b, g = c // 2, c % 2
        out[b].reshape(16, 128, D)[g::2] = res.results[c]["y"].reshape(NQT, 128, D)
    return out


# version 24
# speedup vs baseline: 1.0086x; 1.0063x over previous
import math
import numpy as np
import ml_dtypes
import concourse.bass as bass
import concourse.mybir as mybir
from concourse.bass_utils import run_bass_kernel_spmd

F32 = mybir.dt.float32
BF16 = mybir.dt.bfloat16
I32 = mybir.dt.int32
ACT = mybir.ActivationFunctionType
ALU = mybir.AluOpType
AX = mybir.AxisListType

S = 2048
D = 2048
H = 16
DH = 128
T = 1024
NQT = 8
HID = 8192
O2 = 16384
EPS = 1e-5
ISQ = 1.0 / math.sqrt(128.0)
NEGBIG = -1.0e38
STAGE_ALL = 99

OF_G, OF_B, OF_BV, OF_BO, OF_NW = 38912, 40960, 43008, 45056, 47104


class Prog:
    def __init__(self, stage=STAGE_ALL, dbg=False):
        self.stage = stage
        self.dbg = dbg
        nc = bass.Bass("TRN2", target_bir_lowering=False)
        self.nc = nc
        dram = lambda n, sh, dt=F32, kind="ExternalInput": nc.dram_tensor(n, sh, dt, kind=kind)
        skind = "ExternalOutput" if dbg else "Internal"

        self.x = dram("x", [S, D])
        self.xq = dram("xq", [T, D])
        self.wq = dram("wq", [D, D])
        self.wk = dram("wk", [D, D])
        self.wv = dram("wv", [D, D])
        self.wo = dram("wo", [D, D])
        self.bq_p = dram("bq_p", [128, H])
        self.bk_p = dram("bk_p", [128, H])
        self.bv_rep = dram("bv_rep", [128, D])
        self.bo_rep = dram("bo_rep", [128, D])
        self.g_rep = dram("g_rep", [128, D])
        self.b_rep = dram("b_rep", [128, D])
        self.nw_rep = dram("nw_rep", [128, D])
        self.wgv = dram("wgv", [O2, D])
        self.wout = dram("wout", [D, HID])
        self.maskq = dram("maskq", [NQT, 128, 256])
        self.ident = dram("ident", [128, 128])
        self.identb = dram("identb", [128, 128], BF16)
        self.y = dram("y", [T, D], kind="ExternalOutput")

        self.Kscr = dram("Kscr", [H, DH, S], kind=skind)
        self.Vscr = dram("Vscr", [H, S, DH], kind=skind)
        self.Qscr = dram("Qscr", [H, DH, T], kind=skind)
        self.X1scr = dram("X1scr", [T, D], kind=skind)
        self.HMID = dram("HMID", [T, HID], kind=skind)
        self.TOUT = dram("TOUT", [16, 64, 128, 128], BF16, kind="Internal")
        if dbg:
            self.dbg_s = dram("dbg_s", [128, 16], kind="ExternalOutput")

        self.AF = nc.alloc_sbuf_tensor("AF", [128, 49152], F32)
        fx = lambda n, cols, dt=F32: nc.alloc_sbuf_tensor(n, [128, cols], dt)
        self.C_bq = fx("C_bq", H)
        self.C_bqs = fx("C_bqs", H)
        self.C_bk = fx("C_bk", H)
        self.C_id = fx("C_id", 128)
        self.C_idb = fx("C_idb", 128, BF16)
        self.C_ones = fx("C_ones", 2)
        self.C_127 = fx("C_127", 1)
        self.V_sgv = fx("V_sgv", 4)
        self.V_sout = fx("V_sout", 4)
        self.ST_mu = fx("ST_mu", 16)
        self.ST_R = fx("ST_R", 16)
        self.STq_mu = fx("STq_mu", NQT)
        self.STq_R = fx("STq_R", NQT)
        self.ST1_mu = fx("ST1_mu", NQT)
        self.ST1_r1 = fx("ST1_r1", NQT)
        self.ST1_al = fx("ST1_al", NQT)
        self.ST1_a = fx("ST1_a", NQT)
        self.ST1_b = fx("ST1_b", NQT)
        self.C1c = fx("C1c", NQT)
        self.RMX = fx("RMX", NQT)
        self.SC2 = fx("SC2", NQT)
        self.SO2 = fx("SO2", NQT)
        self.SE = fx("SE", 4)
        self.TMP = fx("TMP", 16)
        self.BN6 = fx("BN6", 96)
        self.BN2 = fx("BN2", 4)
        self.PARTS = fx("PARTS", 392)

        self.PS_SC = nc.alloc_psum_tensor("PS_SC", [128, 2048], F32)
        self.PS_A = nc.alloc_psum_tensor("PS_A", [128, 512], F32)
        self.PS_B = nc.alloc_psum_tensor("PS_B", [128, 512], F32)
        self.PS_C = nc.alloc_psum_tensor("PS_C", [128, 512], F32)
        self.PS_D = nc.alloc_psum_tensor("PS_D", [128, 512], F32)

        self.sems = {}
        self.cnt = {}
        self._sem_cms = []
        for name in ["LD", "CR", "DV", "AC", "AE", "PE", "GP", "ST", "Y", "VH",
                     "L0", "L1", "L2", "L3", "X0", "X1", "G0", "G1",
                     "S0", "S1", "Y0", "Y1", "H0", "H1"]:
            cm = nc.semaphore(name)
            self.sems[name] = cm.__enter__()
            self._sem_cms.append(cm)
            self.cnt[name] = 0

    def af(self, off, cols):
        return self.AF.ap()[:, off:off + cols]

    def afb(self, off_f32, cols_bf16):
        return self.AF.ap().bitcast(BF16)[:, 2 * off_f32: 2 * off_f32 + cols_bf16]

    def af3(self, off, stride, n, inner):
        return bass.AP(self.AF, off, [[49152, 128], [stride, n], [1, inner]])

    def bap(self, off_b, dims_b):
        f32dims = [[s // 2, n] for (s, n) in dims_b[:-1]] + \
                  [[1, dims_b[-1][1] // 2]]
        return bass.AP(self.AF, off_b // 2,
                       [[49152, 128]] + f32dims).bitcast(BF16)

    DMA_SEMS = ["LD", "GP", "ST", "Y", "L0", "L1", "L2", "L3", "X0", "X1",
                "G0", "G1", "S0", "S1", "Y0", "Y1", "H0", "H1"]

    def fence_block(self, name="fence"):
        with self.nc.Block(name) as blk:
            @blk.sync
            def _(sy):
                for s in self.DMA_SEMS + ["DV", "AC", "AE", "PE", "CR", "VH"]:
                    if self.cnt[s]:
                        sy.wait_ge(self.sems[s], self.cnt[s])

    def inc(self, name, n=16):
        self.cnt[name] += n
        return self.sems[name]

    def dma(self, eng, dst, src, sem="LD"):
        eng.dma_start(dst, src).then_inc(self.inc(sem, 16), 16)

    def vmark(self, inst):
        inst.then_inc(self.inc("VH", 1), 1)
        return self.cnt["VH"]

    def vbar(self, v, inst):
        v.wait_ge(self.sems["VH"], self.vmark(inst))

    P1B = 31744
    P1QS = ["G0", "G1", "GP"]

    def p1_buf(self, k):
        return self.af(self.P1B + (k % 3) * 2048, 2048)

    def p1_chunk_src(self, k):
        if k < 128:
            return self.wgv.ap()[k * 128:(k + 1) * 128, :]
        r, cc = divmod(k - 128, 4)
        return self.wout.ap()[r * 128:(r + 1) * 128, cc * 2048:(cc + 1) * 2048]

    def p1_load_sw(self, gp, k):
        if k >= 3:
            gp.wait_ge(self.sems["CR"], self._p1cr0 + k - 2)
        self.dma(gp, self.p1_buf(k), self.p1_chunk_src(k), self.P1QS[k % 3])

    def p1_reduce_sw(self, sc):
        k = self._p1k_sw
        self._p1k_sw += 1
        q = self.P1QS[k % 3]
        sc.wait_ge(self.sems[q], self._p1q0[q] + 16 * (k // 3 + 1))
        buf = self.p1_buf(k)
        sc.activation(buf, buf, ACT.Abs,
                      accum_out=self.PARTS.ap()[:, k:k + 1]
                      ).then_inc(self.inc("CR", 1), 1)

    def stats(self, v, off, cols=2048):
        nchunk = cols // 512
        last = None
        for cc in range(nchunk):
            last = v.bn_stats(self.BN6.ap()[:, cc * 6:(cc + 1) * 6],
                              self.af(off + cc * 512, 512))
        self.vbar(v, last)
        self.vbar(v, v.bn_aggr(self.BN2.ap()[:, 0:2], self.BN6.ap()[:, 0:6 * nchunk]))

    def rp_pipe(self, v, zb, cols):
        P = self
        st = {"pend": None, "p": 0}

        def zoff(p, j):
            return zb + p * 3 * cols + j * cols

        def s3(item):
            p, dst_b, m2 = item
            v.wait_ge(P.sems["VH"], m2)
            v.scalar_tensor_tensor(
                P.bap(dst_b, [[1, cols]]),
                P.af(zoff(p, 1), cols).bitcast(I32), -128.0,
                P.af(zoff(p, 2), cols), ALU.add,
                ALU.add).then_inc(P.inc("DV", 1), 1)

        def feed(pre, src, scale_col, dst_b):
            prev = st["pend"]
            if prev:
                s3(prev)
            p = st["p"]
            st["p"] ^= 1
            pre()
            m0 = P.vmark(v.tensor_scalar(P.af(zoff(p, 0), cols), src, scale_col,
                                         128.0, ALU.mult, ALU.add))
            v.wait_ge(P.sems["VH"], m0)
            m1 = P.vmark(v.tensor_copy(P.af(zoff(p, 1), cols).bitcast(I32),
                                       P.af(zoff(p, 0), cols)))
            v.wait_ge(P.sems["VH"], m1)
            m2 = P.vmark(v.scalar_tensor_tensor(
                P.af(zoff(p, 2), cols), P.af(zoff(p, 0), cols), -0.5,
                P.af(zoff(p, 1), cols).bitcast(I32), ALU.add, ALU.is_gt))
            st["pend"] = (p, dst_b, m2)

        def flush():
            prev = st["pend"]
            if prev is not None:
                s3(prev)
            st["pend"] = None
        return feed, flush

    def round_pos(self, v, src, scale_col, z0, z1, z2, dst_b, cols):
        self.vbar(v, v.tensor_scalar(self.af(z0, cols), src, scale_col, 128.0,
                                     ALU.mult, ALU.add))
        self.vbar(v, v.tensor_copy(self.af(z1, cols).bitcast(I32),
                                   self.af(z0, cols)))
        self.vbar(v, v.scalar_tensor_tensor(self.af(z2, cols),
                                            self.af(z0, cols), -0.5,
                                            self.af(z1, cols).bitcast(I32),
                                            ALU.add, ALU.is_gt))
        return v.scalar_tensor_tensor(self.bap(dst_b, [[1, cols]]),
                                      self.af(z1, cols).bitcast(I32), -128.0,
                                      self.af(z2, cols), ALU.add, ALU.add)

    def build(self):
        self.p0_consts()
        if self.stage == 1:
            self.p1_absum()
        if self.stage >= 2:
            self.p2_proj(True)
        if self.stage >= 3:
            self.p2_proj(False)
        if self.stage >= 4:
            self.p3_attn()
        if self.stage >= 5:
            self.p4_ffn_a()
        if self.stage >= 6:
            self.p5_gv()
        if self.stage >= 7:
            self.p6_tout()
            self.p7_out2()
        self.p_final()
        for cm in reversed(self._sem_cms):
            cm.__exit__(None, None, None)
        return self.nc

    def p0_consts(self):
        nc = self.nc
        with nc.Block("p0") as blk:
            @blk.sync
            def _(sy):
                self.dma(sy, self.C_bq.ap(), self.bq_p.ap())
                self.dma(sy, self.C_bk.ap(), self.bk_p.ap())
                self.dma(sy, self.C_id.ap(), self.ident.ap())
                self.dma(sy, self.C_idb.ap(), self.identb.ap())
                self.dma(sy, self.af(OF_G, D), self.g_rep.ap())
                self.dma(sy, self.af(OF_B, D), self.b_rep.ap())
                self.dma(sy, self.af(OF_BV, D), self.bv_rep.ap())
                self.dma(sy, self.af(OF_BO, D), self.bo_rep.ap())
                self.dma(sy, self.af(OF_NW, D), self.nw_rep.ap())

            @blk.vector
            def _(v):
                v.memset(self.C_ones.ap()[:, 0:1], 1.0)
                v.memset(self.C_127.ap(), 127.0)
                v.memset(self.RMX.ap(), 0.0)
                v.memset(self.TMP.ap(), 0.0)
                v.memset(self.SE.ap(), 0.0)

            @blk.gpsimd
            def _(gp):
                gp.wait_ge(self.sems["LD"], self.cnt["LD"])

    def p1_absum(self):
        nc = self.nc
        tiles = []
        for r in range(O2 // 128):
            tiles.append((self.wgv.ap()[r * 128:(r + 1) * 128, :], 0))
        for r in range(D // 128):
            for cc in range(4):
                tiles.append((self.wout.ap()[r * 128:(r + 1) * 128,
                                             cc * 2048:(cc + 1) * 2048], 1))
        NB = 4
        self.fence_block("f_p1")
        cr0 = self.cnt["CR"]
        c0 = dict(self.cnt)
        with nc.Block("p1") as blk:
            @blk.sync
            def _(sy):
                for t, (src, _) in enumerate(tiles):
                    if t >= NB:
                        sy.wait_ge(self.sems["CR"], cr0 + (t - NB + 1))
                    self.dma(sy, self.af((t % NB) * 2048, 2048), src,
                             f"L{t % 4}")

            PARTS = 4 * 2048

            @blk.vector
            def _(v):
                for t, (_, j) in enumerate(tiles):
                    v.wait_ge(self.sems[f"L{t % 4}"],
                              c0[f"L{t % 4}"] + 16 * (t // 4 + 1))
                    v.tensor_reduce(self.af(PARTS + t, 1), self.af((t % NB) * 2048, 2048),
                                    AX.X, ALU.add,
                                    apply_absolute_value=True).then_inc(self.inc("CR", 1), 1)
                v.wait_ge(self.sems["CR"], cr0 + len(tiles))
                self.vbar(v, v.tensor_reduce(self.TMP.ap()[:, 0:1],
                                             self.af(PARTS, 128), AX.X, ALU.add))
                self.vbar(v, v.tensor_reduce(self.TMP.ap()[:, 1:2],
                                             self.af(PARTS + 128, 64), AX.X, ALU.add))
                v.nop().then_inc(self.inc("CR", 1), 1)

            @blk.tensor
            def _(te):
                te.wait_ge(self.sems["CR"], cr0 + len(tiles) + 1)
                te.matmul(self.PS_A.ap()[0:1, 0:1], self.TMP.ap()[:, 0:1],
                          self.C_ones.ap()[:, 0:1], start=True, stop=True)
                te.matmul(self.PS_B.ap()[0:1, 0:1], self.TMP.ap()[:, 1:2],
                          self.C_ones.ap()[:, 0:1], start=True, stop=True).then_inc(self.inc("PE", 1), 1)

            @blk.scalar
            def _(sc):
                sc.wait_ge(self.sems["PE"], self.cnt["PE"])
                sc.copy(self.TMP.ap()[0:1, 8:9], self.PS_A.ap()[0:1, 0:1])
                sc.copy(self.TMP.ap()[0:1, 9:10], self.PS_B.ap()[0:1, 0:1]).then_inc(self.inc("AC", 1), 1)

            @blk.tensor
            def _(te):
                te.wait_ge(self.sems["AC"], self.cnt["AC"])
                te.matmul(self.PS_C.ap()[:, 0:2],
                          self.C_ones.ap()[0:1, 0:1].to_broadcast((1, 128)),
                          self.TMP.ap()[0:1, 8:10], start=True, stop=True).then_inc(self.inc("PE", 1), 1)

            @blk.vector
            def _(v):
                v.wait_ge(self.sems["PE"], self.cnt["PE"])
                v.tensor_scalar(self.V_sgv.ap()[:, 0:1], self.PS_C.ap()[:, 0:1],
                                1.0 / (O2 * D), 1e-5, ALU.mult, ALU.max)
                self.vbar(v, v.tensor_scalar(self.V_sout.ap()[:, 0:1],
                                             self.PS_C.ap()[:, 1:2],
                                             1.0 / (D * HID), 1e-5, ALU.mult, ALU.max))
                insts = []
                for vv in (self.V_sgv, self.V_sout):
                    insts.append(v.tensor_scalar(vv.ap()[:, 1:2], vv.ap()[:, 0:1],
                                                 0.5, None, ALU.mult, ALU.bypass))
                    insts.append(v.tensor_scalar(vv.ap()[:, 2:3], vv.ap()[:, 0:1],
                                                 -0.5, None, ALU.mult, ALU.bypass))
                insts[-1].then_inc(self.inc("DV", 1), 1)

            if self.dbg:
                @blk.scalar
                def _(sc):
                    sc.wait_ge(self.sems["DV"], self.cnt["DV"])
                    sc.copy(self.TMP.ap()[:, 12:13], self.V_sgv.ap()[:, 0:1])
                    sc.copy(self.TMP.ap()[:, 13:14], self.V_sout.ap()[:, 0:1]).then_inc(self.inc("AC", 1), 1)
                    sc.wait_ge(self.sems["AC"], self.cnt["AC"])

                @blk.gpsimd
                def _(gp):
                    gp.wait_ge(self.sems["AC"], self.cnt["AC"])
                    self.dma(gp, self.dbg_s.ap(), self.TMP.ap()[:, 0:16], "GP")

    def p2_proj(self, is_q):
        nc = self.nc
        ntile = NQT if is_q else 16
        ncols = ntile * 128
        xsrc = self.xq if is_q else self.x
        mu_st, R_st = (self.STq_mu, self.STq_R) if is_q else (self.ST_mu, self.ST_R)
        XNTH = 0
        XNTM = 8 * ncols
        XB = 16 * ncols
        XBH = XB + 2048
        XBM = XBH + 1024
        tag = "q" if is_q else "kv"

        XB0 = (16384, 18432) if is_q else (32768, 34816)
        XBH = 20480 if is_q else 36864
        XBM = 21504 if is_q else 37888
        self.fence_block(f"f_p2{tag}x")
        if is_q:
            self._p1k_sw = 0
            self._p1cr0 = self.cnt["CR"]
            self._p1q0 = dict(self.cnt)
        ld0, dv0, ac0, ae0, pe0 = (self.cnt[k] for k in ["LD", "DV", "AC", "AE", "PE"])
        cx0 = dict(self.cnt)
        d_split = lambda t: dv0 + 2 * t + 2
        a_sqrt = lambda t: ac0 + (1 if t == 0 else 9 * t - 7)
        a_evac = lambda t, r: ac0 + 9 * t + 3 + r
        with nc.Block(f"p2{tag}_xn") as blk:
            @blk.sync
            def _(sy):
                for t in range(ntile):
                    if t >= 2:
                        sy.wait_ge(self.sems["DV"], d_split(t - 2))
                    self.dma(sy, self.af(XB0[t % 2], 2048),
                             xsrc.ap()[t * 128:(t + 1) * 128, :], f"L{t % 2}")
                    if is_q and t < 2:
                        for c in range(16):
                            self.dma(sy, self.af((22528, 24576)[t] + c * 128, 128),
                                     self.wq.ap()[c * 128:(c + 1) * 128,
                                                  t * 128:(t + 1) * 128],
                                     f"L{2 + t}")

            if is_q:
                @blk.gpsimd
                def _(gp):
                    for k in range(27):
                        self.p1_load_sw(gp, k)

            @blk.vector
            def _(v):
                if is_q:
                    v.tensor_scalar(self.C_bqs.ap(), self.C_bq.ap(), ISQ, None,
                                    ALU.mult, ALU.bypass)
                m = {}
                for slot in range(ntile + 1):
                    tn, to = slot, slot - 1
                    pn, po = tn % 2, to % 2
                    if to >= 0:
                        v.wait_ge(self.sems["VH"], m["st", to])
                        self.vbar(v, v.bn_aggr(self.BN2.ap()[:, 2 * po:2 * po + 2],
                                               self.BN6.ap()[:, 24 * po:24 * po + 24]))
                        m["mu", to] = self.vmark(v.tensor_copy(
                            mu_st.ap()[:, to:to + 1], self.BN2.ap()[:, 2 * po:2 * po + 1]))
                        v.tensor_scalar(self.TMP.ap()[:, 4 + po:5 + po],
                                        self.BN2.ap()[:, 2 * po + 1:2 * po + 2],
                                        1.0 + EPS, EPS * EPS, ALU.mult,
                                        ALU.add).then_inc(self.inc("DV", 1), 1)
                    if tn < ntile:
                        v.wait_ge(self.sems[f"L{tn % 2}"],
                                  cx0[f"L{tn % 2}"] + 16 * (tn // 2 + 1))
                        v.bn_stats(self.BN6.ap()[:, 24 * pn:24 * pn + 6],
                                   self.af(XB0[pn], 512))
                    if to >= 0:
                        v.wait_ge(self.sems["AC"], a_sqrt(to))
                        m["rc", to] = self.vmark(v.reciprocal(
                            R_st.ap()[:, to:to + 1], self.TMP.ap()[:, 6 + po:7 + po]))
                    if tn < ntile:
                        v.bn_stats(self.BN6.ap()[:, 24 * pn + 6:24 * pn + 12],
                                   self.af(XB0[pn] + 512, 512))
                    if to >= 0:
                        v.wait_ge(self.sems["VH"], m["mu", to])
                        m["s1", to] = self.vmark(v.scalar_tensor_tensor(
                            self.af(XB0[po], 2048), self.af(XB0[po], 2048),
                            mu_st.ap()[:, to:to + 1], self.af(OF_G, D),
                            ALU.subtract, ALU.mult))
                    if tn < ntile:
                        v.bn_stats(self.BN6.ap()[:, 24 * pn + 12:24 * pn + 18],
                                   self.af(XB0[pn] + 1024, 512))
                    if to >= 0:
                        v.wait_ge(self.sems["VH"], m["s1", to])
                        m["s2", to] = self.vmark(v.scalar_tensor_tensor(
                            self.af(XB0[po], 2048), self.af(XB0[po], 2048),
                            R_st.ap()[:, to:to + 1], self.af(OF_B, D),
                            ALU.mult, ALU.add))
                    if tn < ntile:
                        m["st", tn] = self.vmark(v.bn_stats(
                            self.BN6.ap()[:, 24 * pn + 18:24 * pn + 24],
                            self.af(XB0[pn] + 1536, 512)))
                    if to >= 0:
                        if to >= 1:
                            v.wait_ge(self.sems["PE"], pe0 + 8 * to)
                        v.wait_ge(self.sems["VH"], m["s2", to])
                        m["ha", to] = self.vmark(v.tensor_copy(
                            self.bap(2 * XBH, [[1, 1024]]), self.af(XB0[po], 1024)))
                        m["hb", to] = self.vmark(v.tensor_copy(
                            self.bap(2 * XBH + 1024, [[1, 1024]]),
                            self.af(XB0[po] + 1024, 1024)))
                        v.wait_ge(self.sems["VH"], m["ha", to])
                        v.scalar_tensor_tensor(self.bap(2 * XBM, [[1, 1024]]),
                                               self.af(XB0[po], 1024), 0.0,
                                               self.bap(2 * XBH, [[1, 1024]]),
                                               ALU.add, ALU.subtract)
                        v.wait_ge(self.sems["VH"], m["hb", to])
                        v.scalar_tensor_tensor(self.bap(2 * XBM + 1024, [[1, 1024]]),
                                               self.af(XB0[po] + 1024, 1024), 0.0,
                                               self.bap(2 * XBH + 1024, [[1, 1024]]),
                                               ALU.add, ALU.subtract
                                               ).then_inc(self.inc("DV", 1), 1)
                if not is_q:
                    v.wait_ge(self.sems["CR"], self._p1cr0 + 192)
                    v.tensor_reduce(self.TMP.ap()[:, 0:1], self.PARTS.ap()[:, 0:128],
                                    AX.X, ALU.add)
                    v.tensor_reduce(self.TMP.ap()[:, 1:2], self.PARTS.ap()[:, 128:192],
                                    AX.X, ALU.add).then_inc(self.inc("DV", 1), 1)
                    v.wait_ge(self.sems["PE"], pe0 + 8 * ntile + 2)
                    v.tensor_scalar(self.V_sgv.ap()[:, 0:1], self.PS_A.ap()[:, 4:5],
                                    1.0 / (O2 * D), 1e-5, ALU.mult, ALU.max)
                    self.vbar(v, v.tensor_scalar(self.V_sout.ap()[:, 0:1],
                                                 self.PS_A.ap()[:, 5:6],
                                                 1.0 / (D * HID), 1e-5, ALU.mult,
                                                 ALU.max))
                    insts = []
                    for vv in (self.V_sgv, self.V_sout):
                        insts.append(v.tensor_scalar(vv.ap()[:, 1:2], vv.ap()[:, 0:1],
                                                     0.5, None, ALU.mult, ALU.bypass))
                        insts.append(v.tensor_scalar(vv.ap()[:, 2:3], vv.ap()[:, 0:1],
                                                     -0.5, None, ALU.mult, ALU.bypass))
                    insts[-1].then_inc(self.inc("DV", 1), 1)

            @blk.tensor
            def _(te):
                for t in range(ntile):
                    te.wait_ge(self.sems["DV"], d_split(t))
                    for hm in range(2):
                        srcb = XBH if hm == 0 else XBM
                        for cg in range(4):
                            idx = 8 * t + 4 * hm + cg
                            if idx >= 2:
                                tp_, r = divmod(idx - 2, 8)
                                te.wait_ge(self.sems["AC"], a_evac(tp_, r))
                            psb = (self.PS_C if idx % 2 == 0
                                   else self.PS_D).ap().bitcast(BF16)
                            for c4 in range(4):
                                c = 4 * cg + c4
                                _t = te.transpose(psb[:, c4 * 128:(c4 + 1) * 128],
                                                  self.bap(2 * srcb + c * 128,
                                                           [[1, 128]]),
                                                  self.C_idb.ap())
                            _t.then_inc(self.inc("PE", 1), 1)
                if not is_q:
                    te.wait_ge(self.sems["DV"], dv0 + 2 * ntile + 1)
                    te.matmul(self.PS_A.ap()[0:1, 0:1], self.TMP.ap()[:, 0:1],
                              self.C_ones.ap()[:, 0:1], start=True, stop=True)
                    te.matmul(self.PS_B.ap()[0:1, 0:1], self.TMP.ap()[:, 1:2],
                              self.C_ones.ap()[:, 0:1], start=True,
                              stop=True).then_inc(self.inc("PE", 1), 1)
                    te.wait_ge(self.sems["AC"], ac0 + 9 * ntile + 1)
                    te.matmul(self.PS_A.ap()[:, 4:6],
                              self.C_ones.ap()[0:1, 0:1].to_broadcast((1, 128)),
                              self.TMP.ap()[0:1, 8:10], start=True,
                              stop=True).then_inc(self.inc("PE", 1), 1)

            @blk.scalar
            def _(sc):
                for t in range(ntile + 1):
                    if t < ntile:
                        sc.wait_ge(self.sems["DV"], dv0 + 2 * t + 1)
                        sc.activation(self.TMP.ap()[:, 6 + t % 2:7 + t % 2],
                                      self.TMP.ap()[:, 4 + t % 2:5 + t % 2],
                                      ACT.Sqrt).then_inc(self.inc("AC", 1), 1)
                    if t >= 1:
                        to = t - 1
                        for hm in range(2):
                            dstb = XNTH if hm == 0 else XNTM
                            for cg in range(4):
                                idx = 8 * to + 4 * hm + cg
                                sc.wait_ge(self.sems["PE"], pe0 + idx + 1)
                                psb = (self.PS_C if idx % 2 == 0
                                       else self.PS_D).ap().bitcast(BF16)
                                dst = self.bap(2 * dstb + (4 * cg) * ncols + to * 128,
                                               [[ncols, 4], [1, 128]])
                                sc.copy(dst, psb[:, 0:512]
                                        .rearrange("p (a c) -> p a c", a=4)
                                        ).then_inc(self.inc("AC", 1), 1)
                    if is_q and t < ntile:
                        for _ in range(3):
                            self.p1_reduce_sw(sc)
                if not is_q:
                    sc.wait_ge(self.sems["PE"], pe0 + 8 * ntile + 1)
                    sc.copy(self.TMP.ap()[0:1, 8:9], self.PS_A.ap()[0:1, 0:1])
                    sc.copy(self.TMP.ap()[0:1, 9:10],
                            self.PS_B.ap()[0:1, 0:1]).then_inc(self.inc("AC", 1), 1)

        ngrp = (T // 512) if is_q else (S // 512)
        wsrc = self.wq if is_q else self.wk
        bias = self.C_bqs if is_q else self.C_bk
        evsc = ISQ if is_q else 1.0
        dst_scr = self.Qscr if is_q else self.Kscr
        if is_q:
            WF = (22528, 24576)
            WH = (26624, 27648)
            WM = (28672, 29696)
            EVQ = 30720
        else:
            WF = (32768, 34816)
            WH = (36864, 37888)
            WM = (OF_G, OF_G + 1024)
            EVQ = OF_B
        self.fence_block(f"f_p2{tag}qk")
        ae0, pe0, dv0 = (self.cnt[k] for k in ["AE", "PE", "DV"])
        c0 = dict(self.cnt)
        with nc.Block(f"p2{tag}_qk") as blk:
            @blk.vector
            def _(v):
                for h in range(H):
                    if is_q and h < 2:
                        pass
                    else:
                        lp = f"L{h % 2}"
                        off = -256 if is_q else 0
                        v.wait_ge(self.sems[lp], c0[lp] + off + 16 * 16 * (h // 2 + 1))
                    if h >= 2:
                        v.wait_ge(self.sems["PE"], pe0 + (h - 1) * ngrp)
                    self.vbar(v, v.tensor_copy(self.bap(2 * WH[h % 2], [[1, 2048]]),
                                               self.af(WF[h % 2], 2048)))
                    v.scalar_tensor_tensor(self.bap(2 * WM[h % 2], [[1, 2048]]),
                                           self.af(WF[h % 2], 2048), 0.0,
                                           self.bap(2 * WH[h % 2], [[1, 2048]]),
                                           ALU.add, ALU.subtract
                                           ).then_inc(self.inc("DV", 1), 1)

            @blk.sync
            def _(sy):
                for h in range(H):
                    if is_q and h < 2:
                        continue
                    if h >= 2:
                        sy.wait_ge(self.sems["DV"], dv0 + h - 1)
                    for c in range(16):
                        self.dma(sy, self.af(WF[h % 2] + c * 128, 128),
                                 wsrc.ap()[c * 128:(c + 1) * 128, h * 128:(h + 1) * 128],
                                 f"L{h % 2}")

            if is_q:
                @blk.gpsimd
                def _(gp):
                    for k in range(27, 192):
                        self.p1_load_sw(gp, k)

            @blk.tensor
            def _(te):
                for h in range(H):
                    te.wait_ge(self.sems["DV"], dv0 + h + 1)
                    for grp in range(ngrp):
                        idx = h * ngrp + grp
                        if idx >= 2:
                            te.wait_ge(self.sems["AE"], ae0 + idx - 1)
                        ps = (self.PS_C if idx % 2 == 0 else self.PS_D).ap()
                        for c in range(16):
                            for (wb, xb) in ((WH, XNTH), (WH, XNTM), (WM, XNTH)):
                                _pe_last = te.matmul(
                                    ps,
                                    self.bap(2 * wb[h % 2] + c * 128, [[1, 128]]),
                                    self.bap(2 * xb + c * ncols + grp * 512,
                                             [[1, 512]]),
                                    start=(c == 0 and wb is WH and xb is XNTH),
                                    stop=(c == 15 and xb is XNTH and wb is WM))
                        _pe_last.then_inc(self.inc("PE", 1), 1)

            @blk.scalar
            def _(sc):
                for h in range(H):
                    for grp in range(ngrp):
                        idx = h * ngrp + grp
                        sc.wait_ge(self.sems["PE"], pe0 + idx + 1)
                        sp = f"S{idx % 2}"
                        if idx >= 2:
                            sc.wait_ge(self.sems[sp], c0[sp] + 16 * (idx // 2))
                        ps = (self.PS_C if idx % 2 == 0 else self.PS_D).ap()
                        ev = self.af(EVQ + (idx % 2) * 512, 512)
                        sc.activation(ev, ps, ACT.Identity, bias=bias.ap()[:, h:h + 1],
                                      scale=evsc).then_inc(self.inc("AE", 1), 1)
                        sc.wait_ge(self.sems["AE"], self.cnt["AE"])
                        self.dma(sc, dst_scr.ap()[h, :, grp * 512:(grp + 1) * 512],
                                 ev, sp)
                        if is_q:
                            quota = 24 + ((idx + 1) * 168 + 31) // 32
                            while self._p1k_sw < quota:
                                self.p1_reduce_sw(sc)

        if is_q:
            return

        WVF = 32768
        WVH = 33792
        WVM = 37888
        EV2 = 41984
        self.fence_block("f_p2v")
        dv0, pe0 = (self.cnt[k] for k in ["DV", "PE"])
        c0 = dict(self.cnt)

        def ev_dv(idx):
            return dv0 + 32 * (idx // 16) + 16 + (idx % 16) + 1

        with nc.Block("p2kv_v") as blk:
            @blk.sync
            def _(sy):
                for oc in range(4):
                    for c in range(16):
                        li = oc * 16 + c
                        if li >= 2:
                            sy.wait_ge(self.sems["DV"],
                                       dv0 + 32 * ((li - 2) // 16) + ((li - 2) % 16) + 1)
                        self.dma(sy, self.af(WVF + (li % 2) * 512, 512),
                                 self.wv.ap()[c * 128:(c + 1) * 128,
                                              oc * 512:(oc + 1) * 512],
                                 f"L{li % 2}")

            @blk.vector
            def _(v):
                for oc in range(4):
                    for c in range(16):
                        li = oc * 16 + c
                        lp = f"L{li % 2}"
                        v.wait_ge(self.sems[lp], c0[lp] + 16 * (li // 2 + 1))
                        if oc > 0 and c == 0:
                            v.wait_ge(self.sems["PE"], pe0 + oc * 16)
                        w = self.af(WVF + (li % 2) * 512, 512)
                        self.vbar(v, v.tensor_copy(
                            self.bap(2 * (WVH + c * 256), [[1, 512]]), w))
                        v.scalar_tensor_tensor(
                            self.bap(2 * (WVM + c * 256), [[1, 512]]), w, 0.0,
                            self.bap(2 * (WVH + c * 256), [[1, 512]]),
                            ALU.add, ALU.subtract).then_inc(self.inc("DV", 1), 1)
                    for tk in range(16):
                        idx = oc * 16 + tk
                        v.wait_ge(self.sems["PE"], pe0 + idx + 1)
                        sp = f"G{idx % 2}"
                        v.wait_ge(self.sems[sp], c0[sp] + 64 * (idx // 2))
                        ps = (self.PS_C if idx % 2 == 0 else self.PS_D).ap()
                        ev = self.af(EV2 + (idx % 2) * 512, 512)
                        v.tensor_tensor(ev, ps, self.af(OF_BV + oc * 512, 512),
                                        ALU.add).then_inc(self.inc("DV", 1), 1)

            @blk.tensor
            def _(te):
                for oc in range(4):
                    te.wait_ge(self.sems["DV"], dv0 + 32 * oc + 16)
                    for tk in range(16):
                        idx = oc * 16 + tk
                        if idx >= 2:
                            te.wait_ge(self.sems["DV"], ev_dv(idx - 2))
                        ps = (self.PS_C if idx % 2 == 0 else self.PS_D).ap()
                        for c in range(16):
                            for (xb, wv_) in ((XNTH, WVH), (XNTM, WVH), (XNTH, WVM)):
                                _pe_last = te.matmul(
                                    ps,
                                    self.bap(2 * xb + c * ncols + tk * 128, [[1, 128]]),
                                    self.bap(2 * (wv_ + c * 256), [[1, 512]]),
                                    start=(c == 0 and xb is XNTH and wv_ is WVH),
                                    stop=(c == 15 and xb is XNTH and wv_ is WVM))
                        _pe_last.then_inc(self.inc("PE", 1), 1)

            @blk.gpsimd
            def _(gp):
                for oc in range(4):
                    for tk in range(16):
                        idx = oc * 16 + tk
                        gp.wait_ge(self.sems["DV"], ev_dv(idx))
                        ev = self.af(EV2 + (idx % 2) * 512, 512)
                        sp = f"G{idx % 2}"
                        for j in range(4):
                            h = oc * 4 + j
                            self.dma(gp, self.Vscr.ap()[h, tk * 128:(tk + 1) * 128, :],
                                     ev[:, j * 128:(j + 1) * 128], sp)

    def p3_attn(self):
        nc = self.nc
        PB0, AVT = 0, 16384
        KB0, VB0, QB0, PTB0, MASKS = 24576, 28672, 34816, 35840, 36864
        X1B, WOB, X0B = 0, 8192, 24576
        st_done = self.cnt["ST"]
        gp_done = self.cnt["GP"]
        VBS = ["X1", "S0", "S1"]

        for s in range(2):
            Rs = 1024 * (s + 1)
            nkt = Rs // 128
            self.fence_block(f"f_p3s{s}")
            c0 = dict(self.cnt)

            def rngq(qt):
                return 256 * (4 * s + qt + 1)

            def qlo_of(kt):
                q = 0
                while 256 * (4 * s + q + 1) <= kt * 128:
                    q += 1
                return q

            pe_prog, dv_prog, ac_prog = [], [], []
            for i in range(17):
                do_sm, do_kt = i <= 15, i >= 1
                h, hp = i, i - 1
                scpos = {(qt * nkt) // 4: qt for qt in range(4)}
                if do_sm and not do_kt:
                    for qt in range(4):
                        pe_prog.append(("sc", h, qt))
                    for qt in range(4):
                        ac_prog.append(("exp", h, qt))
                if do_kt:
                    pe_prog.append(("T", hp, 0))
                    for kt in range(nkt):
                        if do_sm and kt in scpos:
                            pe_prog.append(("sc", h, scpos[kt]))
                        if kt + 1 < nkt:
                            pe_prog.append(("T", hp, kt + 1))
                        pe_prog.append(("PV", hp, kt))
                    for kt in range(nkt):
                        ac_prog.append(("evac", hp, kt))
                        if do_sm and kt in scpos:
                            ac_prog.append(("exp", h, scpos[kt]))
                if do_sm:
                    for kind, qt in [("bTT", 0), ("bRed", 0),
                                     ("bTT", 1), ("bRed", 1), ("rcp", 0),
                                     ("bTT", 2), ("bRed", 2), ("rcp", 1),
                                     ("nTS", 0),
                                     ("bTT", 3), ("bRed", 3), ("rcp", 2),
                                     ("nTS", 1), ("rcp", 3),
                                     ("nTS", 2), ("nTS", 3)]:
                        dv_prog.append((kind, h, qt))
                if do_kt:
                    dv_prog.append(("avt", hp))
            PEn = {op: c0["PE"] + k + 1 for k, op in enumerate(pe_prog)}
            DVn = {op: c0["DV"] + k + 1 for k, op in enumerate(dv_prog)}
            ACn = {op: c0["AC"] + k + 1 for k, op in enumerate(ac_prog)}

            with nc.Block(f"p3_s{s}") as blk:
                @blk.sync
                def _(sy):
                    if s == 0:
                        sy.wait_ge(self.sems["ST"], st_done)
                        sy.wait_ge(self.sems["GP"], gp_done)
                        for i in range(NQT):
                            self.dma(sy, self.af(MASKS + i * 256, 256),
                                     self.maskq.ap()[i, :, :], "X0")
                    for h in range(H):
                        if h >= 2:
                            sy.wait_ge(self.sems["PE"], PEn[("sc", h - 2, 3)])
                        self.dma(sy, self.af(KB0 + (h % 2) * 2048, Rs),
                                 self.Kscr.ap()[h, :, 0:Rs], f"L{h % 2}")
                        self.dma(sy, self.af(QB0 + (h % 2) * 512, 512),
                                 self.Qscr.ap()[h, :, s * 512:(s + 1) * 512],
                                 f"L{2 + h % 2}")
                        if h >= 3:
                            sy.wait_ge(self.sems["PE"], PEn[("PV", h - 3, nkt - 1)])
                        vsrc = bass.AP(self.Vscr, h * S * DH,
                                       [[DH, 128], [128 * DH, nkt], [1, DH]])
                        self.dma(sy, self.af3(VB0 + (h % 3) * 2048, 128, nkt, 128),
                                 vsrc, VBS[h % 3])

                @blk.tensor
                def _(te):
                    for op in pe_prog:
                        kind, h, x = op
                        if kind == "sc":
                            qt = x
                            rng = rngq(qt)
                            if qt == 0:
                                lp = f"L{h % 2}"
                                te.wait_ge(self.sems[lp], c0[lp] + 16 * (h // 2 + 1))
                                lp = f"L{2 + h % 2}"
                                te.wait_ge(self.sems[lp], c0[lp] + 16 * (h // 2 + 1))
                            prev = ("exp", h, qt - 1) if qt > 0 else ("exp", h - 1, 3)
                            if prev in ACn:
                                te.wait_ge(self.sems["AC"], ACn[prev])
                            for ck in range((rng + 511) // 512):
                                n = min(512, rng - ck * 512)
                                _l = te.matmul(
                                    self.PS_SC.ap()[:, ck * 512:ck * 512 + n],
                                    self.af(QB0 + (h % 2) * 512 + qt * 128, 128),
                                    self.af(KB0 + (h % 2) * 2048 + ck * 512, n),
                                    start=True, stop=True)
                            _l.then_inc(self.inc("PE", 1), 1)
                        elif kind == "T":
                            kt = x
                            if kt == 0:
                                te.wait_ge(self.sems["DV"], DVn[("nTS", h, 3)])
                            if kt >= 2:
                                te.wait_ge(self.sems["AC"], ACn[("evac", h, kt - 2)])
                            elif h >= 1:
                                te.wait_ge(self.sems["AC"],
                                           ACn[("evac", h - 1, nkt - 2 + kt)])
                            ps = (self.PS_A if kt % 2 == 0 else self.PS_B).ap()
                            qlo = qlo_of(kt)
                            for qt in range(qlo, 4):
                                _l = te.transpose(
                                    ps[:, qt * 128:(qt + 1) * 128],
                                    self.af(PB0 + (h % 2) * 8192 + qt * 2048
                                            + kt * 128, 128),
                                    self.C_id.ap())
                            _l.then_inc(self.inc("PE", 1), 1)
                        else:
                            kt = x
                            te.wait_ge(self.sems["AC"], ACn[("evac", h, kt)])
                            if kt == 0:
                                vp = VBS[h % 3]
                                te.wait_ge(self.sems[vp], c0[vp] + 16 * (h // 3 + 1))
                                if h >= 1:
                                    te.wait_ge(self.sems["DV"], DVn[("avt", h - 1)])
                            qlo = qlo_of(kt)
                            te.matmul(self.PS_C.ap()[:, qlo * 128:512],
                                      self.af(VB0 + (h % 3) * 2048 + kt * 128, 128),
                                      self.af(PTB0 + (kt % 2) * 512 + qlo * 128,
                                              512 - qlo * 128),
                                      start=(kt == 0), stop=(kt == nkt - 1)
                                      ).then_inc(self.inc("PE", 1), 1)

                @blk.vector
                def _(v):
                    first = True
                    for op in dv_prog:
                        kind, h = op[0], op[1]
                        if kind == "bTT":
                            qt = op[2]
                            rng = rngq(qt)
                            v.wait_ge(self.sems["PE"], PEn[("sc", h, qt)])
                            if first and s == 0:
                                v.wait_ge(self.sems["X0"], c0["X0"] + 16 * NQT)
                                first = False
                            i = 4 * s + qt
                            v.tensor_tensor(
                                self.PS_SC.ap()[:, rng - 256:rng],
                                self.PS_SC.ap()[:, rng - 256:rng],
                                self.af(MASKS + i * 256, 256),
                                ALU.add).then_inc(self.inc("DV", 1), 1)
                        elif kind == "bRed":
                            qt = op[2]
                            rng = rngq(qt)
                            v.wait_ge(self.sems["DV"], DVn[("bTT", h, qt)])
                            if ("exp", h - 1, qt) in ACn:
                                v.wait_ge(self.sems["AC"], ACn[("exp", h - 1, qt)])
                            v.tensor_reduce(
                                self.TMP.ap()[:, 4 + qt:5 + qt],
                                self.PS_SC.ap()[:, 0:rng],
                                AX.X, ALU.max,
                                negate=True).then_inc(self.inc("DV", 1), 1)
                        elif kind == "rcp":
                            qt = op[2]
                            v.wait_ge(self.sems["AC"], ACn[("exp", h, qt)])
                            v.reciprocal(self.TMP.ap()[:, 8 + qt:9 + qt],
                                         self.SE.ap()[:, qt:qt + 1]
                                         ).then_inc(self.inc("DV", 1), 1)
                        elif kind == "nTS":
                            qt = op[2]
                            rng = rngq(qt)
                            v.wait_ge(self.sems["DV"], DVn[("rcp", h, qt)])
                            v.tensor_scalar(
                                self.af(PB0 + (h % 2) * 8192 + qt * 2048, rng),
                                self.af(PB0 + (h % 2) * 8192 + qt * 2048, rng),
                                self.TMP.ap()[:, 8 + qt:9 + qt], None, ALU.mult,
                                ALU.bypass).then_inc(self.inc("DV", 1), 1)
                        else:
                            v.wait_ge(self.sems["PE"], PEn[("PV", h, nkt - 1)])
                            v.tensor_copy(self.af(AVT + h * 512, 512),
                                          self.PS_C.ap()).then_inc(self.inc("DV", 1), 1)

                @blk.scalar
                def _(sc):
                    for op in ac_prog:
                        kind, h, x = op
                        if kind == "exp":
                            qt = x
                            rng = rngq(qt)
                            sc.wait_ge(self.sems["DV"], DVn[("bRed", h, qt)])
                            sc.activation(self.af(PB0 + (h % 2) * 8192 + qt * 2048, rng),
                                          self.PS_SC.ap()[:, 0:rng], ACT.Exp,
                                          bias=self.TMP.ap()[:, 4 + qt:5 + qt],
                                          scale=1.0,
                                          accum_out=self.SE.ap()[:, qt:qt + 1]
                                          ).then_inc(self.inc("AC", 1), 1)
                        else:
                            kt = x
                            sc.wait_ge(self.sems["PE"], PEn[("T", h, kt)])
                            if kt >= 2:
                                sc.wait_ge(self.sems["PE"], PEn[("PV", h, kt - 2)])
                            elif h >= 1:
                                sc.wait_ge(self.sems["PE"],
                                           PEn[("PV", h - 1, nkt - 2 + kt)])
                            ps = (self.PS_A if kt % 2 == 0 else self.PS_B).ap()
                            qlo = qlo_of(kt)
                            sc.copy(self.af(PTB0 + (kt % 2) * 512 + qlo * 128,
                                            512 - qlo * 128),
                                    ps[:, qlo * 128:512]
                                    ).then_inc(self.inc("AC", 1), 1)

            self.fence_block(f"f_p3w{s}")
            ld0, dv0, ac0, pe0 = (self.cnt[k] for k in ["LD", "DV", "AC", "PE"])
            c0 = dict(self.cnt)
            with nc.Block(f"p3_wo{s}") as blk:
                @blk.sync
                def _(sy):
                    for oc in range(4):
                        if oc > 0:
                            sy.wait_ge(self.sems["PE"], pe0 + oc * 4)
                        for c in range(16):
                            self.dma(sy, self.af(WOB + c * 512, 512),
                                     self.wo.ap()[c * 128:(c + 1) * 128,
                                                  oc * 512:(oc + 1) * 512])
                        for qt in range(4):
                            idx = oc * 4 + qt
                            if idx >= 2:
                                sy.wait_ge(self.sems["DV"], dv0 + idx - 1)
                            self.dma(sy, self.af(X0B + (idx % 2) * 512, 512),
                                     self.xq.ap()[(4 * s + qt) * 128:(4 * s + qt + 1) * 128,
                                                  oc * 512:(oc + 1) * 512],
                                     f"X{idx % 2}")

                @blk.tensor
                def _(te):
                    for oc in range(4):
                        te.wait_ge(self.sems["LD"], ld0 + 16 * 16 * (oc + 1))
                        for qt in range(4):
                            idx = oc * 4 + qt
                            if idx >= 2:
                                te.wait_ge(self.sems["DV"], dv0 + idx - 1)
                            ps = (self.PS_C if idx % 2 == 0 else self.PS_D).ap()
                            for c in range(16):
                                _pe_last = te.matmul(
                                    ps, self.af(AVT + c * 512 + qt * 128, 128),
                                    self.af(WOB + c * 512, 512),
                                    start=(c == 0), stop=(c == 15))
                            _pe_last.then_inc(self.inc("PE", 1), 1)

                @blk.vector
                def _(v):
                    mkbn = {}

                    def bnchunk(idx):
                        oc, qt = idx // 4, idx % 4
                        v.wait_ge(self.sems["DV"], dv0 + idx + 1)
                        mkbn[qt] = self.vmark(v.bn_stats(
                            self.BN6.ap()[:, qt * 24 + oc * 6:qt * 24 + oc * 6 + 6],
                            self.af(X1B + qt * 2048 + oc * 512, 512)))

                    for oc in range(4):
                        for qt in range(4):
                            idx = oc * 4 + qt
                            v.wait_ge(self.sems["PE"], pe0 + idx + 1)
                            v.wait_ge(self.sems[f"X{idx % 2}"],
                                      c0[f"X{idx % 2}"] + 16 * (idx // 2 + 1))
                            ps = (self.PS_C if idx % 2 == 0 else self.PS_D).ap()
                            dst = self.af(X1B + qt * 2048 + oc * 512, 512)
                            self.vbar(v, v.scalar_tensor_tensor(
                                dst, ps, 0.0,
                                self.af(X0B + (idx % 2) * 512, 512),
                                ALU.add, ALU.add))
                            v.tensor_tensor(dst, dst, self.af(OF_BO + oc * 512, 512),
                                            ALU.add).then_inc(self.inc("DV", 1), 1)
                            if idx >= 1:
                                bnchunk(idx - 1)
                    bnchunk(15)
                    for qt in range(4):
                        i = 4 * s + qt
                        v.wait_ge(self.sems["VH"], mkbn[qt])
                        self.vbar(v, v.bn_aggr(self.BN2.ap()[:, 0:2],
                                               self.BN6.ap()[:, qt * 24:qt * 24 + 24]))
                        v.tensor_copy(self.ST1_mu.ap()[:, i:i + 1], self.BN2.ap()[:, 0:1])
                        v.tensor_scalar(self.TMP.ap()[:, 4:5], self.BN2.ap()[:, 1:2],
                                        EPS, None, ALU.add, ALU.bypass)
                        v.tensor_scalar(self.TMP.ap()[:, 5:6], self.BN2.ap()[:, 1:2],
                                        1.0 + EPS, EPS * EPS, ALU.mult,
                                        ALU.add).then_inc(self.inc("DV", 1), 1)
                        v.wait_ge(self.sems["AC"], ac0 + qt + 1)
                        v.reciprocal(self.ST1_r1.ap()[:, i:i + 1], self.TMP.ap()[:, 6:7])
                        self.vbar(v, v.reciprocal(self.ST1_al.ap()[:, i:i + 1],
                                                  self.TMP.ap()[:, 7:8]))
                        v.tensor_scalar(self.ST1_a.ap()[:, i:i + 1],
                                        self.ST1_r1.ap()[:, i:i + 1], 1.0, None,
                                        ALU.add, ALU.bypass)
                        v.scalar_tensor_tensor(self.ST1_b.ap()[:, i:i + 1],
                                               self.ST1_mu.ap()[:, i:i + 1], -1.0,
                                               self.ST1_r1.ap()[:, i:i + 1],
                                               ALU.mult, ALU.mult).then_inc(self.inc("DV", 1), 1)

                @blk.scalar
                def _(sc):
                    for qt in range(4):
                        sc.wait_ge(self.sems["DV"], dv0 + 16 + 2 * qt + 1)
                        sc.activation(self.TMP.ap()[:, 6:7], self.TMP.ap()[:, 4:5], ACT.Sqrt)
                        sc.activation(self.TMP.ap()[:, 7:8], self.TMP.ap()[:, 5:6],
                                      ACT.Sqrt).then_inc(self.inc("AC", 1), 1)

                @blk.gpsimd
                def _(gp):
                    for qt in range(4):
                        i = 4 * s + qt
                        gp.wait_ge(self.sems["DV"], dv0 + 16 + 2 * qt + 2)
                        self.dma(gp, self.X1scr.ap()[i * 128:(i + 1) * 128, :],
                                 self.af(X1B + qt * 2048, 2048), "GP")

    def p4_ffn_a(self):
        nc = self.nc
        AQ1T, XB4, GB, Z0, Z1, Z2, AQN = 0, 8192, 10240, 12288, 14336, 16384, 18432
        gp_done = self.cnt["GP"]
        self.fence_block("f_p4")
        ld0, dv0, ae0, pe0 = (self.cnt[k] for k in ["LD", "DV", "AE", "PE"])
        with nc.Block("p4") as blk:
            @blk.sync
            def _(sy):
                sy.wait_ge(self.sems["GP"], gp_done)
                for i in range(NQT):
                    if i > 0:
                        sy.wait_ge(self.sems["DV"], dv0 + i)
                    self.dma(sy, self.af(XB4, 2048),
                             self.X1scr.ap()[i * 128:(i + 1) * 128, :])

            @blk.vector
            def _(v):
                for i in range(NQT):
                    v.wait_ge(self.sems["LD"], ld0 + 16 * (i + 1))
                    if i > 0:
                        v.wait_ge(self.sems["PE"], pe0 + 4 * i)
                    self.vbar(v, v.scalar_tensor_tensor(
                        self.af(GB, 2048), self.af(XB4, 2048),
                        self.ST1_mu.ap()[:, i:i + 1],
                        self.af(OF_NW, D), ALU.subtract, ALU.mult))
                    self.vbar(v, v.tensor_reduce(self.TMP.ap()[:, 4:5],
                                                 self.af(GB, 2048), AX.X,
                                                 ALU.max, apply_absolute_value=True))
                    self.vbar(v, v.tensor_scalar(self.TMP.ap()[:, 4:5],
                                                 self.TMP.ap()[:, 4:5],
                                                 1e-5, None, ALU.max, ALU.bypass))
                    self.vbar(v, v.reciprocal(self.TMP.ap()[:, 5:6],
                                               self.TMP.ap()[:, 4:5]))
                    v.tensor_scalar(self.TMP.ap()[:, 5:6], self.TMP.ap()[:, 5:6],
                                    127.0, None, ALU.mult, ALU.bypass)
                    self.vbar(v, v.tensor_tensor(self.TMP.ap()[:, 6:7],
                                                 self.ST1_al.ap()[:, i:i + 1],
                                                 self.TMP.ap()[:, 4:5], ALU.mult))
                    v.tensor_tensor(self.TMP.ap()[:, 7:8], self.TMP.ap()[:, 6:7],
                                    self.V_sgv.ap()[:, 0:1], ALU.mult)
                    self.round_pos(v, self.af(GB, 2048), self.TMP.ap()[:, 5:6],
                                   Z0, Z1, Z2, 2 * AQN, 2048)
                    v.tensor_scalar(self.C1c.ap()[:, i:i + 1], self.TMP.ap()[:, 7:8],
                                    1.0 / 127.0, None, ALU.mult,
                                    ALU.bypass).then_inc(self.inc("DV", 1), 1)

            @blk.tensor
            def _(te):
                for i in range(NQT):
                    te.wait_ge(self.sems["DV"], dv0 + i + 1)
                    for cg in range(4):
                        idx = 4 * i + cg
                        if idx >= 2:
                            te.wait_ge(self.sems["AE"], ae0 + idx - 1)
                        psb = (self.PS_C if idx % 2 == 0 else self.PS_D).ap().bitcast(BF16)
                        for c4 in range(4):
                            c = 4 * cg + c4
                            _pe_last = te.transpose(psb[:, c4 * 128:(c4 + 1) * 128],
                                         self.bap(2 * AQN + c * 128, [[1, 128]]),
                                         self.C_idb.ap())
                        _pe_last.then_inc(self.inc("PE", 1), 1)

            @blk.scalar
            def _(sc):
                for i in range(NQT):
                    for cg in range(4):
                        idx = 4 * i + cg
                        sc.wait_ge(self.sems["PE"], pe0 + idx + 1)
                        psb = (self.PS_C if idx % 2 == 0 else self.PS_D).ap().bitcast(BF16)
                        dst = self.bap(2 * AQ1T + (4 * cg) * 1024 + i * 128,
                                       [[1024, 4], [1, 128]])
                        sc.copy(dst, psb[:, 0:512].rearrange("p (a c) -> p a c", a=4)
                                ).then_inc(self.inc("AE", 1), 1)

    def p5_gv(self):
        nc = self.nc
        AQ1T = 0
        TT0 = 8192
        WGB = 24576
        M1B = 40960
        NTB = 43008
        SGB = 45056
        HMB = 46080
        self.fence_block("f_p5")
        GSEM = ["L0", "L1", "L2", "L3", "X0", "X1", "S0", "S1"]
        c00 = dict(self.cnt)
        for b in range(17):
            tern = b <= 15
            mm = b >= 1
            tp = b % 2
            mp = (b - 1) % 2
            c0 = dict(self.cnt)
            pe_per = (1 if mm else 0) + (4 if tern else 0)
            ac_per = (1 if mm else 0) + (4 if tern else 0)
            dv_per = (1 if mm else 0) + (1 if tern else 0)

            def dv_after_tern(g):
                return c0["DV"] + dv_per * g + 1

            def dv_after_hm(k):
                return c0["DV"] + dv_per * k + (2 if tern else 1)

            def pe_after_mm(k):
                return c0["PE"] + pe_per * k + 1

            def pe_after_tg(k, cg):
                return c0["PE"] + pe_per * k + (1 if mm else 0) + cg + 1

            def ac_after_silu(k):
                return c0["AC"] + ac_per * k + 1

            def ac_after_evac(k, cg):
                return c0["AC"] + ac_per * k + (1 if mm else 0) + cg + 1

            with nc.Block(f"p5_{b}") as blk:
                @blk.sync
                def _(sy):
                    if b == 0:
                        for g in range(8):
                            part, ot = g // 4, g % 4
                            self.dma(sy, self.af(WGB + g * 2048, 2048),
                                     self.wgv.ap()[part * HID + ot * 128:
                                                   part * HID + (ot + 1) * 128, :],
                                     GSEM[g])
                    if b <= 14:
                        for g in range(8):
                            part, ot = g // 4, g % 4
                            rowbase = part * HID + (b + 1) * 512
                            sy.wait_ge(self.sems["DV"], dv_after_tern(g))
                            self.dma(sy, self.af(WGB + g * 2048, 2048),
                                     self.wgv.ap()[rowbase + ot * 128:
                                                   rowbase + (ot + 1) * 128, :],
                                     GSEM[g])

                @blk.vector
                def _(v):
                    for k in range(8):
                        if tern:
                            g = k
                            lp = GSEM[g]
                            v.wait_ge(self.sems[lp], c00[lp] + 16 * (b + 1))
                            if g >= 2:
                                v.wait_ge(self.sems["PE"], pe_after_tg(g - 2, 3))
                            w = self.af(WGB + g * 2048, 2048)
                            m1 = self.bap(2 * (M1B + (g % 2) * 1024), [[1, 2048]])
                            self.vbar(v, v.tensor_scalar(
                                m1, w,
                                self.V_sgv.ap()[:, 2:3], None,
                                ALU.is_lt, ALU.bypass))
                            v.scalar_tensor_tensor(
                                self.bap(2 * (NTB + (g % 2) * 1024), [[1, 2048]]),
                                w, self.V_sgv.ap()[:, 1:2], m1,
                                ALU.is_gt, ALU.subtract
                                ).then_inc(self.inc("DV", 1), 1)
                        if mm:
                            qt = k
                            v.wait_ge(self.sems["AC"], ac_after_silu(qt))
                            gpar = f"G{qt % 2}"
                            v.wait_ge(self.sems[gpar], c0[gpar] + 16 * (qt // 2))
                            hm = self.af(HMB + (qt % 2) * 512, 512)
                            psv = self.PS_SC.ap()[:, (qt % 2) * 1024:(qt % 2) * 1024 + 512]
                            self.vbar(v, v.scalar_tensor_tensor(
                                hm, psv,
                                self.C1c.ap()[:, qt:qt + 1],
                                self.af(SGB + (qt % 2) * 512, 512), ALU.mult, ALU.mult))
                            self.vbar(v, v.tensor_reduce(
                                self.TMP.ap()[:, 4:5], hm, AX.X, ALU.max,
                                apply_absolute_value=True))
                            v.tensor_tensor(self.RMX.ap()[:, qt:qt + 1],
                                            self.RMX.ap()[:, qt:qt + 1],
                                            self.TMP.ap()[:, 4:5],
                                            ALU.max).then_inc(self.inc("DV", 1), 1)

                @blk.tensor
                def _(te):
                    for k in range(8):
                        if mm:
                            qt = k
                            if qt >= 2:
                                te.wait_ge(self.sems["DV"], dv_after_hm(qt - 2))
                            psv = self.PS_SC.ap()[:, (qt % 2) * 1024:(qt % 2) * 1024 + 512]
                            psg = self.PS_SC.ap()[:, (qt % 2) * 1024 + 512:
                                                  (qt % 2) * 1024 + 1024]
                            for c in range(16):
                                te.matmul(psv,
                                          self.bap(2 * AQ1T + c * 1024 + qt * 128,
                                                   [[1, 128]]),
                                          self.bap(2 * (TT0 + mp * 8192) + c * 512,
                                                   [[1, 512]]),
                                          start=(c == 0), stop=(c == 15))
                                _l = te.matmul(psg,
                                               self.bap(2 * AQ1T + c * 1024 + qt * 128,
                                                        [[1, 128]]),
                                               self.bap(2 * (TT0 + mp * 8192 + 4096)
                                                        + c * 512, [[1, 512]]),
                                               start=(c == 0), stop=(c == 15))
                            _l.then_inc(self.inc("PE", 1), 1)
                        if tern:
                            g = k
                            te.wait_ge(self.sems["DV"], dv_after_tern(g))
                            part, ot = g // 4, g % 4
                            base = TT0 + tp * 8192 + (0 if part == 0 else 4096)
                            for cg in range(4):
                                tid = 8 * b + g
                                if b * 32 + g * 4 + cg >= 2:
                                    kp, cgp = divmod(g * 4 + cg - 2, 4)
                                    if g * 4 + cg >= 2:
                                        te.wait_ge(self.sems["AC"],
                                                   ac_after_evac(kp, cgp))
                                psb = (self.PS_C if (g * 4 + cg) % 2 == 0
                                       else self.PS_D).ap().bitcast(BF16)
                                for c4 in range(4):
                                    c = 4 * cg + c4
                                    _t = te.transpose(
                                        psb[:, c4 * 128:(c4 + 1) * 128],
                                        self.bap(2 * (NTB + (g % 2) * 1024) + c * 128,
                                                 [[1, 128]]),
                                        self.C_idb.ap())
                                _t.then_inc(self.inc("PE", 1), 1)

                @blk.scalar
                def _(sc):
                    for k in range(8):
                        if mm:
                            qt = k
                            sc.wait_ge(self.sems["PE"], pe_after_mm(qt))
                            psg = self.PS_SC.ap()[:, (qt % 2) * 1024 + 512:
                                                  (qt % 2) * 1024 + 1024]
                            sc.activation(self.af(SGB + (qt % 2) * 512, 512), psg,
                                          ACT.Silu, bias=0.0,
                                          scale=self.C1c.ap()[:, qt:qt + 1]
                                          ).then_inc(self.inc("AC", 1), 1)
                        if tern:
                            g = k
                            part, ot = g // 4, g % 4
                            base = TT0 + tp * 8192 + (0 if part == 0 else 4096)
                            for cg in range(4):
                                sc.wait_ge(self.sems["PE"], pe_after_tg(g, cg))
                                psb = (self.PS_C if (g * 4 + cg) % 2 == 0
                                       else self.PS_D).ap().bitcast(BF16)
                                dst = self.bap(2 * base + (4 * cg) * 512 + ot * 128,
                                               [[512, 4], [1, 128]])
                                sc.copy(dst, psb[:, 0:512]
                                        .rearrange("p (a c) -> p a c", a=4)
                                        ).then_inc(self.inc("AC", 1), 1)

                if mm:
                    @blk.gpsimd
                    def _(gp):
                        for qt in range(8):
                            gp.wait_ge(self.sems["DV"], dv_after_hm(qt))
                            self.dma(gp, self.HMID.ap()[qt * 128:(qt + 1) * 128,
                                                        (b - 1) * 512:b * 512],
                                     self.af(HMB + (qt % 2) * 512, 512),
                                     f"G{qt % 2}")

    def p6_tout(self):
        nc = self.nc
        WOB, M1C, NT5, TSB = 0, 12288, 14336, 16384
        LQ = ["L0", "L1", "L2", "L3", "X0", "X1"]
        SQ = ["S0", "S1", "LD"]
        self.fence_block("f_p6")
        c0 = dict(self.cnt)
        DVn = lambda ci: c0["DV"] + ci + 1
        PEn = lambda g: c0["PE"] + g + 1
        ACn = lambda g: c0["AC"] + g + 1
        store_cnt = {}
        qn = {q: 0 for q in SQ}
        for bt in range(128):
            q = SQ[bt % 3]
            qn[q] += 1
            store_cnt[bt] = (q, qn[q])
        with nc.Block("p6") as blk:
            @blk.sync
            def _(sy):
                for ci in range(64):
                    if ci >= 6:
                        sy.wait_ge(self.sems["DV"], DVn(ci - 6))
                    rt, ch = divmod(ci, 4)
                    self.dma(sy, self.af(WOB + (ci % 6) * 2048, 2048),
                             self.wout.ap()[rt * 128:(rt + 1) * 128,
                                            ch * 2048:(ch + 1) * 2048],
                             LQ[ci % 6])

            @blk.vector
            def _(v):
                mk = {}
                for s in range(65):
                    if s < 64:
                        q = LQ[s % 6]
                        v.wait_ge(self.sems[q], c0[q] + 16 * (s // 6 + 1))
                        if s >= 2:
                            v.wait_ge(self.sems["DV"], DVn(s - 2))
                        mk[s] = self.vmark(v.tensor_scalar(
                            self.bap(2 * (M1C + (s % 2) * 1024), [[1, 2048]]),
                            self.af(WOB + (s % 6) * 2048, 2048),
                            self.V_sout.ap()[:, 2:3], None,
                            ALU.is_lt, ALU.bypass))
                    if s >= 1:
                        ci = s - 1
                        if ci >= 2:
                            v.wait_ge(self.sems["PE"], PEn(4 * (ci - 2) + 3))
                        v.wait_ge(self.sems["VH"], mk[ci])
                        v.scalar_tensor_tensor(
                            self.bap(2 * (NT5 + (ci % 2) * 1024), [[1, 2048]]),
                            self.af(WOB + (ci % 6) * 2048, 2048),
                            self.V_sout.ap()[:, 1:2],
                            self.bap(2 * (M1C + (ci % 2) * 1024), [[1, 2048]]),
                            ALU.is_gt, ALU.subtract).then_inc(self.inc("DV", 1), 1)

            @blk.tensor
            def _(te):
                for ci in range(64):
                    te.wait_ge(self.sems["DV"], DVn(ci))
                    for u in range(4):
                        g = 4 * ci + u
                        if g >= 2:
                            te.wait_ge(self.sems["AC"], ACn(g - 2))
                        psb = (self.PS_C if g % 2 == 0
                               else self.PS_D).ap().bitcast(BF16)
                        for s in range(4):
                            tl = 4 * u + s
                            _l = te.transpose(
                                psb[:, s * 128:(s + 1) * 128],
                                self.bap(2 * (NT5 + (ci % 2) * 1024) + tl * 128,
                                         [[1, 128]]),
                                self.C_idb.ap())
                        _l.then_inc(self.inc("PE", 1), 1)

            @blk.scalar
            def _(sc):
                for ci in range(64):
                    rt, ch = divmod(ci, 4)
                    for u in range(4):
                        g = 4 * ci + u
                        bt = g // 2
                        sc.wait_ge(self.sems["PE"], PEn(g))
                        if u % 2 == 0 and bt >= 3:
                            q, n = store_cnt[bt - 3]
                            sc.wait_ge(self.sems[q], c0[q] + 16 * n)
                        psb = (self.PS_C if g % 2 == 0
                               else self.PS_D).ap().bitcast(BF16)
                        tsb = self.bap(2 * (TSB + (bt % 3) * 512 + (u % 2) * 256),
                                       [[1, 512]])
                        sc.copy(tsb, psb[:, 0:512]).then_inc(self.inc("AC", 1), 1)
                        if u % 2 == 1:
                            sc.wait_ge(self.sems["AC"], self.cnt["AC"])
                            oc0 = ch * 16 + 4 * (u - 1)
                            dst = bass.AP(self.TOUT, rt * 1048576 + oc0 * 16384,
                                          [[128, 128], [16384, 8], [1, 128]])
                            self.dma(sc, dst,
                                     self.bap(2 * (TSB + (bt % 3) * 512),
                                              [[128, 8], [1, 128]]),
                                     SQ[bt % 3])

    def p7_out2(self):
        nc = self.nc
        AQ2T = 0
        HM2 = 32768
        ZA, ZBC, ZCC = 34816, 35328, 35840
        AQ2N = 36352
        AQN = (36352, 43776)
        ZP7 = 40704
        TOB = 36608
        X1B7 = 37632
        XH = 38656
        YB = 39680
        gp_done = self.cnt["GP"]
        st_done = self.cnt["ST"]
        TSEM = ["H0", "H1", "S0", "S1"]
        TOB2 = 44032

        def tobo(s):
            return TOB + s * 256 if s < 4 else TOB2 + (s - 4) * 256

        for b in range(3):
            bld = b <= 1
            mm = b >= 1
            TS_Q = TSEM if b == 1 else TSEM + ["L0", "L1", "L2", "L3"]
            NS = len(TS_Q)
            self.fence_block(f"f_p7_{b}")
            c0 = dict(self.cnt)
            pe_prog, dv_prog, ac_prog = [], [], []
            if mm:
                k_iter = iter(range(64))
                for j in range(4):
                    for oc in range(64):
                        if bld and oc % 4 == 0:
                            try:
                                k = next(k_iter)
                                pe_prog.append(("T", k))
                            except StopIteration:
                                pass
                        pe_prog.append(("MM", j, oc))
            elif bld:
                for k in range(64):
                    pe_prog.append(("T", k))
            if bld and mm:
                for j in range(4):
                    for k in range(j * 16, (j + 1) * 16):
                        dv_prog.append(("rnd", k))
                    for qt in range(4):
                        dv_prog.append(("yb", j, qt))
            elif bld:
                for k in range(64):
                    dv_prog.append(("rnd", k))
            else:
                for j in range(4):
                    for qt in range(4):
                        dv_prog.append(("yb", j, qt))
            if bld:
                for k in range(64):
                    ac_prog.append(("evac", k))
            PEn = {op: c0["PE"] + i + 1 for i, op in enumerate(pe_prog)}
            DVn = {op: c0["DV"] + i + 1 for i, op in enumerate(dv_prog)}
            ACn = {op: c0["AC"] + i + 1 for i, op in enumerate(ac_prog)}

            with nc.Block(f"p7_{b}") as blk:
                @blk.sync
                def _(sy):
                    if b == 0:
                        sy.wait_ge(self.sems["GP"], gp_done)
                        sy.wait_ge(self.sems["ST"], st_done)

                    def hm2_load(k):
                        qt, oc16 = k // 16, k % 16
                        i = b * 4 + qt
                        if k >= 4:
                            sy.wait_ge(self.sems["DV"], DVn[("rnd", k - 4)])
                        self.dma(sy, self.af(HM2 + (k % 4) * 512, 512),
                                 self.HMID.ap()[i * 128:(i + 1) * 128,
                                                oc16 * 512:(oc16 + 1) * 512],
                                 f"L{k % 4}")

                    def tob_load(j, oc):
                        idx = j * 64 + oc
                        if idx >= NS:
                            sy.wait_ge(self.sems["PE"], PEn[("MM", (idx - NS) // 64,
                                                            (idx - NS) % 64)])
                        vsrc = bass.AP(self.TOUT, (j * 4) * 1048576 + oc * 16384,
                                       [[128, 128], [1048576, 4], [1, 128]])
                        self.dma(sy, self.bap(2 * tobo(idx % NS),
                                              [[128, 4], [1, 128]]),
                                 vsrc, TS_Q[idx % NS])

                    def x1_load(j, qt):
                        i = (b - 1) * 4 + qt
                        idx2 = j * 4 + qt
                        if idx2 >= 2:
                            sy.wait_ge(self.sems["DV"], DVn[("yb", (idx2 - 2) // 4,
                                                            (idx2 - 2) % 4)])
                        self.dma(sy, self.af(X1B7 + (idx2 % 2) * 512, 512),
                                 self.X1scr.ap()[i * 128:(i + 1) * 128,
                                                 j * 512:(j + 1) * 512],
                                 f"X{idx2 % 2}")

                    if bld and mm:
                        for j in range(4):
                            for kk in range(16):
                                hm2_load(j * 16 + kk)
                                for o4 in range(4):
                                    tob_load(j, 4 * kk + o4)
                            for qt in range(4):
                                x1_load(j, qt)
                    elif bld:
                        for k in range(64):
                            hm2_load(k)
                    else:
                        for j in range(4):
                            for oc in range(64):
                                tob_load(j, oc)
                            for qt in range(4):
                                x1_load(j, qt)

                @blk.vector
                def _(v):
                    rp_feed, rp_flush = self.rp_pipe(v, ZP7, 512)
                    for op in dv_prog:
                        if op[0] == "rnd":
                            k = op[1]
                            qt, oc16 = k // 16, k % 16
                            i = b * 4 + qt
                            if oc16 == 0:
                                self.vbar(v, v.tensor_scalar(
                                    self.TMP.ap()[:, 4:5], self.RMX.ap()[:, i:i + 1],
                                    1e-5, None, ALU.max, ALU.bypass))
                                v.reciprocal(self.TMP.ap()[:, 6:7], self.TMP.ap()[:, 4:5])
                                self.vbar(v, v.tensor_tensor(
                                    self.TMP.ap()[:, 5:6], self.V_sout.ap()[:, 0:1],
                                    self.TMP.ap()[:, 4:5], ALU.mult))
                                v.tensor_scalar(self.SC2.ap()[:, i:i + 1],
                                                self.TMP.ap()[:, 6:7],
                                                127.0, None, ALU.mult, ALU.bypass)
                                self.vbar(v, v.tensor_scalar(
                                    self.SO2.ap()[:, i:i + 1], self.TMP.ap()[:, 5:6],
                                    1.0 / 127.0, None, ALU.mult, ALU.bypass))
                            def pre(k=k):
                                lp = f"L{k % 4}"
                                v.wait_ge(self.sems[lp], c0[lp] + 16 * (k // 4 + 1))
                                if k >= 2:
                                    v.wait_ge(self.sems["PE"], PEn[("T", k - 2)])
                            hm = self.af(HM2 + (k % 4) * 512, 512)
                            rp_feed(pre, hm, self.SC2.ap()[:, i:i + 1],
                                    2 * AQN[k % 2])
                        else:
                            rp_flush()
                            _, j, qt = op
                            i = (b - 1) * 4 + qt
                            idx2 = j * 4 + qt
                            v.wait_ge(self.sems["PE"], PEn[("MM", j, 63)])
                            v.wait_ge(self.sems[f"X{idx2 % 2}"],
                                      c0[f"X{idx2 % 2}"] + 16 * (idx2 // 2 + 1))
                            ypar = f"Y{idx2 % 2}"
                            v.wait_ge(self.sems[ypar], c0[ypar] + 16 * (idx2 // 2))
                            xh = self.af(XH + (idx2 % 2) * 512, 512)
                            self.vbar(v, v.tensor_scalar(
                                xh, self.af(X1B7 + (idx2 % 2) * 512, 512),
                                self.ST1_a.ap()[:, i:i + 1],
                                self.ST1_b.ap()[:, i:i + 1], ALU.mult, ALU.add))
                            v.scalar_tensor_tensor(
                                self.af(YB + (idx2 % 2) * 512, 512),
                                self.PS_SC.ap()[:, qt * 512:(qt + 1) * 512],
                                self.SO2.ap()[:, i:i + 1], xh, ALU.mult,
                                ALU.add).then_inc(self.inc("DV", 1), 1)
                    rp_flush()

                @blk.tensor
                def _(te):
                    for op in pe_prog:
                        if op[0] == "T":
                            k = op[1]
                            qt = k // 16
                            qtg = b * 4 + qt
                            te.wait_ge(self.sems["DV"], DVn[("rnd", k)])
                            if k >= 2:
                                te.wait_ge(self.sems["AC"], ACn[("evac", k - 2)])
                            psb = (self.PS_C if k % 2 == 0
                                   else self.PS_D).ap().bitcast(BF16)
                            for sub in range(4):
                                _l = te.transpose(psb[:, sub * 128:(sub + 1) * 128],
                                                  self.bap(2 * AQN[k % 2] + sub * 128,
                                                           [[1, 128]]),
                                                  self.C_idb.ap())
                            _l.then_inc(self.inc("PE", 1), 1)
                        else:
                            _, j, oc = op
                            idx = j * 64 + oc
                            qtg0 = (b - 1) * 4
                            te.wait_ge(self.sems[TS_Q[idx % NS]],
                                       c0[TS_Q[idx % NS]] + 16 * (idx // NS + 1))
                            if oc == 0 and j > 0:
                                te.wait_ge(self.sems["DV"], DVn[("yb", j - 1, 3)])
                            for qt in range(4):
                                _l = te.matmul(
                                    self.PS_SC.ap()[:, qt * 512:(qt + 1) * 512],
                                    self.bap(2 * AQ2T + oc * 1024 + (qtg0 + qt) * 128,
                                             [[1, 128]]),
                                    self.bap(2 * tobo(idx % NS), [[1, 512]]),
                                    start=(oc == 0), stop=(oc == 63))
                            _l.then_inc(self.inc("PE", 1), 1)

                @blk.scalar
                def _(sc):
                    for op in ac_prog:
                        k = op[1]
                        qt, oc16 = k // 16, k % 16
                        qtg = b * 4 + qt
                        sc.wait_ge(self.sems["PE"], PEn[("T", k)])
                        psb = (self.PS_C if k % 2 == 0
                               else self.PS_D).ap().bitcast(BF16)
                        dst = self.bap(2 * AQ2T + (oc16 * 4) * 1024 + qtg * 128,
                                       [[1024, 4], [1, 128]])
                        sc.copy(dst, psb[:, 0:512]
                                .rearrange("p (a c) -> p a c", a=4)
                                ).then_inc(self.inc("AC", 1), 1)

                if mm:
                    @blk.gpsimd
                    def _(gp):
                        for j in range(4):
                            for qt in range(4):
                                i = (b - 1) * 4 + qt
                                idx2 = j * 4 + qt
                                gp.wait_ge(self.sems["DV"], DVn[("yb", j, qt)])
                                self.dma(gp, self.y.ap()[i * 128:(i + 1) * 128,
                                                         j * 512:(j + 1) * 512],
                                         self.af(YB + (idx2 % 2) * 512, 512),
                                         f"Y{idx2 % 2}")

    def p_final(self):
        nc = self.nc
        with nc.Block("fin") as blk:
            @blk.gpsimd
            def _(gp):
                for s in self.DMA_SEMS:
                    if self.cnt[s]:
                        gp.wait_ge(self.sems[s], self.cnt[s])


def _host_inputs(core, X, ln_g, ln_b, Wq, bq, Wk, bk, Wv, bv, Wo, bo,
                 norm_w, w_gv, w_out):
    b, g = core // 2, core % 2
    f32 = np.float32
    rep = lambda v: np.ascontiguousarray(np.broadcast_to(np.asarray(v, f32), (128, D)))
    xb = np.ascontiguousarray(np.asarray(X, f32)[b])
    xqv = np.ascontiguousarray(xb.reshape(16, 128, D)[g::2].reshape(T, D))
    mask = np.zeros((NQT, 128, 256), f32)
    for i in range(NQT):
        qg = 256 * i + 128 * g + np.arange(128)[:, None]
        kg = 256 * i + np.arange(256)[None, :]
        mask[i] = np.where(kg <= qg, 0.0, NEGBIG)
    return {
        "x": xb, "xq": xqv,
        "wq": np.ascontiguousarray(np.asarray(Wq, f32)),
        "wk": np.ascontiguousarray(np.asarray(Wk, f32)),
        "wv": np.ascontiguousarray(np.asarray(Wv, f32)),
        "wo": np.ascontiguousarray(np.asarray(Wo, f32)),
        "bq_p": np.ascontiguousarray(np.asarray(bq, f32).reshape(H, 128).T),
        "bk_p": np.ascontiguousarray(np.asarray(bk, f32).reshape(H, 128).T),
        "bv_rep": rep(bv), "bo_rep": rep(bo), "g_rep": rep(ln_g),
        "b_rep": rep(ln_b), "nw_rep": rep(norm_w),
        "wgv": np.ascontiguousarray(np.asarray(w_gv, f32)),
        "wout": np.ascontiguousarray(np.asarray(w_out, f32)),
        "maskq": mask,
        "ident": np.eye(128, dtype=f32),
        "identb": np.eye(128, dtype=ml_dtypes.bfloat16),
    }


def kernel(L0_attn_X, L0_attn_ln_g, L0_attn_ln_b, L0_attn_Wq, L0_attn_bq,
           L0_attn_Wk, L0_attn_bk, L0_attn_Wv, L0_attn_bv, L0_attn_Wo,
           L0_attn_bo, L0_ffn_norm_w, L0_ffn_w_gv, L0_ffn_w_out):
    prog = Prog()
    nc = prog.build()
    in_maps = [
        _host_inputs(c, L0_attn_X, L0_attn_ln_g, L0_attn_ln_b, L0_attn_Wq,
                     L0_attn_bq, L0_attn_Wk, L0_attn_bk, L0_attn_Wv, L0_attn_bv,
                     L0_attn_Wo, L0_attn_bo, L0_ffn_norm_w, L0_ffn_w_gv,
                     L0_ffn_w_out)
        for c in range(8)
    ]
    res = run_bass_kernel_spmd(nc, in_maps, core_ids=list(range(8)))
    out = np.empty((4, S, D), np.float32)
    for c in range(8):
        b, g = c // 2, c % 2
        out[b].reshape(16, 128, D)[g::2] = res.results[c]["y"].reshape(NQT, 128, D)
    return out


# version 25 (final)
# speedup vs baseline: 1.0349x; 1.0261x over previous
import math
import numpy as np
import ml_dtypes
import concourse.bass as bass
import concourse.mybir as mybir
from concourse.bass_utils import run_bass_kernel_spmd

F32 = mybir.dt.float32
BF16 = mybir.dt.bfloat16
I32 = mybir.dt.int32
ACT = mybir.ActivationFunctionType
ALU = mybir.AluOpType
AX = mybir.AxisListType

S = 2048
D = 2048
H = 16
DH = 128
T = 1024
NQT = 8
HID = 8192
O2 = 16384
EPS = 1e-5
ISQ = 1.0 / math.sqrt(128.0)
NEGBIG = -1.0e38
STAGE_ALL = 99

OF_G, OF_B, OF_BV, OF_BO, OF_NW = 38912, 40960, 43008, 45056, 47104


class Prog:
    def __init__(self, stage=STAGE_ALL, dbg=False):
        self.stage = stage
        self.dbg = dbg
        nc = bass.Bass("TRN2", target_bir_lowering=False)
        self.nc = nc
        dram = lambda n, sh, dt=F32, kind="ExternalInput": nc.dram_tensor(n, sh, dt, kind=kind)
        skind = "ExternalOutput" if dbg else "Internal"

        self.x = dram("x", [S, D])
        self.xq = dram("xq", [T, D])
        self.wq = dram("wq", [D, D])
        self.wk = dram("wk", [D, D])
        self.wv = dram("wv", [D, D])
        self.wo = dram("wo", [D, D])
        self.bq_p = dram("bq_p", [128, H])
        self.bk_p = dram("bk_p", [128, H])
        self.bv_rep = dram("bv_rep", [128, D])
        self.bo_rep = dram("bo_rep", [128, D])
        self.g_rep = dram("g_rep", [128, D])
        self.b_rep = dram("b_rep", [128, D])
        self.nw_rep = dram("nw_rep", [128, D])
        self.wgv = dram("wgv", [O2, D])
        self.wout = dram("wout", [D, HID])
        self.maskq = dram("maskq", [NQT, 128, 256])
        self.ident = dram("ident", [128, 128])
        self.identb = dram("identb", [128, 128], BF16)
        self.y = dram("y", [T, D], kind="ExternalOutput")

        self.Kscr = dram("Kscr", [H, DH, S], kind=skind)
        self.Vscr = dram("Vscr", [H, S, DH], kind=skind)
        self.Qscr = dram("Qscr", [H, DH, T], kind=skind)
        self.X1scr = dram("X1scr", [T, D], kind=skind)
        self.HMID = dram("HMID", [T, HID], kind=skind)
        self.TOUT = dram("TOUT", [16, 64, 128, 128], BF16, kind="Internal")
        if dbg:
            self.dbg_s = dram("dbg_s", [128, 16], kind="ExternalOutput")

        self.AF = nc.alloc_sbuf_tensor("AF", [128, 49152], F32)
        fx = lambda n, cols, dt=F32: nc.alloc_sbuf_tensor(n, [128, cols], dt)
        self.C_bq = fx("C_bq", H)
        self.C_bqs = fx("C_bqs", H)
        self.C_bk = fx("C_bk", H)
        self.C_id = fx("C_id", 128)
        self.C_idb = fx("C_idb", 128, BF16)
        self.C_ones = fx("C_ones", 2)
        self.C_127 = fx("C_127", 1)
        self.V_sgv = fx("V_sgv", 4)
        self.V_sout = fx("V_sout", 4)
        self.ST_mu = fx("ST_mu", 16)
        self.ST_R = fx("ST_R", 16)
        self.STq_mu = fx("STq_mu", NQT)
        self.STq_R = fx("STq_R", NQT)
        self.ST1_mu = fx("ST1_mu", NQT)
        self.ST1_r1 = fx("ST1_r1", NQT)
        self.ST1_al = fx("ST1_al", NQT)
        self.ST1_a = fx("ST1_a", NQT)
        self.ST1_b = fx("ST1_b", NQT)
        self.C1c = fx("C1c", NQT)
        self.RMX = fx("RMX", NQT)
        self.SC2 = fx("SC2", NQT)
        self.SO2 = fx("SO2", NQT)
        self.SE = fx("SE", 4)
        self.TMP = fx("TMP", 16)
        self.BN6 = fx("BN6", 96)
        self.BN2 = fx("BN2", 4)
        self.PARTS = fx("PARTS", 392)

        self.PS_SC = nc.alloc_psum_tensor("PS_SC", [128, 2048], F32)
        self.PS_A = nc.alloc_psum_tensor("PS_A", [128, 512], F32)
        self.PS_B = nc.alloc_psum_tensor("PS_B", [128, 512], F32)
        self.PS_C = nc.alloc_psum_tensor("PS_C", [128, 512], F32)
        self.PS_D = nc.alloc_psum_tensor("PS_D", [128, 512], F32)

        self.sems = {}
        self.cnt = {}
        self._sem_cms = []
        for name in ["LD", "CR", "DV", "AC", "AE", "PE", "GP", "ST", "Y", "VH",
                     "L0", "L1", "L2", "L3", "X0", "X1", "G0", "G1",
                     "S0", "S1", "Y0", "Y1", "H0", "H1"]:
            cm = nc.semaphore(name)
            self.sems[name] = cm.__enter__()
            self._sem_cms.append(cm)
            self.cnt[name] = 0

    def af(self, off, cols):
        return self.AF.ap()[:, off:off + cols]

    def afb(self, off_f32, cols_bf16):
        return self.AF.ap().bitcast(BF16)[:, 2 * off_f32: 2 * off_f32 + cols_bf16]

    def af3(self, off, stride, n, inner):
        return bass.AP(self.AF, off, [[49152, 128], [stride, n], [1, inner]])

    def bap(self, off_b, dims_b):
        f32dims = [[s // 2, n] for (s, n) in dims_b[:-1]] + \
                  [[1, dims_b[-1][1] // 2]]
        return bass.AP(self.AF, off_b // 2,
                       [[49152, 128]] + f32dims).bitcast(BF16)

    DMA_SEMS = ["LD", "GP", "ST", "Y", "L0", "L1", "L2", "L3", "X0", "X1",
                "G0", "G1", "S0", "S1", "Y0", "Y1", "H0", "H1"]

    def fence_block(self, name="fence"):
        with self.nc.Block(name) as blk:
            @blk.sync
            def _(sy):
                for s in self.DMA_SEMS + ["DV", "AC", "AE", "PE", "CR", "VH"]:
                    if self.cnt[s]:
                        sy.wait_ge(self.sems[s], self.cnt[s])

    def inc(self, name, n=16):
        self.cnt[name] += n
        return self.sems[name]

    def dma(self, eng, dst, src, sem="LD"):
        eng.dma_start(dst, src).then_inc(self.inc(sem, 16), 16)

    def vmark(self, inst):
        inst.then_inc(self.inc("VH", 1), 1)
        return self.cnt["VH"]

    def vbar(self, v, inst):
        v.wait_ge(self.sems["VH"], self.vmark(inst))

    P1B = 31744
    P1QS = ["G0", "G1", "GP"]

    def p1_buf(self, k):
        return self.af(self.P1B + (k % 3) * 2048, 2048)

    def p1_chunk_src(self, k):
        if k < 128:
            return self.wgv.ap()[k * 128:(k + 1) * 128, :]
        r, cc = divmod(k - 128, 4)
        return self.wout.ap()[r * 128:(r + 1) * 128, cc * 2048:(cc + 1) * 2048]

    def p1_load_sw(self, gp, k):
        if k >= 3:
            gp.wait_ge(self.sems["CR"], self._p1cr0 + k - 2)
        self.dma(gp, self.p1_buf(k), self.p1_chunk_src(k), self.P1QS[k % 3])

    def p1_reduce_sw(self, sc):
        k = self._p1k_sw
        self._p1k_sw += 1
        q = self.P1QS[k % 3]
        sc.wait_ge(self.sems[q], self._p1q0[q] + 16 * (k // 3 + 1))
        buf = self.p1_buf(k)
        sc.activation(buf, buf, ACT.Abs,
                      accum_out=self.PARTS.ap()[:, k:k + 1]
                      ).then_inc(self.inc("CR", 1), 1)

    def stats(self, v, off, cols=2048):
        nchunk = cols // 512
        last = None
        for cc in range(nchunk):
            last = v.bn_stats(self.BN6.ap()[:, cc * 6:(cc + 1) * 6],
                              self.af(off + cc * 512, 512))
        self.vbar(v, last)
        self.vbar(v, v.bn_aggr(self.BN2.ap()[:, 0:2], self.BN6.ap()[:, 0:6 * nchunk]))

    def rp_pipe(self, v, zb, cols):
        P = self
        st = {"pend": None, "p": 0}

        def zoff(p, j):
            return zb + p * 3 * cols + j * cols

        def s3(item):
            p, dst_b, m2 = item
            v.wait_ge(P.sems["VH"], m2)
            v.scalar_tensor_tensor(
                P.bap(dst_b, [[1, cols]]),
                P.af(zoff(p, 1), cols).bitcast(I32), -128.0,
                P.af(zoff(p, 2), cols), ALU.add,
                ALU.add).then_inc(P.inc("DV", 1), 1)

        def feed(pre, src, scale_col, dst_b):
            prev = st["pend"]
            if prev:
                s3(prev)
            p = st["p"]
            st["p"] ^= 1
            pre()
            m0 = P.vmark(v.tensor_scalar(P.af(zoff(p, 0), cols), src, scale_col,
                                         128.0, ALU.mult, ALU.add))
            v.wait_ge(P.sems["VH"], m0)
            m1 = P.vmark(v.tensor_copy(P.af(zoff(p, 1), cols).bitcast(I32),
                                       P.af(zoff(p, 0), cols)))
            v.wait_ge(P.sems["VH"], m1)
            m2 = P.vmark(v.scalar_tensor_tensor(
                P.af(zoff(p, 2), cols), P.af(zoff(p, 0), cols), -0.5,
                P.af(zoff(p, 1), cols).bitcast(I32), ALU.add, ALU.is_gt))
            st["pend"] = (p, dst_b, m2)

        def flush():
            prev = st["pend"]
            if prev is not None:
                s3(prev)
            st["pend"] = None
        return feed, flush

    def round_pos(self, v, src, scale_col, z0, z1, z2, dst_b, cols):
        self.vbar(v, v.tensor_scalar(self.af(z0, cols), src, scale_col, 128.0,
                                     ALU.mult, ALU.add))
        self.vbar(v, v.tensor_copy(self.af(z1, cols).bitcast(I32),
                                   self.af(z0, cols)))
        self.vbar(v, v.scalar_tensor_tensor(self.af(z2, cols),
                                            self.af(z0, cols), -0.5,
                                            self.af(z1, cols).bitcast(I32),
                                            ALU.add, ALU.is_gt))
        return v.scalar_tensor_tensor(self.bap(dst_b, [[1, cols]]),
                                      self.af(z1, cols).bitcast(I32), -128.0,
                                      self.af(z2, cols), ALU.add, ALU.add)

    def build(self):
        self.p0_consts()
        if self.stage == 1:
            self.p1_absum()
        if self.stage >= 2:
            self.p2_proj(True)
        if self.stage >= 3:
            self.p2_proj(False)
        if self.stage >= 4:
            self.p3_attn()
        if self.stage >= 5:
            self.p4_ffn_a()
        if self.stage >= 6:
            self.p5_gv()
        if self.stage >= 7:
            self.p6_tout()
            self.p7_out2()
        self.p_final()
        for cm in reversed(self._sem_cms):
            cm.__exit__(None, None, None)
        return self.nc

    def p0_consts(self):
        nc = self.nc
        with nc.Block("p0") as blk:
            @blk.sync
            def _(sy):
                self.dma(sy, self.C_bq.ap(), self.bq_p.ap())
                self.dma(sy, self.C_bk.ap(), self.bk_p.ap())
                self.dma(sy, self.C_id.ap(), self.ident.ap())
                self.dma(sy, self.C_idb.ap(), self.identb.ap())
                self.dma(sy, self.af(OF_G, D), self.g_rep.ap())
                self.dma(sy, self.af(OF_B, D), self.b_rep.ap())
                self.dma(sy, self.af(OF_BV, D), self.bv_rep.ap())
                self.dma(sy, self.af(OF_BO, D), self.bo_rep.ap())
                self.dma(sy, self.af(OF_NW, D), self.nw_rep.ap())

            @blk.vector
            def _(v):
                v.memset(self.C_ones.ap()[:, 0:1], 1.0)
                v.memset(self.C_127.ap(), 127.0)
                v.memset(self.RMX.ap(), 0.0)
                v.memset(self.TMP.ap(), 0.0)
                v.memset(self.SE.ap(), 0.0)

            @blk.gpsimd
            def _(gp):
                gp.wait_ge(self.sems["LD"], self.cnt["LD"])

    def p1_absum(self):
        nc = self.nc
        tiles = []
        for r in range(O2 // 128):
            tiles.append((self.wgv.ap()[r * 128:(r + 1) * 128, :], 0))
        for r in range(D // 128):
            for cc in range(4):
                tiles.append((self.wout.ap()[r * 128:(r + 1) * 128,
                                             cc * 2048:(cc + 1) * 2048], 1))
        NB = 4
        self.fence_block("f_p1")
        cr0 = self.cnt["CR"]
        c0 = dict(self.cnt)
        with nc.Block("p1") as blk:
            @blk.sync
            def _(sy):
                for t, (src, _) in enumerate(tiles):
                    if t >= NB:
                        sy.wait_ge(self.sems["CR"], cr0 + (t - NB + 1))
                    self.dma(sy, self.af((t % NB) * 2048, 2048), src,
                             f"L{t % 4}")

            PARTS = 4 * 2048

            @blk.vector
            def _(v):
                for t, (_, j) in enumerate(tiles):
                    v.wait_ge(self.sems[f"L{t % 4}"],
                              c0[f"L{t % 4}"] + 16 * (t // 4 + 1))
                    v.tensor_reduce(self.af(PARTS + t, 1), self.af((t % NB) * 2048, 2048),
                                    AX.X, ALU.add,
                                    apply_absolute_value=True).then_inc(self.inc("CR", 1), 1)
                v.wait_ge(self.sems["CR"], cr0 + len(tiles))
                self.vbar(v, v.tensor_reduce(self.TMP.ap()[:, 0:1],
                                             self.af(PARTS, 128), AX.X, ALU.add))
                self.vbar(v, v.tensor_reduce(self.TMP.ap()[:, 1:2],
                                             self.af(PARTS + 128, 64), AX.X, ALU.add))
                v.nop().then_inc(self.inc("CR", 1), 1)

            @blk.tensor
            def _(te):
                te.wait_ge(self.sems["CR"], cr0 + len(tiles) + 1)
                te.matmul(self.PS_A.ap()[0:1, 0:1], self.TMP.ap()[:, 0:1],
                          self.C_ones.ap()[:, 0:1], start=True, stop=True)
                te.matmul(self.PS_B.ap()[0:1, 0:1], self.TMP.ap()[:, 1:2],
                          self.C_ones.ap()[:, 0:1], start=True, stop=True).then_inc(self.inc("PE", 1), 1)

            @blk.scalar
            def _(sc):
                sc.wait_ge(self.sems["PE"], self.cnt["PE"])
                sc.copy(self.TMP.ap()[0:1, 8:9], self.PS_A.ap()[0:1, 0:1])
                sc.copy(self.TMP.ap()[0:1, 9:10], self.PS_B.ap()[0:1, 0:1]).then_inc(self.inc("AC", 1), 1)

            @blk.tensor
            def _(te):
                te.wait_ge(self.sems["AC"], self.cnt["AC"])
                te.matmul(self.PS_C.ap()[:, 0:2],
                          self.C_ones.ap()[0:1, 0:1].to_broadcast((1, 128)),
                          self.TMP.ap()[0:1, 8:10], start=True, stop=True).then_inc(self.inc("PE", 1), 1)

            @blk.vector
            def _(v):
                v.wait_ge(self.sems["PE"], self.cnt["PE"])
                v.tensor_scalar(self.V_sgv.ap()[:, 0:1], self.PS_C.ap()[:, 0:1],
                                1.0 / (O2 * D), 1e-5, ALU.mult, ALU.max)
                self.vbar(v, v.tensor_scalar(self.V_sout.ap()[:, 0:1],
                                             self.PS_C.ap()[:, 1:2],
                                             1.0 / (D * HID), 1e-5, ALU.mult, ALU.max))
                insts = []
                for vv in (self.V_sgv, self.V_sout):
                    insts.append(v.tensor_scalar(vv.ap()[:, 1:2], vv.ap()[:, 0:1],
                                                 0.5, None, ALU.mult, ALU.bypass))
                    insts.append(v.tensor_scalar(vv.ap()[:, 2:3], vv.ap()[:, 0:1],
                                                 -0.5, None, ALU.mult, ALU.bypass))
                insts[-1].then_inc(self.inc("DV", 1), 1)

            if self.dbg:
                @blk.scalar
                def _(sc):
                    sc.wait_ge(self.sems["DV"], self.cnt["DV"])
                    sc.copy(self.TMP.ap()[:, 12:13], self.V_sgv.ap()[:, 0:1])
                    sc.copy(self.TMP.ap()[:, 13:14], self.V_sout.ap()[:, 0:1]).then_inc(self.inc("AC", 1), 1)
                    sc.wait_ge(self.sems["AC"], self.cnt["AC"])

                @blk.gpsimd
                def _(gp):
                    gp.wait_ge(self.sems["AC"], self.cnt["AC"])
                    self.dma(gp, self.dbg_s.ap(), self.TMP.ap()[:, 0:16], "GP")

    def p2_proj(self, is_q):
        nc = self.nc
        ntile = NQT if is_q else 16
        ncols = ntile * 128
        xsrc = self.xq if is_q else self.x
        mu_st, R_st = (self.STq_mu, self.STq_R) if is_q else (self.ST_mu, self.ST_R)
        XNTH = 0
        XNTM = 8 * ncols
        XB = 16 * ncols
        XBH = XB + 2048
        XBM = XBH + 1024
        tag = "q" if is_q else "kv"

        XB0 = (16384, 18432) if is_q else (32768, 34816)
        XBH = 20480 if is_q else 36864
        XBM = 21504 if is_q else 37888
        self.fence_block(f"f_p2{tag}x")
        if is_q:
            self._p1k_sw = 0
            self._p1cr0 = self.cnt["CR"]
            self._p1q0 = dict(self.cnt)
        ld0, dv0, ac0, ae0, pe0 = (self.cnt[k] for k in ["LD", "DV", "AC", "AE", "PE"])
        cx0 = dict(self.cnt)
        d_split = lambda t: dv0 + 2 * t + 2
        a_sqrt = lambda t: ac0 + (1 if t == 0 else 9 * t - 7)
        a_evac = lambda t, r: ac0 + 9 * t + 3 + r
        with nc.Block(f"p2{tag}_xn") as blk:
            @blk.sync
            def _(sy):
                for t in range(ntile):
                    if t >= 2:
                        sy.wait_ge(self.sems["DV"], d_split(t - 2))
                    self.dma(sy, self.af(XB0[t % 2], 2048),
                             xsrc.ap()[t * 128:(t + 1) * 128, :], f"L{t % 2}")
                    if is_q and t < 2:
                        for c in range(16):
                            self.dma(sy, self.af((22528, 24576)[t] + c * 128, 128),
                                     self.wq.ap()[c * 128:(c + 1) * 128,
                                                  t * 128:(t + 1) * 128],
                                     f"L{2 + t}")

            if is_q:
                @blk.gpsimd
                def _(gp):
                    for k in range(27):
                        self.p1_load_sw(gp, k)

            @blk.vector
            def _(v):
                if is_q:
                    v.tensor_scalar(self.C_bqs.ap(), self.C_bq.ap(), ISQ, None,
                                    ALU.mult, ALU.bypass)
                m = {}
                for slot in range(ntile + 1):
                    tn, to = slot, slot - 1
                    pn, po = tn % 2, to % 2
                    if to >= 0:
                        v.wait_ge(self.sems["VH"], m["st", to])
                        self.vbar(v, v.bn_aggr(self.BN2.ap()[:, 2 * po:2 * po + 2],
                                               self.BN6.ap()[:, 24 * po:24 * po + 24]))
                        m["mu", to] = self.vmark(v.tensor_copy(
                            mu_st.ap()[:, to:to + 1], self.BN2.ap()[:, 2 * po:2 * po + 1]))
                        v.tensor_scalar(self.TMP.ap()[:, 4 + po:5 + po],
                                        self.BN2.ap()[:, 2 * po + 1:2 * po + 2],
                                        1.0 + EPS, EPS * EPS, ALU.mult,
                                        ALU.add).then_inc(self.inc("DV", 1), 1)
                    if tn < ntile:
                        v.wait_ge(self.sems[f"L{tn % 2}"],
                                  cx0[f"L{tn % 2}"] + 16 * (tn // 2 + 1))
                        v.bn_stats(self.BN6.ap()[:, 24 * pn:24 * pn + 6],
                                   self.af(XB0[pn], 512))
                    if to >= 0:
                        v.wait_ge(self.sems["AC"], a_sqrt(to))
                        m["rc", to] = self.vmark(v.reciprocal(
                            R_st.ap()[:, to:to + 1], self.TMP.ap()[:, 6 + po:7 + po]))
                    if tn < ntile:
                        v.bn_stats(self.BN6.ap()[:, 24 * pn + 6:24 * pn + 12],
                                   self.af(XB0[pn] + 512, 512))
                    if to >= 0:
                        v.wait_ge(self.sems["VH"], m["mu", to])
                        m["s1", to] = self.vmark(v.scalar_tensor_tensor(
                            self.af(XB0[po], 2048), self.af(XB0[po], 2048),
                            mu_st.ap()[:, to:to + 1], self.af(OF_G, D),
                            ALU.subtract, ALU.mult))
                    if tn < ntile:
                        v.bn_stats(self.BN6.ap()[:, 24 * pn + 12:24 * pn + 18],
                                   self.af(XB0[pn] + 1024, 512))
                    if to >= 0:
                        v.wait_ge(self.sems["VH"], m["s1", to])
                        m["s2", to] = self.vmark(v.scalar_tensor_tensor(
                            self.af(XB0[po], 2048), self.af(XB0[po], 2048),
                            R_st.ap()[:, to:to + 1], self.af(OF_B, D),
                            ALU.mult, ALU.add))
                    if tn < ntile:
                        m["st", tn] = self.vmark(v.bn_stats(
                            self.BN6.ap()[:, 24 * pn + 18:24 * pn + 24],
                            self.af(XB0[pn] + 1536, 512)))
                    if to >= 0:
                        if to >= 1:
                            v.wait_ge(self.sems["PE"], pe0 + 8 * to)
                        v.wait_ge(self.sems["VH"], m["s2", to])
                        m["ha", to] = self.vmark(v.tensor_copy(
                            self.bap(2 * XBH, [[1, 1024]]), self.af(XB0[po], 1024)))
                        m["hb", to] = self.vmark(v.tensor_copy(
                            self.bap(2 * XBH + 1024, [[1, 1024]]),
                            self.af(XB0[po] + 1024, 1024)))
                        v.wait_ge(self.sems["VH"], m["ha", to])
                        v.scalar_tensor_tensor(self.bap(2 * XBM, [[1, 1024]]),
                                               self.af(XB0[po], 1024), 0.0,
                                               self.bap(2 * XBH, [[1, 1024]]),
                                               ALU.add, ALU.subtract)
                        v.wait_ge(self.sems["VH"], m["hb", to])
                        v.scalar_tensor_tensor(self.bap(2 * XBM + 1024, [[1, 1024]]),
                                               self.af(XB0[po] + 1024, 1024), 0.0,
                                               self.bap(2 * XBH + 1024, [[1, 1024]]),
                                               ALU.add, ALU.subtract
                                               ).then_inc(self.inc("DV", 1), 1)
                if not is_q:
                    v.wait_ge(self.sems["CR"], self._p1cr0 + 192)
                    v.tensor_reduce(self.TMP.ap()[:, 0:1], self.PARTS.ap()[:, 0:128],
                                    AX.X, ALU.add)
                    v.tensor_reduce(self.TMP.ap()[:, 1:2], self.PARTS.ap()[:, 128:192],
                                    AX.X, ALU.add).then_inc(self.inc("DV", 1), 1)
                    v.wait_ge(self.sems["PE"], pe0 + 8 * ntile + 2)
                    v.tensor_scalar(self.V_sgv.ap()[:, 0:1], self.PS_A.ap()[:, 4:5],
                                    1.0 / (O2 * D), 1e-5, ALU.mult, ALU.max)
                    self.vbar(v, v.tensor_scalar(self.V_sout.ap()[:, 0:1],
                                                 self.PS_A.ap()[:, 5:6],
                                                 1.0 / (D * HID), 1e-5, ALU.mult,
                                                 ALU.max))
                    insts = []
                    for vv in (self.V_sgv, self.V_sout):
                        insts.append(v.tensor_scalar(vv.ap()[:, 1:2], vv.ap()[:, 0:1],
                                                     0.5, None, ALU.mult, ALU.bypass))
                        insts.append(v.tensor_scalar(vv.ap()[:, 2:3], vv.ap()[:, 0:1],
                                                     -0.5, None, ALU.mult, ALU.bypass))
                    insts[-1].then_inc(self.inc("DV", 1), 1)

            @blk.tensor
            def _(te):
                for t in range(ntile):
                    te.wait_ge(self.sems["DV"], d_split(t))
                    for hm in range(2):
                        srcb = XBH if hm == 0 else XBM
                        for cg in range(4):
                            idx = 8 * t + 4 * hm + cg
                            if idx >= 2:
                                tp_, r = divmod(idx - 2, 8)
                                te.wait_ge(self.sems["AC"], a_evac(tp_, r))
                            psb = (self.PS_C if idx % 2 == 0
                                   else self.PS_D).ap().bitcast(BF16)
                            for c4 in range(4):
                                c = 4 * cg + c4
                                _t = te.transpose(psb[:, c4 * 128:(c4 + 1) * 128],
                                                  self.bap(2 * srcb + c * 128,
                                                           [[1, 128]]),
                                                  self.C_idb.ap())
                            _t.then_inc(self.inc("PE", 1), 1)
                if not is_q:
                    te.wait_ge(self.sems["DV"], dv0 + 2 * ntile + 1)
                    te.matmul(self.PS_A.ap()[0:1, 0:1], self.TMP.ap()[:, 0:1],
                              self.C_ones.ap()[:, 0:1], start=True, stop=True)
                    te.matmul(self.PS_B.ap()[0:1, 0:1], self.TMP.ap()[:, 1:2],
                              self.C_ones.ap()[:, 0:1], start=True,
                              stop=True).then_inc(self.inc("PE", 1), 1)
                    te.wait_ge(self.sems["AC"], ac0 + 9 * ntile + 1)
                    te.matmul(self.PS_A.ap()[:, 4:6],
                              self.C_ones.ap()[0:1, 0:1].to_broadcast((1, 128)),
                              self.TMP.ap()[0:1, 8:10], start=True,
                              stop=True).then_inc(self.inc("PE", 1), 1)

            @blk.scalar
            def _(sc):
                for t in range(ntile + 1):
                    if t < ntile:
                        sc.wait_ge(self.sems["DV"], dv0 + 2 * t + 1)
                        sc.activation(self.TMP.ap()[:, 6 + t % 2:7 + t % 2],
                                      self.TMP.ap()[:, 4 + t % 2:5 + t % 2],
                                      ACT.Sqrt).then_inc(self.inc("AC", 1), 1)
                    if t >= 1:
                        to = t - 1
                        for hm in range(2):
                            dstb = XNTH if hm == 0 else XNTM
                            for cg in range(4):
                                idx = 8 * to + 4 * hm + cg
                                sc.wait_ge(self.sems["PE"], pe0 + idx + 1)
                                psb = (self.PS_C if idx % 2 == 0
                                       else self.PS_D).ap().bitcast(BF16)
                                dst = self.bap(2 * dstb + (4 * cg) * ncols + to * 128,
                                               [[ncols, 4], [1, 128]])
                                sc.copy(dst, psb[:, 0:512]
                                        .rearrange("p (a c) -> p a c", a=4)
                                        ).then_inc(self.inc("AC", 1), 1)
                    if is_q and t < ntile:
                        for _ in range(3):
                            self.p1_reduce_sw(sc)
                if not is_q:
                    sc.wait_ge(self.sems["PE"], pe0 + 8 * ntile + 1)
                    sc.copy(self.TMP.ap()[0:1, 8:9], self.PS_A.ap()[0:1, 0:1])
                    sc.copy(self.TMP.ap()[0:1, 9:10],
                            self.PS_B.ap()[0:1, 0:1]).then_inc(self.inc("AC", 1), 1)

        ngrp = (T // 512) if is_q else (S // 512)
        wsrc = self.wq if is_q else self.wk
        bias = self.C_bqs if is_q else self.C_bk
        evsc = ISQ if is_q else 1.0
        dst_scr = self.Qscr if is_q else self.Kscr
        if is_q:
            WF = (22528, 24576)
            WH = (26624, 27648)
            WM = (28672, 29696)
            EVQ = 30720
        else:
            WF = (32768, 34816)
            WH = (36864, 37888)
            WM = (OF_G, OF_G + 1024)
            EVQ = OF_B
        self.fence_block(f"f_p2{tag}qk")
        ae0, pe0, dv0 = (self.cnt[k] for k in ["AE", "PE", "DV"])
        c0 = dict(self.cnt)
        with nc.Block(f"p2{tag}_qk") as blk:
            @blk.vector
            def _(v):
                for h in range(H):
                    if is_q and h < 2:
                        pass
                    else:
                        lp = f"L{h % 2}"
                        off = -256 if is_q else 0
                        v.wait_ge(self.sems[lp], c0[lp] + off + 16 * 16 * (h // 2 + 1))
                    if h >= 2:
                        v.wait_ge(self.sems["PE"], pe0 + (h - 1) * ngrp)
                    self.vbar(v, v.tensor_copy(self.bap(2 * WH[h % 2], [[1, 2048]]),
                                               self.af(WF[h % 2], 2048)))
                    v.scalar_tensor_tensor(self.bap(2 * WM[h % 2], [[1, 2048]]),
                                           self.af(WF[h % 2], 2048), 0.0,
                                           self.bap(2 * WH[h % 2], [[1, 2048]]),
                                           ALU.add, ALU.subtract
                                           ).then_inc(self.inc("DV", 1), 1)

            @blk.sync
            def _(sy):
                for h in range(H):
                    if is_q and h < 2:
                        continue
                    if h >= 2:
                        sy.wait_ge(self.sems["DV"], dv0 + h - 1)
                    for c in range(16):
                        self.dma(sy, self.af(WF[h % 2] + c * 128, 128),
                                 wsrc.ap()[c * 128:(c + 1) * 128, h * 128:(h + 1) * 128],
                                 f"L{h % 2}")

            if is_q:
                @blk.gpsimd
                def _(gp):
                    for k in range(27, 192):
                        self.p1_load_sw(gp, k)

            @blk.tensor
            def _(te):
                for h in range(H):
                    te.wait_ge(self.sems["DV"], dv0 + h + 1)
                    for grp in range(ngrp):
                        idx = h * ngrp + grp
                        if idx >= 2:
                            te.wait_ge(self.sems["AE"], ae0 + idx - 1)
                        ps = (self.PS_C if idx % 2 == 0 else self.PS_D).ap()
                        for c in range(16):
                            for (wb, xb) in ((WH, XNTH), (WH, XNTM), (WM, XNTH)):
                                _pe_last = te.matmul(
                                    ps,
                                    self.bap(2 * wb[h % 2] + c * 128, [[1, 128]]),
                                    self.bap(2 * xb + c * ncols + grp * 512,
                                             [[1, 512]]),
                                    start=(c == 0 and wb is WH and xb is XNTH),
                                    stop=(c == 15 and xb is XNTH and wb is WM))
                        _pe_last.then_inc(self.inc("PE", 1), 1)

            @blk.scalar
            def _(sc):
                for h in range(H):
                    for grp in range(ngrp):
                        idx = h * ngrp + grp
                        sc.wait_ge(self.sems["PE"], pe0 + idx + 1)
                        sp = f"S{idx % 2}"
                        if idx >= 2:
                            sc.wait_ge(self.sems[sp], c0[sp] + 16 * (idx // 2))
                        ps = (self.PS_C if idx % 2 == 0 else self.PS_D).ap()
                        ev = self.af(EVQ + (idx % 2) * 512, 512)
                        sc.activation(ev, ps, ACT.Identity, bias=bias.ap()[:, h:h + 1],
                                      scale=evsc).then_inc(self.inc("AE", 1), 1)
                        sc.wait_ge(self.sems["AE"], self.cnt["AE"])
                        self.dma(sc, dst_scr.ap()[h, :, grp * 512:(grp + 1) * 512],
                                 ev, sp)
                        if is_q:
                            quota = 24 + ((idx + 1) * 168 + 31) // 32
                            while self._p1k_sw < quota:
                                self.p1_reduce_sw(sc)

        if is_q:
            return

        WVF = 32768
        WVH = 33792
        WVM = 37888
        EV2 = 41984
        self.fence_block("f_p2v")
        dv0, pe0 = (self.cnt[k] for k in ["DV", "PE"])
        c0 = dict(self.cnt)

        def ev_dv(idx):
            return dv0 + 32 * (idx // 16) + 16 + (idx % 16) + 1

        with nc.Block("p2kv_v") as blk:
            @blk.sync
            def _(sy):
                for oc in range(4):
                    for c in range(16):
                        li = oc * 16 + c
                        if li >= 2:
                            sy.wait_ge(self.sems["DV"],
                                       dv0 + 32 * ((li - 2) // 16) + ((li - 2) % 16) + 1)
                        self.dma(sy, self.af(WVF + (li % 2) * 512, 512),
                                 self.wv.ap()[c * 128:(c + 1) * 128,
                                              oc * 512:(oc + 1) * 512],
                                 f"L{li % 2}")

            @blk.vector
            def _(v):
                for oc in range(4):
                    for c in range(16):
                        li = oc * 16 + c
                        lp = f"L{li % 2}"
                        v.wait_ge(self.sems[lp], c0[lp] + 16 * (li // 2 + 1))
                        if oc > 0 and c == 0:
                            v.wait_ge(self.sems["PE"], pe0 + oc * 16)
                        w = self.af(WVF + (li % 2) * 512, 512)
                        self.vbar(v, v.tensor_copy(
                            self.bap(2 * (WVH + c * 256), [[1, 512]]), w))
                        v.scalar_tensor_tensor(
                            self.bap(2 * (WVM + c * 256), [[1, 512]]), w, 0.0,
                            self.bap(2 * (WVH + c * 256), [[1, 512]]),
                            ALU.add, ALU.subtract).then_inc(self.inc("DV", 1), 1)
                    for tk in range(16):
                        idx = oc * 16 + tk
                        v.wait_ge(self.sems["PE"], pe0 + idx + 1)
                        sp = f"G{idx % 2}"
                        v.wait_ge(self.sems[sp], c0[sp] + 64 * (idx // 2))
                        ps = (self.PS_C if idx % 2 == 0 else self.PS_D).ap()
                        ev = self.af(EV2 + (idx % 2) * 512, 512)
                        v.tensor_tensor(ev, ps, self.af(OF_BV + oc * 512, 512),
                                        ALU.add).then_inc(self.inc("DV", 1), 1)

            @blk.tensor
            def _(te):
                for oc in range(4):
                    te.wait_ge(self.sems["DV"], dv0 + 32 * oc + 16)
                    for tk in range(16):
                        idx = oc * 16 + tk
                        if idx >= 2:
                            te.wait_ge(self.sems["DV"], ev_dv(idx - 2))
                        ps = (self.PS_C if idx % 2 == 0 else self.PS_D).ap()
                        for c in range(16):
                            for (xb, wv_) in ((XNTH, WVH), (XNTM, WVH), (XNTH, WVM)):
                                _pe_last = te.matmul(
                                    ps,
                                    self.bap(2 * xb + c * ncols + tk * 128, [[1, 128]]),
                                    self.bap(2 * (wv_ + c * 256), [[1, 512]]),
                                    start=(c == 0 and xb is XNTH and wv_ is WVH),
                                    stop=(c == 15 and xb is XNTH and wv_ is WVM))
                        _pe_last.then_inc(self.inc("PE", 1), 1)

            @blk.gpsimd
            def _(gp):
                for oc in range(4):
                    for tk in range(16):
                        idx = oc * 16 + tk
                        gp.wait_ge(self.sems["DV"], ev_dv(idx))
                        ev = self.af(EV2 + (idx % 2) * 512, 512)
                        sp = f"G{idx % 2}"
                        for j in range(4):
                            h = oc * 4 + j
                            self.dma(gp, self.Vscr.ap()[h, tk * 128:(tk + 1) * 128, :],
                                     ev[:, j * 128:(j + 1) * 128], sp)

    def p3_attn(self):
        nc = self.nc
        PB0, AVT = 0, 16384
        KB0, VB0, QB0, PTB0, MASKS = 24576, 28672, 34816, 35840, 36864
        X1B, WOB, X0B = 0, 8192, 24576
        st_done = self.cnt["ST"]
        gp_done = self.cnt["GP"]
        VBS = ["X1", "S0", "S1"]

        for s in range(2):
            Rs = 1024 * (s + 1)
            nkt = Rs // 128
            self.fence_block(f"f_p3s{s}")
            c0 = dict(self.cnt)

            def rngq(qt):
                return 256 * (4 * s + qt + 1)

            def qlo_of(kt):
                q = 0
                while 256 * (4 * s + q + 1) <= kt * 128:
                    q += 1
                return q

            pe_prog, dv_prog, ac_prog = [], [], []
            for i in range(17):
                do_sm, do_kt = i <= 15, i >= 1
                h, hp = i, i - 1
                scpos = {(qt * nkt) // 4: qt for qt in range(4)}
                if do_sm and not do_kt:
                    for qt in range(4):
                        pe_prog.append(("sc", h, qt))
                    for qt in range(4):
                        ac_prog.append(("exp", h, qt))
                if do_kt:
                    pe_prog.append(("T", hp, 0))
                    for kt in range(nkt):
                        if do_sm and kt in scpos:
                            pe_prog.append(("sc", h, scpos[kt]))
                        if kt + 1 < nkt:
                            pe_prog.append(("T", hp, kt + 1))
                        pe_prog.append(("PV", hp, kt))
                    for kt in range(nkt):
                        ac_prog.append(("evac", hp, kt))
                        if do_sm and kt in scpos:
                            ac_prog.append(("exp", h, scpos[kt]))
                if do_sm:
                    for kind, qt in [("bTT", 0), ("bRed", 0),
                                     ("bTT", 1), ("bRed", 1), ("rcp", 0),
                                     ("bTT", 2), ("bRed", 2), ("rcp", 1),
                                     ("nTS", 0),
                                     ("bTT", 3), ("bRed", 3), ("rcp", 2),
                                     ("nTS", 1), ("rcp", 3),
                                     ("nTS", 2), ("nTS", 3)]:
                        dv_prog.append((kind, h, qt))
                if do_kt:
                    dv_prog.append(("avt", hp))
            PEn = {op: c0["PE"] + k + 1 for k, op in enumerate(pe_prog)}
            DVn = {op: c0["DV"] + k + 1 for k, op in enumerate(dv_prog)}
            ACn = {op: c0["AC"] + k + 1 for k, op in enumerate(ac_prog)}

            with nc.Block(f"p3_s{s}") as blk:
                @blk.sync
                def _(sy):
                    if s == 0:
                        sy.wait_ge(self.sems["ST"], st_done)
                        sy.wait_ge(self.sems["GP"], gp_done)
                        for i in range(NQT):
                            self.dma(sy, self.af(MASKS + i * 256, 256),
                                     self.maskq.ap()[i, :, :], "X0")
                    for h in range(H):
                        if h >= 2:
                            sy.wait_ge(self.sems["PE"], PEn[("sc", h - 2, 3)])
                        self.dma(sy, self.af(KB0 + (h % 2) * 2048, Rs),
                                 self.Kscr.ap()[h, :, 0:Rs], f"L{h % 2}")
                        self.dma(sy, self.af(QB0 + (h % 2) * 512, 512),
                                 self.Qscr.ap()[h, :, s * 512:(s + 1) * 512],
                                 f"L{2 + h % 2}")
                        if h >= 3:
                            sy.wait_ge(self.sems["PE"], PEn[("PV", h - 3, nkt - 1)])
                        vsrc = bass.AP(self.Vscr, h * S * DH,
                                       [[DH, 128], [128 * DH, nkt], [1, DH]])
                        self.dma(sy, self.af3(VB0 + (h % 3) * 2048, 128, nkt, 128),
                                 vsrc, VBS[h % 3])

                @blk.tensor
                def _(te):
                    for op in pe_prog:
                        kind, h, x = op
                        if kind == "sc":
                            qt = x
                            rng = rngq(qt)
                            if qt == 0:
                                lp = f"L{h % 2}"
                                te.wait_ge(self.sems[lp], c0[lp] + 16 * (h // 2 + 1))
                                lp = f"L{2 + h % 2}"
                                te.wait_ge(self.sems[lp], c0[lp] + 16 * (h // 2 + 1))
                            prev = ("exp", h, qt - 1) if qt > 0 else ("exp", h - 1, 3)
                            if prev in ACn:
                                te.wait_ge(self.sems["AC"], ACn[prev])
                            for ck in range((rng + 511) // 512):
                                n = min(512, rng - ck * 512)
                                _l = te.matmul(
                                    self.PS_SC.ap()[:, ck * 512:ck * 512 + n],
                                    self.af(QB0 + (h % 2) * 512 + qt * 128, 128),
                                    self.af(KB0 + (h % 2) * 2048 + ck * 512, n),
                                    start=True, stop=True)
                            _l.then_inc(self.inc("PE", 1), 1)
                        elif kind == "T":
                            kt = x
                            if kt == 0:
                                te.wait_ge(self.sems["DV"], DVn[("nTS", h, 3)])
                            if kt >= 2:
                                te.wait_ge(self.sems["AC"], ACn[("evac", h, kt - 2)])
                            elif h >= 1:
                                te.wait_ge(self.sems["AC"],
                                           ACn[("evac", h - 1, nkt - 2 + kt)])
                            ps = (self.PS_A if kt % 2 == 0 else self.PS_B).ap()
                            qlo = qlo_of(kt)
                            for qt in range(qlo, 4):
                                _l = te.transpose(
                                    ps[:, qt * 128:(qt + 1) * 128],
                                    self.af(PB0 + (h % 2) * 8192 + qt * 2048
                                            + kt * 128, 128),
                                    self.C_id.ap())
                            _l.then_inc(self.inc("PE", 1), 1)
                        else:
                            kt = x
                            te.wait_ge(self.sems["AC"], ACn[("evac", h, kt)])
                            if kt == 0:
                                vp = VBS[h % 3]
                                te.wait_ge(self.sems[vp], c0[vp] + 16 * (h // 3 + 1))
                                if h >= 1:
                                    te.wait_ge(self.sems["DV"], DVn[("avt", h - 1)])
                            qlo = qlo_of(kt)
                            te.matmul(self.PS_C.ap()[:, qlo * 128:512],
                                      self.af(VB0 + (h % 3) * 2048 + kt * 128, 128),
                                      self.af(PTB0 + (kt % 2) * 512 + qlo * 128,
                                              512 - qlo * 128),
                                      start=(kt == 0), stop=(kt == nkt - 1)
                                      ).then_inc(self.inc("PE", 1), 1)

                @blk.vector
                def _(v):
                    first = True
                    for op in dv_prog:
                        kind, h = op[0], op[1]
                        if kind == "bTT":
                            qt = op[2]
                            rng = rngq(qt)
                            v.wait_ge(self.sems["PE"], PEn[("sc", h, qt)])
                            if first and s == 0:
                                v.wait_ge(self.sems["X0"], c0["X0"] + 16 * NQT)
                                first = False
                            i = 4 * s + qt
                            v.tensor_tensor(
                                self.PS_SC.ap()[:, rng - 256:rng],
                                self.PS_SC.ap()[:, rng - 256:rng],
                                self.af(MASKS + i * 256, 256),
                                ALU.add).then_inc(self.inc("DV", 1), 1)
                        elif kind == "bRed":
                            qt = op[2]
                            rng = rngq(qt)
                            v.wait_ge(self.sems["DV"], DVn[("bTT", h, qt)])
                            if ("exp", h - 1, qt) in ACn:
                                v.wait_ge(self.sems["AC"], ACn[("exp", h - 1, qt)])
                            v.tensor_reduce(
                                self.TMP.ap()[:, 4 + qt:5 + qt],
                                self.PS_SC.ap()[:, 0:rng],
                                AX.X, ALU.max,
                                negate=True).then_inc(self.inc("DV", 1), 1)
                        elif kind == "rcp":
                            qt = op[2]
                            v.wait_ge(self.sems["AC"], ACn[("exp", h, qt)])
                            v.reciprocal(self.TMP.ap()[:, 8 + qt:9 + qt],
                                         self.SE.ap()[:, qt:qt + 1]
                                         ).then_inc(self.inc("DV", 1), 1)
                        elif kind == "nTS":
                            qt = op[2]
                            rng = rngq(qt)
                            v.wait_ge(self.sems["DV"], DVn[("rcp", h, qt)])
                            v.tensor_scalar(
                                self.af(PB0 + (h % 2) * 8192 + qt * 2048, rng),
                                self.af(PB0 + (h % 2) * 8192 + qt * 2048, rng),
                                self.TMP.ap()[:, 8 + qt:9 + qt], None, ALU.mult,
                                ALU.bypass).then_inc(self.inc("DV", 1), 1)
                        else:
                            v.wait_ge(self.sems["PE"], PEn[("PV", h, nkt - 1)])
                            v.tensor_copy(self.af(AVT + h * 512, 512),
                                          self.PS_C.ap()).then_inc(self.inc("DV", 1), 1)

                @blk.scalar
                def _(sc):
                    for op in ac_prog:
                        kind, h, x = op
                        if kind == "exp":
                            qt = x
                            rng = rngq(qt)
                            sc.wait_ge(self.sems["DV"], DVn[("bRed", h, qt)])
                            sc.activation(self.af(PB0 + (h % 2) * 8192 + qt * 2048, rng),
                                          self.PS_SC.ap()[:, 0:rng], ACT.Exp,
                                          bias=self.TMP.ap()[:, 4 + qt:5 + qt],
                                          scale=1.0,
                                          accum_out=self.SE.ap()[:, qt:qt + 1]
                                          ).then_inc(self.inc("AC", 1), 1)
                        else:
                            kt = x
                            sc.wait_ge(self.sems["PE"], PEn[("T", h, kt)])
                            if kt >= 2:
                                sc.wait_ge(self.sems["PE"], PEn[("PV", h, kt - 2)])
                            elif h >= 1:
                                sc.wait_ge(self.sems["PE"],
                                           PEn[("PV", h - 1, nkt - 2 + kt)])
                            ps = (self.PS_A if kt % 2 == 0 else self.PS_B).ap()
                            qlo = qlo_of(kt)
                            sc.copy(self.af(PTB0 + (kt % 2) * 512 + qlo * 128,
                                            512 - qlo * 128),
                                    ps[:, qlo * 128:512]
                                    ).then_inc(self.inc("AC", 1), 1)

            self.fence_block(f"f_p3w{s}")
            ld0, dv0, ac0, pe0 = (self.cnt[k] for k in ["LD", "DV", "AC", "PE"])
            c0 = dict(self.cnt)
            with nc.Block(f"p3_wo{s}") as blk:
                @blk.sync
                def _(sy):
                    for oc in range(4):
                        if oc > 0:
                            sy.wait_ge(self.sems["PE"], pe0 + oc * 4)
                        for c in range(16):
                            self.dma(sy, self.af(WOB + c * 512, 512),
                                     self.wo.ap()[c * 128:(c + 1) * 128,
                                                  oc * 512:(oc + 1) * 512])
                        for qt in range(4):
                            idx = oc * 4 + qt
                            if idx >= 2:
                                sy.wait_ge(self.sems["DV"], dv0 + idx - 1)
                            self.dma(sy, self.af(X0B + (idx % 2) * 512, 512),
                                     self.xq.ap()[(4 * s + qt) * 128:(4 * s + qt + 1) * 128,
                                                  oc * 512:(oc + 1) * 512],
                                     f"X{idx % 2}")

                @blk.tensor
                def _(te):
                    for oc in range(4):
                        te.wait_ge(self.sems["LD"], ld0 + 16 * 16 * (oc + 1))
                        for qt in range(4):
                            idx = oc * 4 + qt
                            if idx >= 2:
                                te.wait_ge(self.sems["DV"], dv0 + idx - 1)
                            ps = (self.PS_C if idx % 2 == 0 else self.PS_D).ap()
                            for c in range(16):
                                _pe_last = te.matmul(
                                    ps, self.af(AVT + c * 512 + qt * 128, 128),
                                    self.af(WOB + c * 512, 512),
                                    start=(c == 0), stop=(c == 15))
                            _pe_last.then_inc(self.inc("PE", 1), 1)

                @blk.vector
                def _(v):
                    mkbn = {}

                    def bnchunk(idx):
                        oc, qt = idx // 4, idx % 4
                        v.wait_ge(self.sems["DV"], dv0 + idx + 1)
                        mkbn[qt] = self.vmark(v.bn_stats(
                            self.BN6.ap()[:, qt * 24 + oc * 6:qt * 24 + oc * 6 + 6],
                            self.af(X1B + qt * 2048 + oc * 512, 512)))

                    for oc in range(4):
                        for qt in range(4):
                            idx = oc * 4 + qt
                            v.wait_ge(self.sems["PE"], pe0 + idx + 1)
                            v.wait_ge(self.sems[f"X{idx % 2}"],
                                      c0[f"X{idx % 2}"] + 16 * (idx // 2 + 1))
                            ps = (self.PS_C if idx % 2 == 0 else self.PS_D).ap()
                            dst = self.af(X1B + qt * 2048 + oc * 512, 512)
                            self.vbar(v, v.scalar_tensor_tensor(
                                dst, ps, 0.0,
                                self.af(X0B + (idx % 2) * 512, 512),
                                ALU.add, ALU.add))
                            v.tensor_tensor(dst, dst, self.af(OF_BO + oc * 512, 512),
                                            ALU.add).then_inc(self.inc("DV", 1), 1)
                            if idx >= 1:
                                bnchunk(idx - 1)
                    bnchunk(15)
                    for qt in range(4):
                        i = 4 * s + qt
                        v.wait_ge(self.sems["VH"], mkbn[qt])
                        self.vbar(v, v.bn_aggr(self.BN2.ap()[:, 0:2],
                                               self.BN6.ap()[:, qt * 24:qt * 24 + 24]))
                        v.tensor_copy(self.ST1_mu.ap()[:, i:i + 1], self.BN2.ap()[:, 0:1])
                        v.tensor_scalar(self.TMP.ap()[:, 4:5], self.BN2.ap()[:, 1:2],
                                        EPS, None, ALU.add, ALU.bypass)
                        v.tensor_scalar(self.TMP.ap()[:, 5:6], self.BN2.ap()[:, 1:2],
                                        1.0 + EPS, EPS * EPS, ALU.mult,
                                        ALU.add).then_inc(self.inc("DV", 1), 1)
                        v.wait_ge(self.sems["AC"], ac0 + qt + 1)
                        v.reciprocal(self.ST1_r1.ap()[:, i:i + 1], self.TMP.ap()[:, 6:7])
                        self.vbar(v, v.reciprocal(self.ST1_al.ap()[:, i:i + 1],
                                                  self.TMP.ap()[:, 7:8]))
                        v.tensor_scalar(self.ST1_a.ap()[:, i:i + 1],
                                        self.ST1_r1.ap()[:, i:i + 1], 1.0, None,
                                        ALU.add, ALU.bypass)
                        v.scalar_tensor_tensor(self.ST1_b.ap()[:, i:i + 1],
                                               self.ST1_mu.ap()[:, i:i + 1], -1.0,
                                               self.ST1_r1.ap()[:, i:i + 1],
                                               ALU.mult, ALU.mult).then_inc(self.inc("DV", 1), 1)

                @blk.scalar
                def _(sc):
                    for qt in range(4):
                        sc.wait_ge(self.sems["DV"], dv0 + 16 + 2 * qt + 1)
                        sc.activation(self.TMP.ap()[:, 6:7], self.TMP.ap()[:, 4:5], ACT.Sqrt)
                        sc.activation(self.TMP.ap()[:, 7:8], self.TMP.ap()[:, 5:6],
                                      ACT.Sqrt).then_inc(self.inc("AC", 1), 1)

                @blk.gpsimd
                def _(gp):
                    for qt in range(4):
                        i = 4 * s + qt
                        gp.wait_ge(self.sems["DV"], dv0 + 16 + 2 * qt + 2)
                        self.dma(gp, self.X1scr.ap()[i * 128:(i + 1) * 128, :],
                                 self.af(X1B + qt * 2048, 2048), "GP")

    def p4_ffn_a(self):
        nc = self.nc
        AQ1T, XB4, GB, Z0, Z1, Z2, AQN = 0, 8192, 10240, 12288, 14336, 16384, 18432
        gp_done = self.cnt["GP"]
        self.fence_block("f_p4")
        ld0, dv0, ae0, pe0 = (self.cnt[k] for k in ["LD", "DV", "AE", "PE"])
        with nc.Block("p4") as blk:
            @blk.sync
            def _(sy):
                sy.wait_ge(self.sems["GP"], gp_done)
                for i in range(NQT):
                    if i > 0:
                        sy.wait_ge(self.sems["DV"], dv0 + i)
                    self.dma(sy, self.af(XB4, 2048),
                             self.X1scr.ap()[i * 128:(i + 1) * 128, :])

            @blk.vector
            def _(v):
                for i in range(NQT):
                    v.wait_ge(self.sems["LD"], ld0 + 16 * (i + 1))
                    if i > 0:
                        v.wait_ge(self.sems["PE"], pe0 + 4 * i)
                    self.vbar(v, v.scalar_tensor_tensor(
                        self.af(GB, 2048), self.af(XB4, 2048),
                        self.ST1_mu.ap()[:, i:i + 1],
                        self.af(OF_NW, D), ALU.subtract, ALU.mult))
                    self.vbar(v, v.tensor_reduce(self.TMP.ap()[:, 4:5],
                                                 self.af(GB, 2048), AX.X,
                                                 ALU.max, apply_absolute_value=True))
                    self.vbar(v, v.tensor_scalar(self.TMP.ap()[:, 4:5],
                                                 self.TMP.ap()[:, 4:5],
                                                 1e-5, None, ALU.max, ALU.bypass))
                    self.vbar(v, v.reciprocal(self.TMP.ap()[:, 5:6],
                                               self.TMP.ap()[:, 4:5]))
                    v.tensor_scalar(self.TMP.ap()[:, 5:6], self.TMP.ap()[:, 5:6],
                                    127.0, None, ALU.mult, ALU.bypass)
                    self.vbar(v, v.tensor_tensor(self.TMP.ap()[:, 6:7],
                                                 self.ST1_al.ap()[:, i:i + 1],
                                                 self.TMP.ap()[:, 4:5], ALU.mult))
                    v.tensor_tensor(self.TMP.ap()[:, 7:8], self.TMP.ap()[:, 6:7],
                                    self.V_sgv.ap()[:, 0:1], ALU.mult)
                    self.round_pos(v, self.af(GB, 2048), self.TMP.ap()[:, 5:6],
                                   Z0, Z1, Z2, 2 * AQN, 2048)
                    v.tensor_scalar(self.C1c.ap()[:, i:i + 1], self.TMP.ap()[:, 7:8],
                                    1.0 / 127.0, None, ALU.mult,
                                    ALU.bypass).then_inc(self.inc("DV", 1), 1)

            @blk.tensor
            def _(te):
                for i in range(NQT):
                    te.wait_ge(self.sems["DV"], dv0 + i + 1)
                    for cg in range(4):
                        idx = 4 * i + cg
                        if idx >= 2:
                            te.wait_ge(self.sems["AE"], ae0 + idx - 1)
                        psb = (self.PS_C if idx % 2 == 0 else self.PS_D).ap().bitcast(BF16)
                        for c4 in range(4):
                            c = 4 * cg + c4
                            _pe_last = te.transpose(psb[:, c4 * 128:(c4 + 1) * 128],
                                         self.bap(2 * AQN + c * 128, [[1, 128]]),
                                         self.C_idb.ap())
                        _pe_last.then_inc(self.inc("PE", 1), 1)

            @blk.scalar
            def _(sc):
                for i in range(NQT):
                    for cg in range(4):
                        idx = 4 * i + cg
                        sc.wait_ge(self.sems["PE"], pe0 + idx + 1)
                        psb = (self.PS_C if idx % 2 == 0 else self.PS_D).ap().bitcast(BF16)
                        dst = self.bap(2 * AQ1T + (4 * cg) * 1024 + i * 128,
                                       [[1024, 4], [1, 128]])
                        sc.copy(dst, psb[:, 0:512].rearrange("p (a c) -> p a c", a=4)
                                ).then_inc(self.inc("AE", 1), 1)

    def p5_gv(self):
        nc = self.nc
        AQ1T = 0
        TT0 = 8192
        WGB = 24576
        M1B = 40960
        NTB = 43008
        SGB = 45056
        HMB = 46080
        self.fence_block("f_p5")
        GSEM = ["L0", "L1", "L2", "L3", "X0", "X1", "S0", "S1"]
        c00 = dict(self.cnt)
        for b in range(17):
            tern = b <= 15
            mm = b >= 1
            tp = b % 2
            mp = (b - 1) % 2
            c0 = dict(self.cnt)
            pe_per = (1 if mm else 0) + (4 if tern else 0)
            ac_per = (1 if mm else 0) + (4 if tern else 0)
            dv_per = (1 if mm else 0) + (1 if tern else 0)

            def dv_after_tern(g):
                return c0["DV"] + dv_per * g + 1

            def dv_after_hm(k):
                return c0["DV"] + dv_per * k + (2 if tern else 1)

            def pe_after_mm(k):
                return c0["PE"] + pe_per * k + 1

            def pe_after_tg(k, cg):
                return c0["PE"] + pe_per * k + (1 if mm else 0) + cg + 1

            def ac_after_silu(k):
                return c0["AC"] + ac_per * k + 1

            def ac_after_evac(k, cg):
                return c0["AC"] + ac_per * k + (1 if mm else 0) + cg + 1

            with nc.Block(f"p5_{b}") as blk:
                @blk.sync
                def _(sy):
                    if b == 0:
                        for g in range(8):
                            part, ot = g // 4, g % 4
                            self.dma(sy, self.af(WGB + g * 2048, 2048),
                                     self.wgv.ap()[part * HID + ot * 128:
                                                   part * HID + (ot + 1) * 128, :],
                                     GSEM[g])
                    if b <= 14:
                        for g in range(8):
                            part, ot = g // 4, g % 4
                            rowbase = part * HID + (b + 1) * 512
                            sy.wait_ge(self.sems["DV"], dv_after_tern(g))
                            self.dma(sy, self.af(WGB + g * 2048, 2048),
                                     self.wgv.ap()[rowbase + ot * 128:
                                                   rowbase + (ot + 1) * 128, :],
                                     GSEM[g])

                @blk.vector
                def _(v):
                    for k in range(8):
                        if tern:
                            g = k
                            lp = GSEM[g]
                            v.wait_ge(self.sems[lp], c00[lp] + 16 * (b + 1))
                            if g >= 2:
                                v.wait_ge(self.sems["PE"], pe_after_tg(g - 2, 3))
                            w = self.af(WGB + g * 2048, 2048)
                            m1 = self.bap(2 * (M1B + (g % 2) * 1024), [[1, 2048]])
                            self.vbar(v, v.tensor_scalar(
                                m1, w,
                                self.V_sgv.ap()[:, 2:3], None,
                                ALU.is_lt, ALU.bypass))
                            v.scalar_tensor_tensor(
                                self.bap(2 * (NTB + (g % 2) * 1024), [[1, 2048]]),
                                w, self.V_sgv.ap()[:, 1:2], m1,
                                ALU.is_gt, ALU.subtract
                                ).then_inc(self.inc("DV", 1), 1)
                        if mm:
                            qt = k
                            v.wait_ge(self.sems["AC"], ac_after_silu(qt))
                            gpar = f"G{qt % 2}"
                            v.wait_ge(self.sems[gpar], c0[gpar] + 16 * (qt // 2))
                            hm = self.af(HMB + (qt % 2) * 512, 512)
                            psv = self.PS_SC.ap()[:, (qt % 2) * 1024:(qt % 2) * 1024 + 512]
                            self.vbar(v, v.scalar_tensor_tensor(
                                hm, psv,
                                self.C1c.ap()[:, qt:qt + 1],
                                self.af(SGB + (qt % 2) * 512, 512), ALU.mult, ALU.mult))
                            self.vbar(v, v.tensor_reduce(
                                self.TMP.ap()[:, 4:5], hm, AX.X, ALU.max,
                                apply_absolute_value=True))
                            v.tensor_tensor(self.RMX.ap()[:, qt:qt + 1],
                                            self.RMX.ap()[:, qt:qt + 1],
                                            self.TMP.ap()[:, 4:5],
                                            ALU.max).then_inc(self.inc("DV", 1), 1)

                @blk.tensor
                def _(te):
                    for k in range(8):
                        if mm:
                            qt = k
                            if qt >= 2:
                                te.wait_ge(self.sems["DV"], dv_after_hm(qt - 2))
                            psv = self.PS_SC.ap()[:, (qt % 2) * 1024:(qt % 2) * 1024 + 512]
                            psg = self.PS_SC.ap()[:, (qt % 2) * 1024 + 512:
                                                  (qt % 2) * 1024 + 1024]
                            for c in range(16):
                                te.matmul(psv,
                                          self.bap(2 * AQ1T + c * 1024 + qt * 128,
                                                   [[1, 128]]),
                                          self.bap(2 * (TT0 + mp * 8192) + c * 512,
                                                   [[1, 512]]),
                                          start=(c == 0), stop=(c == 15))
                                _l = te.matmul(psg,
                                               self.bap(2 * AQ1T + c * 1024 + qt * 128,
                                                        [[1, 128]]),
                                               self.bap(2 * (TT0 + mp * 8192 + 4096)
                                                        + c * 512, [[1, 512]]),
                                               start=(c == 0), stop=(c == 15))
                            _l.then_inc(self.inc("PE", 1), 1)
                        if tern:
                            g = k
                            te.wait_ge(self.sems["DV"], dv_after_tern(g))
                            part, ot = g // 4, g % 4
                            base = TT0 + tp * 8192 + (0 if part == 0 else 4096)
                            for cg in range(4):
                                tid = 8 * b + g
                                if b * 32 + g * 4 + cg >= 2:
                                    kp, cgp = divmod(g * 4 + cg - 2, 4)
                                    if g * 4 + cg >= 2:
                                        te.wait_ge(self.sems["AC"],
                                                   ac_after_evac(kp, cgp))
                                psb = (self.PS_C if (g * 4 + cg) % 2 == 0
                                       else self.PS_D).ap().bitcast(BF16)
                                for c4 in range(4):
                                    c = 4 * cg + c4
                                    _t = te.transpose(
                                        psb[:, c4 * 128:(c4 + 1) * 128],
                                        self.bap(2 * (NTB + (g % 2) * 1024) + c * 128,
                                                 [[1, 128]]),
                                        self.C_idb.ap())
                                _t.then_inc(self.inc("PE", 1), 1)

                @blk.scalar
                def _(sc):
                    for k in range(8):
                        if mm:
                            qt = k
                            sc.wait_ge(self.sems["PE"], pe_after_mm(qt))
                            psg = self.PS_SC.ap()[:, (qt % 2) * 1024 + 512:
                                                  (qt % 2) * 1024 + 1024]
                            sc.activation(self.af(SGB + (qt % 2) * 512, 512), psg,
                                          ACT.Silu, bias=0.0,
                                          scale=self.C1c.ap()[:, qt:qt + 1]
                                          ).then_inc(self.inc("AC", 1), 1)
                        if tern:
                            g = k
                            part, ot = g // 4, g % 4
                            base = TT0 + tp * 8192 + (0 if part == 0 else 4096)
                            for cg in range(4):
                                sc.wait_ge(self.sems["PE"], pe_after_tg(g, cg))
                                psb = (self.PS_C if (g * 4 + cg) % 2 == 0
                                       else self.PS_D).ap().bitcast(BF16)
                                dst = self.bap(2 * base + (4 * cg) * 512 + ot * 128,
                                               [[512, 4], [1, 128]])
                                sc.copy(dst, psb[:, 0:512]
                                        .rearrange("p (a c) -> p a c", a=4)
                                        ).then_inc(self.inc("AC", 1), 1)

                if mm:
                    @blk.gpsimd
                    def _(gp):
                        for qt in range(8):
                            gp.wait_ge(self.sems["DV"], dv_after_hm(qt))
                            self.dma(gp, self.HMID.ap()[qt * 128:(qt + 1) * 128,
                                                        (b - 1) * 512:b * 512],
                                     self.af(HMB + (qt % 2) * 512, 512),
                                     f"G{qt % 2}")

    def p6_tout(self):
        nc = self.nc
        WOB, M1C, NT5, TSB = 0, 12288, 14336, 16384
        LQ = ["L0", "L1", "L2", "L3", "X0", "X1"]
        SQ = ["S0", "S1", "LD"]
        self.fence_block("f_p6")
        c0 = dict(self.cnt)
        DVn = lambda ci: c0["DV"] + ci + 1
        PEn = lambda g: c0["PE"] + g + 1
        ACn = lambda g: c0["AC"] + g + 1
        store_cnt = {}
        qn = {q: 0 for q in SQ}
        for bt in range(128):
            q = SQ[bt % 3]
            qn[q] += 1
            store_cnt[bt] = (q, qn[q])
        with nc.Block("p6") as blk:
            @blk.sync
            def _(sy):
                for ci in range(64):
                    if ci >= 6:
                        sy.wait_ge(self.sems["DV"], DVn(ci - 6))
                    rt, ch = divmod(ci, 4)
                    self.dma(sy, self.af(WOB + (ci % 6) * 2048, 2048),
                             self.wout.ap()[rt * 128:(rt + 1) * 128,
                                            ch * 2048:(ch + 1) * 2048],
                             LQ[ci % 6])

            @blk.vector
            def _(v):
                mk = {}
                for s in range(65):
                    if s < 64:
                        q = LQ[s % 6]
                        v.wait_ge(self.sems[q], c0[q] + 16 * (s // 6 + 1))
                        if s >= 2:
                            v.wait_ge(self.sems["DV"], DVn(s - 2))
                        mk[s] = self.vmark(v.tensor_scalar(
                            self.bap(2 * (M1C + (s % 2) * 1024), [[1, 2048]]),
                            self.af(WOB + (s % 6) * 2048, 2048),
                            self.V_sout.ap()[:, 2:3], None,
                            ALU.is_lt, ALU.bypass))
                    if s >= 1:
                        ci = s - 1
                        if ci >= 2:
                            v.wait_ge(self.sems["PE"], PEn(4 * (ci - 2) + 3))
                        v.wait_ge(self.sems["VH"], mk[ci])
                        v.scalar_tensor_tensor(
                            self.bap(2 * (NT5 + (ci % 2) * 1024), [[1, 2048]]),
                            self.af(WOB + (ci % 6) * 2048, 2048),
                            self.V_sout.ap()[:, 1:2],
                            self.bap(2 * (M1C + (ci % 2) * 1024), [[1, 2048]]),
                            ALU.is_gt, ALU.subtract).then_inc(self.inc("DV", 1), 1)

            @blk.tensor
            def _(te):
                for ci in range(64):
                    te.wait_ge(self.sems["DV"], DVn(ci))
                    for u in range(4):
                        g = 4 * ci + u
                        if g >= 2:
                            te.wait_ge(self.sems["AC"], ACn(g - 2))
                        psb = (self.PS_C if g % 2 == 0
                               else self.PS_D).ap().bitcast(BF16)
                        for s in range(4):
                            tl = 4 * u + s
                            _l = te.transpose(
                                psb[:, s * 128:(s + 1) * 128],
                                self.bap(2 * (NT5 + (ci % 2) * 1024) + tl * 128,
                                         [[1, 128]]),
                                self.C_idb.ap())
                        _l.then_inc(self.inc("PE", 1), 1)

            @blk.scalar
            def _(sc):
                for ci in range(64):
                    rt, ch = divmod(ci, 4)
                    for u in range(4):
                        g = 4 * ci + u
                        bt = g // 2
                        sc.wait_ge(self.sems["PE"], PEn(g))
                        if u % 2 == 0 and bt >= 3:
                            q, n = store_cnt[bt - 3]
                            sc.wait_ge(self.sems[q], c0[q] + 16 * n)
                        psb = (self.PS_C if g % 2 == 0
                               else self.PS_D).ap().bitcast(BF16)
                        tsb = self.bap(2 * (TSB + (bt % 3) * 512 + (u % 2) * 256),
                                       [[1, 512]])
                        sc.copy(tsb, psb[:, 0:512]).then_inc(self.inc("AC", 1), 1)
                        if u % 2 == 1:
                            sc.wait_ge(self.sems["AC"], self.cnt["AC"])
                            oc0 = ch * 16 + 4 * (u - 1)
                            dst = bass.AP(self.TOUT, rt * 1048576 + oc0 * 16384,
                                          [[128, 128], [16384, 8], [1, 128]])
                            self.dma(sc, dst,
                                     self.bap(2 * (TSB + (bt % 3) * 512),
                                              [[128, 8], [1, 128]]),
                                     SQ[bt % 3])

    def p7_out2(self):
        nc = self.nc
        AQ2T = 0
        HM2 = 32768
        ZA, ZBC, ZCC = 34816, 35328, 35840
        AQ2N = 36352
        AQN = (36352, 43776)
        ZP7 = 40704
        TOB = 36608
        X1B7 = 37632
        XH = 38656
        YB = 39680
        gp_done = self.cnt["GP"]
        st_done = self.cnt["ST"]
        TSEM = ["H0", "H1", "S0", "S1"]
        TOB2 = 44032

        def tobo(s):
            return TOB + s * 256 if s < 4 else TOB2 + (s - 4) * 256

        for b in range(3):
            bld = b <= 1
            mm = b >= 1
            TS_Q = TSEM + ["L0", "L1", "L2", "L3"]
            NS = len(TS_Q)
            self.fence_block(f"f_p7_{b}")
            c0 = dict(self.cnt)
            pe_prog, dv_prog, ac_prog = [], [], []
            if mm:
                k_iter = iter(range(64))
                for j in range(4):
                    for oc in range(64):
                        if bld and oc % 4 == 0:
                            try:
                                k = next(k_iter)
                                pe_prog.append(("T", k))
                            except StopIteration:
                                pass
                        pe_prog.append(("MM", j, oc))
            elif bld:
                for k in range(64):
                    pe_prog.append(("T", k))
            if bld and mm:
                for j in range(4):
                    for k in range(j * 16, (j + 1) * 16):
                        dv_prog.append(("rnd", k))
                    for qt in range(4):
                        dv_prog.append(("yb", j, qt))
            elif bld:
                for k in range(64):
                    dv_prog.append(("rnd", k))
            else:
                for j in range(4):
                    for qt in range(4):
                        dv_prog.append(("yb", j, qt))
            if bld:
                for k in range(64):
                    ac_prog.append(("evac", k))
            PEn = {op: c0["PE"] + i + 1 for i, op in enumerate(pe_prog)}
            DVn = {op: c0["DV"] + i + 1 for i, op in enumerate(dv_prog)}
            ACn = {op: c0["AC"] + i + 1 for i, op in enumerate(ac_prog)}

            with nc.Block(f"p7_{b}") as blk:
                @blk.sync
                def _(sy):
                    if b == 0:
                        sy.wait_ge(self.sems["GP"], gp_done)
                        sy.wait_ge(self.sems["ST"], st_done)

                    def hm2_load(k):
                        qt, oc16 = k // 16, k % 16
                        i = b * 4 + qt
                        if k >= 4:
                            sy.wait_ge(self.sems["DV"], DVn[("rnd", k - 4)])
                        self.dma(sy, self.af(HM2 + (k % 4) * 512, 512),
                                 self.HMID.ap()[i * 128:(i + 1) * 128,
                                                oc16 * 512:(oc16 + 1) * 512],
                                 f"L{k % 4}")

                    def tob_load(j, oc):
                        idx = j * 64 + oc
                        if idx >= NS:
                            sy.wait_ge(self.sems["PE"], PEn[("MM", (idx - NS) // 64,
                                                            (idx - NS) % 64)])
                        vsrc = bass.AP(self.TOUT, (j * 4) * 1048576 + oc * 16384,
                                       [[128, 128], [1048576, 4], [1, 128]])
                        self.dma(sy, self.bap(2 * tobo(idx % NS),
                                              [[128, 4], [1, 128]]),
                                 vsrc, TS_Q[idx % NS])

                    def x1_load(j, qt):
                        i = (b - 1) * 4 + qt
                        idx2 = j * 4 + qt
                        if idx2 >= 2:
                            sy.wait_ge(self.sems["DV"], DVn[("yb", (idx2 - 2) // 4,
                                                            (idx2 - 2) % 4)])
                        self.dma(sy, self.af(X1B7 + (idx2 % 2) * 512, 512),
                                 self.X1scr.ap()[i * 128:(i + 1) * 128,
                                                 j * 512:(j + 1) * 512],
                                 f"X{idx2 % 2}")

                    if bld and not mm:
                        for k in range(64):
                            hm2_load(k)
                    else:
                        for j in range(4):
                            for oc in range(64):
                                tob_load(j, oc)
                            for qt in range(4):
                                x1_load(j, qt)

                @blk.vector
                def _(v):
                    rp_feed, rp_flush = self.rp_pipe(v, ZP7, 512)
                    for op in dv_prog:
                        if op[0] == "rnd":
                            k = op[1]
                            qt, oc16 = k // 16, k % 16
                            i = b * 4 + qt
                            if oc16 == 0:
                                self.vbar(v, v.tensor_scalar(
                                    self.TMP.ap()[:, 4:5], self.RMX.ap()[:, i:i + 1],
                                    1e-5, None, ALU.max, ALU.bypass))
                                v.reciprocal(self.TMP.ap()[:, 6:7], self.TMP.ap()[:, 4:5])
                                self.vbar(v, v.tensor_tensor(
                                    self.TMP.ap()[:, 5:6], self.V_sout.ap()[:, 0:1],
                                    self.TMP.ap()[:, 4:5], ALU.mult))
                                v.tensor_scalar(self.SC2.ap()[:, i:i + 1],
                                                self.TMP.ap()[:, 6:7],
                                                127.0, None, ALU.mult, ALU.bypass)
                                self.vbar(v, v.tensor_scalar(
                                    self.SO2.ap()[:, i:i + 1], self.TMP.ap()[:, 5:6],
                                    1.0 / 127.0, None, ALU.mult, ALU.bypass))
                            def pre(k=k):
                                if mm:
                                    q = ["G0", "G1", "GP"][k % 3]
                                    v.wait_ge(self.sems[q],
                                              c0[q] + 16 * (k // 3 + 1))
                                else:
                                    lp = f"L{k % 4}"
                                    v.wait_ge(self.sems[lp],
                                              c0[lp] + 16 * (k // 4 + 1))
                                if k >= 2:
                                    v.wait_ge(self.sems["PE"], PEn[("T", k - 2)])
                            hm = self.af(HM2 + (k % (3 if mm else 4)) * 512, 512)
                            rp_feed(pre, hm, self.SC2.ap()[:, i:i + 1],
                                    2 * AQN[k % 2])
                        else:
                            rp_flush()
                            _, j, qt = op
                            i = (b - 1) * 4 + qt
                            idx2 = j * 4 + qt
                            v.wait_ge(self.sems["PE"], PEn[("MM", j, 63)])
                            v.wait_ge(self.sems[f"X{idx2 % 2}"],
                                      c0[f"X{idx2 % 2}"] + 16 * (idx2 // 2 + 1))
                            ypar = f"Y{idx2 % 2}"
                            v.wait_ge(self.sems[ypar], c0[ypar] + 16 * (idx2 // 2))
                            xh = self.af(XH + (idx2 % 2) * 512, 512)
                            self.vbar(v, v.tensor_scalar(
                                xh, self.af(X1B7 + (idx2 % 2) * 512, 512),
                                self.ST1_a.ap()[:, i:i + 1],
                                self.ST1_b.ap()[:, i:i + 1], ALU.mult, ALU.add))
                            v.scalar_tensor_tensor(
                                self.af(YB + (idx2 % 2) * 512, 512),
                                self.PS_SC.ap()[:, qt * 512:(qt + 1) * 512],
                                self.SO2.ap()[:, i:i + 1], xh, ALU.mult,
                                ALU.add).then_inc(self.inc("DV", 1), 1)
                    rp_flush()

                @blk.tensor
                def _(te):
                    for op in pe_prog:
                        if op[0] == "T":
                            k = op[1]
                            qt = k // 16
                            qtg = b * 4 + qt
                            te.wait_ge(self.sems["DV"], DVn[("rnd", k)])
                            if k >= 2:
                                te.wait_ge(self.sems["AC"], ACn[("evac", k - 2)])
                            psb = (self.PS_C if k % 2 == 0
                                   else self.PS_D).ap().bitcast(BF16)
                            for sub in range(4):
                                _l = te.transpose(psb[:, sub * 128:(sub + 1) * 128],
                                                  self.bap(2 * AQN[k % 2] + sub * 128,
                                                           [[1, 128]]),
                                                  self.C_idb.ap())
                            _l.then_inc(self.inc("PE", 1), 1)
                        else:
                            _, j, oc = op
                            idx = j * 64 + oc
                            qtg0 = (b - 1) * 4
                            te.wait_ge(self.sems[TS_Q[idx % NS]],
                                       c0[TS_Q[idx % NS]] + 16 * (idx // NS + 1))
                            if oc == 0 and j > 0:
                                te.wait_ge(self.sems["DV"], DVn[("yb", j - 1, 3)])
                            for qt in range(4):
                                _l = te.matmul(
                                    self.PS_SC.ap()[:, qt * 512:(qt + 1) * 512],
                                    self.bap(2 * AQ2T + oc * 1024 + (qtg0 + qt) * 128,
                                             [[1, 128]]),
                                    self.bap(2 * tobo(idx % NS), [[1, 512]]),
                                    start=(oc == 0), stop=(oc == 63))
                            _l.then_inc(self.inc("PE", 1), 1)

                @blk.scalar
                def _(sc):
                    for op in ac_prog:
                        k = op[1]
                        qt, oc16 = k // 16, k % 16
                        qtg = b * 4 + qt
                        sc.wait_ge(self.sems["PE"], PEn[("T", k)])
                        psb = (self.PS_C if k % 2 == 0
                               else self.PS_D).ap().bitcast(BF16)
                        dst = self.bap(2 * AQ2T + (oc16 * 4) * 1024 + qtg * 128,
                                       [[1024, 4], [1, 128]])
                        sc.copy(dst, psb[:, 0:512]
                                .rearrange("p (a c) -> p a c", a=4)
                                ).then_inc(self.inc("AC", 1), 1)

                if mm:
                    @blk.gpsimd
                    def _(gp):
                        def ystores(j):
                            for qt in range(4):
                                i = (b - 1) * 4 + qt
                                idx2 = j * 4 + qt
                                gp.wait_ge(self.sems["DV"], DVn[("yb", j, qt)])
                                self.dma(gp, self.y.ap()[i * 128:(i + 1) * 128,
                                                         j * 512:(j + 1) * 512],
                                         self.af(YB + (idx2 % 2) * 512, 512),
                                         f"Y{idx2 % 2}")

                        for j in range(4):
                            if j >= 1:
                                ystores(j - 1)
                            if bld:
                                for kk in range(16):
                                    k = j * 16 + kk
                                    if k >= 3:
                                        gp.wait_ge(self.sems["DV"],
                                                   DVn[("rnd", k - 3)])
                                    i = b * 4 + j
                                    self.dma(gp,
                                             self.af(HM2 + (k % 3) * 512, 512),
                                             self.HMID.ap()[i * 128:(i + 1) * 128,
                                                            kk * 512:(kk + 1) * 512],
                                             ["G0", "G1", "GP"][k % 3])
                        ystores(3)

    def p_final(self):
        nc = self.nc
        with nc.Block("fin") as blk:
            @blk.gpsimd
            def _(gp):
                for s in self.DMA_SEMS:
                    if self.cnt[s]:
                        gp.wait_ge(self.sems[s], self.cnt[s])


def _host_inputs(core, X, ln_g, ln_b, Wq, bq, Wk, bk, Wv, bv, Wo, bo,
                 norm_w, w_gv, w_out):
    b, g = core // 2, core % 2
    f32 = np.float32
    rep = lambda v: np.ascontiguousarray(np.broadcast_to(np.asarray(v, f32), (128, D)))
    xb = np.ascontiguousarray(np.asarray(X, f32)[b])
    xqv = np.ascontiguousarray(xb.reshape(16, 128, D)[g::2].reshape(T, D))
    mask = np.zeros((NQT, 128, 256), f32)
    for i in range(NQT):
        qg = 256 * i + 128 * g + np.arange(128)[:, None]
        kg = 256 * i + np.arange(256)[None, :]
        mask[i] = np.where(kg <= qg, 0.0, NEGBIG)
    return {
        "x": xb, "xq": xqv,
        "wq": np.ascontiguousarray(np.asarray(Wq, f32)),
        "wk": np.ascontiguousarray(np.asarray(Wk, f32)),
        "wv": np.ascontiguousarray(np.asarray(Wv, f32)),
        "wo": np.ascontiguousarray(np.asarray(Wo, f32)),
        "bq_p": np.ascontiguousarray(np.asarray(bq, f32).reshape(H, 128).T),
        "bk_p": np.ascontiguousarray(np.asarray(bk, f32).reshape(H, 128).T),
        "bv_rep": rep(bv), "bo_rep": rep(bo), "g_rep": rep(ln_g),
        "b_rep": rep(ln_b), "nw_rep": rep(norm_w),
        "wgv": np.ascontiguousarray(np.asarray(w_gv, f32)),
        "wout": np.ascontiguousarray(np.asarray(w_out, f32)),
        "maskq": mask,
        "ident": np.eye(128, dtype=f32),
        "identb": np.eye(128, dtype=ml_dtypes.bfloat16),
    }


def kernel(L0_attn_X, L0_attn_ln_g, L0_attn_ln_b, L0_attn_Wq, L0_attn_bq,
           L0_attn_Wk, L0_attn_bk, L0_attn_Wv, L0_attn_bv, L0_attn_Wo,
           L0_attn_bo, L0_ffn_norm_w, L0_ffn_w_gv, L0_ffn_w_out):
    prog = Prog()
    nc = prog.build()
    in_maps = [
        _host_inputs(c, L0_attn_X, L0_attn_ln_g, L0_attn_ln_b, L0_attn_Wq,
                     L0_attn_bq, L0_attn_Wk, L0_attn_bk, L0_attn_Wv, L0_attn_bv,
                     L0_attn_Wo, L0_attn_bo, L0_ffn_norm_w, L0_ffn_w_gv,
                     L0_ffn_w_out)
        for c in range(8)
    ]
    res = run_bass_kernel_spmd(nc, in_maps, core_ids=list(range(8)))
    out = np.empty((4, S, D), np.float32)
    for c in range(8):
        b, g = c // 2, c % 2
        out[b].reshape(16, 128, D)[g::2] = res.results[c]["y"].reshape(NQT, 128, D)
    return out
